# version 12
# baseline (speedup 1.0000x reference)
import numpy as np
import ml_dtypes
import concourse.bass as bass
import concourse.mybir as mybir
from concourse.bass_utils import run_bass_kernel_spmd
from contextlib import ExitStack

F32 = mybir.dt.float32
BF16 = mybir.dt.bfloat16
I16 = mybir.dt.int16
ALU = mybir.AluOpType
AX = mybir.AxisListType
AFT = mybir.ActivationFunctionType

S, E, IFF = 2048, 1024, 2816
NT = S // 128
ECH = E // 128
H, D = 16, 64
NBE = E // 16
ICH = IFF // 128
WICH = (8, 8, 6)
WIPW = (1024, 1024, 768)

DEBUG_PHASE = 99
P2_LEVEL = 5
P2_CHUNKS = NT
P2_SKIP = set()
N_SQ = 12
N_GQ = 8


class Prog:
    ENGINES = ("sync", "vector", "scalar", "tensor", "gpsimd")

    def __init__(self, nc, st):
        self.nc = nc
        names = ["dv", "sc", "te", "pl"]
        names += [f"sq{i}" for i in range(N_SQ)]
        names += [f"gq{i}" for i in range(N_GQ)]
        self.sems = {n: st.enter_context(nc.semaphore(n)) for n in names}
        self.cnt = {n: 0 for n in names}
        self.q = {e: [] for e in self.ENGINES}
        self.slot_rr = {"sync": 0, "gpsimd": 0}

    def emit(self, eng, fn):
        self.q[eng].append(fn)

    def wait(self, eng, tok):
        if tok is None:
            return
        if isinstance(tok, list):
            best = {}
            for t in tok:
                if t is None:
                    continue
                s, v = t
                best[s] = max(best.get(s, 0), v)
            for s, v in best.items():
                self._wait1(eng, s, v)
            return
        self._wait1(eng, tok[0], tok[1])

    def _wait1(self, eng, sem, val):
        if val <= 0:
            return
        h = self.sems[sem]
        self.emit(eng, lambda e, h=h, v=val: e.wait_ge(h, v))

    def op(self, eng, fn, inc=None, k=1):
        if inc is None:
            self.emit(eng, fn)
            return None
        h = self.sems[inc]
        self.cnt[inc] += k
        tgt = self.cnt[inc]
        self.emit(eng, lambda e, fn=fn, h=h, k=k: fn(e).then_inc(h, k))
        return (inc, tgt)

    def dma(self, eng, out, in_):
        pool, npool = ("sq", N_SQ) if eng == "sync" else ("gq", N_GQ)
        i = self.slot_rr[eng] % npool
        self.slot_rr[eng] += 1
        iss = "scalar" if eng == "gpsimd" else eng
        sem = f"{pool}{i}"
        prev = self.cnt[sem]
        if prev > 0:
            self._wait1(iss, sem, prev)
        self.cnt[sem] += 16
        tgt = self.cnt[sem]
        h = self.sems[sem]
        self.emit(iss, lambda e, o=out, i_=in_, h=h: e.dma_start(out=o, in_=i_).then_inc(h, 16))
        return (sem, tgt)

    def barrier(self):
        snap = dict(self.cnt)
        for eng in self.ENGINES:
            for s, v in snap.items():
                if v > 0:
                    self._wait1(eng, s, v)

    def finalize(self, block):
        for eng in self.ENGINES:
            fns = self.q[eng]
            if not fns:
                continue
            getattr(block, eng)(lambda e, fns=fns: [f(e) for f in fns])


def rope_tables():
    try:
        import jax
        import jax.numpy as jnp
        with jax.default_device(jax.devices("cpu")[0]):
            inv_freq = 10000.0 ** (-(jnp.arange(D // 2, dtype=jnp.float32) * 2.0) / D)
            posj = jnp.arange(S, dtype=jnp.float32) + 0
            angj = posj[:, None] * inv_freq[None, :]
            cos = np.asarray(jnp.cos(angj))
            sin = np.asarray(jnp.sin(angj))
    except Exception:
        inv32 = (10000.0 ** (-(np.arange(D // 2, dtype=np.float64) * 2.0) / D)).astype(np.float32)
        pos = np.arange(S, dtype=np.float32)
        a64 = (pos[:, None] * inv32[None, :]).astype(np.float64)
        cos = np.cos(a64).astype(np.float32)
        sin = np.sin(a64).astype(np.float32)
    cc = np.concatenate([cos, cos], axis=1)
    ss = np.concatenate([-sin, sin], axis=1)
    cc = np.ascontiguousarray(cc.reshape(NT, 128, 64).transpose(1, 0, 2).reshape(128, NT * 64))
    ss = np.ascontiguousarray(ss.reshape(NT, 128, 64).transpose(1, 0, 2).reshape(128, NT * 64))
    return cc, ss


def sl3(ap, u=16):
    return ap.rearrange("q (nb u) -> q nb u", u=u)


def bc3(a, nb, u=16):
    return a.rearrange("q (nb o) -> q nb o", o=1).broadcast_to([128, nb, u])


class QuantBufs:
    def __init__(self, sb, tag, W, fp32=True):
        self.W = W
        nb = W // 16
        self.nb = nb
        f = F32 if fp32 else BF16
        self.uab = sb(f"{tag}_uab", [128, W], f)
        self.sgn = sb(f"{tag}_sgn", [128, W], f)
        self.ua = sb(f"{tag}_ua", [128, W], f)
        self.shs = sb(f"{tag}_shs", [128, W], f)
        self.rr = [sb(f"{tag}_rrA", [128, W], f),
                   sb(f"{tag}_rrB", [128, W], f)]
        self.fa = sb(f"{tag}_fa", [128, W], f)
        self.bmax = sb(f"{tag}_bmax", [128, nb], F32)
        self.bm2 = sb(f"{tag}_bm2", [128, nb], F32)
        self.hs = sb(f"{tag}_hs", [128, nb], F32)
        self.rcp = sb(f"{tag}_rcp", [128, nb], F32)
        self.par = 0
        self.g_sc = None
        self.g_rr = [None, None]


def emit_quant(p, qb, src_ap, out_b, W, prev=None, out_war=None):
    nb = W // 16
    par = qb.par
    qb.par ^= 1
    rr = qb.rr[par]
    p.wait("scalar", qb.g_sc)
    p.op("scalar", lambda e: e.activation(out=qb.uab[:, 0:W], in_=src_ap, func=AFT.Abs))
    csg = p.op("scalar", lambda e: e.activation(out=qb.sgn[:, 0:W], in_=src_ap, func=AFT.Sign), inc="sc")
    p.wait("vector", csg)
    c = p.op("vector", lambda e: e.reduce_max(out=qb.bmax[:, 0:nb], in_=sl3(qb.uab[:, 0:W]), axis=AX.X), inc="dv")
    p.wait("vector", c)
    c = p.op("vector", lambda e: e.tensor_scalar_max(out=qb.bm2[:, 0:nb], in0=qb.bmax[:, 0:nb], scalar1=6e-12), inc="dv")
    p.wait("vector", c)
    p.op("vector", lambda e: e.tensor_scalar_mul(out=qb.hs[:, 0:nb], in0=qb.bm2[:, 0:nb], scalar1=1.0 / 12))
    c = p.op("vector", lambda e: e.reciprocal(out=qb.rcp[:, 0:nb], in_=qb.bm2[:, 0:nb]), inc="dv")
    p.wait("vector", c)
    cn = p.op("vector", lambda e: e.tensor_scalar_mul(out=qb.rcp[:, 0:nb], in0=qb.rcp[:, 0:nb], scalar1=6.0), inc="dv")
    p.wait("vector", cn)
    c_ua = p.op("vector", lambda e: e.tensor_tensor(out=sl3(qb.ua[:, 0:W]), in0=sl3(qb.uab[:, 0:W]), in1=bc3(qb.rcp[:, 0:nb], nb), op=ALU.mult), inc="dv")
    p.op("vector", lambda e: e.tensor_single_scalar(out=qb.fa[:, 0:W], in_=qb.ua[:, 0:W], scalar=0.25, op=ALU.is_gt))
    for thr in (0.75, 1.25, 1.75):
        p.op("vector", lambda e, thr=thr: e.scalar_tensor_tensor(out=qb.fa[:, 0:W], in0=qb.ua[:, 0:W], scalar=thr, in1=qb.fa[:, 0:W], op0=ALU.is_gt, op1=ALU.add))
    p.wait("vector", qb.g_rr[par])
    p.op("vector", lambda e: e.tensor_single_scalar(out=rr[:, 0:W], in_=qb.ua[:, 0:W], scalar=2.5, op=ALU.is_gt))
    for thr in (3.5, 5.0, 5.0):
        p.op("vector", lambda e, thr=thr: e.scalar_tensor_tensor(out=rr[:, 0:W], in0=qb.ua[:, 0:W], scalar=thr, in1=rr[:, 0:W], op0=ALU.is_gt, op1=ALU.add))
    cm = p.op("vector", lambda e: e.scalar_tensor_tensor(out=rr[:, 0:W], in0=rr[:, 0:W], scalar=2.0, in1=qb.fa[:, 0:W], op0=ALU.mult, op1=ALU.add), inc="dv")
    p.wait("gpsimd", [csg, cn])
    cshs = p.op("gpsimd", lambda e: e.tensor_tensor(out=sl3(qb.shs[:, 0:W]), in0=sl3(qb.sgn[:, 0:W]), in1=bc3(qb.hs[:, 0:nb], nb), op=ALU.mult), inc="pl")
    p.wait("gpsimd", [cm, out_war])
    cq = p.op("gpsimd", lambda e: e.tensor_tensor(out=out_b, in0=rr[:, 0:W], in1=qb.shs[:, 0:W], op=ALU.mult), inc="pl")
    qb.g_sc = [c_ua, cshs]
    qb.g_rr[par] = cq
    return cq


def build_nc(debug_phase=DEBUG_PHASE, races=False):
    nc = bass.Bass("TRN2", target_bir_lowering=False, detect_race_conditions=races)
    dt = lambda n, s, d, k: nc.dram_tensor(n, s, d, kind=k).ap()
    x_d = dt("x", [S, E], F32, "ExternalInput")
    wq_d = dt("wq", [E, E], F32, "ExternalInput")
    wk_d = dt("wk", [E, E], F32, "ExternalInput")
    wv_d = dt("wv", [E, E], F32, "ExternalInput")
    wo_d = dt("wo", [E, E], F32, "ExternalInput")
    wg1_d = dt("wg1", [E, D], F32, "ExternalInput")
    wg2_d = dt("wg2", [D, E], F32, "ExternalInput")
    wgt_d = dt("wgate", [IFF, E], F32, "ExternalInput")
    wup_d = dt("wup", [IFF, E], F32, "ExternalInput")
    wdn_d = dt("wdown", [E, IFF], F32, "ExternalInput")
    id_d = dt("ident", [128, 128], BF16, "ExternalInput")
    tri_d = dt("triu", [128, 128], BF16, "ExternalInput")
    ct_d = dt("ctab", [128, NT * 64], F32, "ExternalInput")
    st_d = dt("stab", [128, NT * 64], F32, "ExternalInput")
    gam_d = dt("gam", [128, E], F32, "ExternalInput")
    bet_d = dt("bet", [128, E], F32, "ExternalInput")
    out_d = dt("out", [S, E], F32, "ExternalOutput")
    wgT_s = dt("wgT_s", [E, IFF], BF16, "Internal")
    wuT_s = dt("wuT_s", [E, IFF], BF16, "Internal")
    wdT_s = dt("wdT_s", [IFF, E], BF16, "Internal")
    wo_s = dt("wo_s", [E, E], BF16, "Internal")
    attn_s = dt("attn_s", [S, E], BF16, "Internal")
    dbg = {}
    if debug_phase == 2:
        dbg["attn"] = dt("dbg_attn", [128, NT * E], BF16, "ExternalOutput")
    if debug_phase == 3:
        dbg["ln2"] = dt("dbg_ln2", [128, ECH * S], BF16, "ExternalOutput")

    with ExitStack() as st:
        sb = lambda n, s, d: st.enter_context(nc.sbuf_tensor(n, s, d))
        p = Prog(nc, st)

        ident = sb("identt", [128, 128], BF16)
        triu = sb("triut", [128, 128], BF16)
        ccb = sb("ccb", [128, NT * 64], BF16)
        ssb = sb("ssb", [128, NT * 64], BF16)
        gamb = sb("gamb", [128, E], BF16)
        betb = sb("betb", [128, E], BF16)
        xq_fm = sb("xq_fm", [128, ECH * S], BF16)

        with nc.sbuf_tensor("cload", [128, NT * 64], F32) as cload:
            t_id = p.dma("sync", ident[:], id_d[:, :])
            t_tri = p.dma("sync", triu[:], tri_d[:, :])
            cv = None
            for buf, d, w in ((gamb, gam_d, E), (betb, bet_d, E),
                              (ccb, ct_d, NT * 64), (ssb, st_d, NT * 64)):
                p.wait("sync", cv)
                tok = p.dma("sync", cload[:, 0:w], d[:, :])
                p.wait("vector", tok)
                cv = p.op("vector", lambda e, buf=buf, w=w: e.tensor_copy(out=buf[:], in_=cload[:, 0:w]), inc="dv")
            c_consts = cv
            p.wait("sync", c_consts)

        phaseW(nc, p, ident, wgt_d, wup_d, wdn_d, wo_d,
               wgT_s, wuT_s, wdT_s, wo_s, t_id)
        with nc.sbuf_tensor("ln2_fm", [128, ECH * S], BF16) as ln2_fm:
            with nc.sbuf_tensor("xn_fm", [128, ECH * S], BF16) as xn_fm:
                phase1(nc, p, x_d, ident, xn_fm, xq_fm, t_id)
                phase2(nc, p, wq_d, wk_d, wv_d, wg1_d, wg2_d,
                       ident, triu, ccb, ssb, xn_fm, attn_s, c_consts, t_tri,
                       dbg.get("attn"))
            phase3(nc, p, ident, attn_s, gamb, betb, ln2_fm)
            if debug_phase == 3:
                p.barrier()
                p.dma("gpsimd", dbg["ln2"][:, :], ln2_fm[:])
            if debug_phase >= 4:
                phase4(nc, p, x_d, xq_fm, ln2_fm, wgT_s, wuT_s, wdT_s, wo_s, out_d, ident)

        if debug_phase < 4:
            p.barrier()
            with nc.sbuf_tensor("zz", [128, E], F32) as zz:
                cz = p.op("vector", lambda e: e.memset(zz[:], 0.0), inc="dv")
                p.wait("scalar", cz)
                for t in range(NT):
                    p.dma("gpsimd", out_d[t * 128:(t + 1) * 128, :], zz[:])
        p.barrier()

        with nc.Block() as block:
            p.finalize(block)
    return nc


def phase1(nc, p, x_d, ident, xn_fm, xq_fm, t_id):
    p.barrier()
    with ExitStack() as s1:
        sb1 = lambda n, s, d: s1.enter_context(nc.sbuf_tensor(n, s, d))
        xr = [sb1(f"xr{i}", [128, E], F32) for i in range(2)]
        stat = sb1("stat", [128, 8], F32)
        xnf = sb1("xnf", [128, E], F32)
        xnb2 = [sb1(f"xnb{i}", [128, E], BF16) for i in range(2)]
        xqb2 = [sb1(f"xqb{i}", [128, E], BF16) for i in range(2)]
        qb = QuantBufs(sb1, "p1", E)
        with nc.psum_tensor("pt1", [128, 1024], BF16) as pt1:
            xr_rel = [None, None]
            tp_n = [None, None]
            tp_q = [None, None]
            evq_rel = None
            pend = {"v": None}

            def flush_p1():
                nonlocal evq_rel
                if pend["v"] is None:
                    return
                t, par, c_nb, c_q = pend["v"]
                pend["v"] = None
                p.wait("tensor", [c_nb, evq_rel, t_id])
                for cch in range(ECH):
                    fn = lambda e, cch=cch, par=par: e.transpose(
                        out=pt1[:, cch * 128:(cch + 1) * 128],
                        in_=xnb2[par][:, cch * 128:(cch + 1) * 128], identity=ident[:])
                    c_tp = p.op("tensor", fn, inc="te" if cch == ECH - 1 else None)
                tp_n[par] = c_tp
                p.wait("vector", c_tp)
                c_evn = p.op("vector", lambda e, t=t: e.tensor_copy(
                    out=xn_fm[:, t * 1024:(t + 1) * 1024], in_=pt1[:]), inc="dv")
                p.wait("tensor", [c_evn, c_q])
                for cch in range(ECH):
                    fn = lambda e, cch=cch, par=par: e.transpose(
                        out=pt1[:, cch * 128:(cch + 1) * 128],
                        in_=xqb2[par][:, cch * 128:(cch + 1) * 128], identity=ident[:])
                    c_tp = p.op("tensor", fn, inc="te" if cch == ECH - 1 else None)
                tp_q[par] = c_tp
                p.wait("vector", c_tp)
                evq_rel = p.op("vector", lambda e, t=t: e.tensor_copy(
                    out=xq_fm[:, t * 1024:(t + 1) * 1024], in_=pt1[:]), inc="dv")

            for t in range(NT):
                par = t % 2
                b = xr[t % 2]
                p.wait("sync", xr_rel[t % 2])
                c_x = p.dma("sync", b[:], x_d[t * 128:(t + 1) * 128, :])
                p.wait("vector", c_x)
                c = p.op("vector", lambda e, b=b: e.reduce_sum(out=stat[:, 0:1], in_=b[:], axis=AX.X), inc="dv")
                p.wait("vector", c)
                c_mu = p.op("vector", lambda e: e.tensor_scalar_mul(out=stat[:, 0:1], in0=stat[:, 0:1], scalar1=1.0 / E), inc="dv")
                p.wait("scalar", c_x)
                p.wait("scalar", qb.g_sc)
                c_sq = p.op("scalar", lambda e, b=b: e.activation(out=qb.uab[:], in_=b[:], func=AFT.Square, accum_out=stat[:, 1:2]), inc="sc")
                p.wait("vector", [c_sq, c_mu])
                p.op("vector", lambda e: e.tensor_scalar_mul(out=stat[:, 2:3], in0=stat[:, 1:2], scalar1=1.0 / E))
                c = p.op("vector", lambda e: e.tensor_tensor(out=stat[:, 3:4], in0=stat[:, 0:1], in1=stat[:, 0:1], op=ALU.mult), inc="dv")
                p.wait("vector", c)
                c = p.op("vector", lambda e: e.tensor_tensor(out=stat[:, 2:3], in0=stat[:, 2:3], in1=stat[:, 3:4], op=ALU.subtract), inc="dv")
                p.wait("vector", c)
                c = p.op("vector", lambda e: e.tensor_scalar_add(out=stat[:, 6:7], in0=stat[:, 2:3], scalar1=1e-5), inc="dv")
                p.wait("scalar", c)
                c_sqrt = p.op("scalar", lambda e: e.activation(out=stat[:, 4:5], in_=stat[:, 6:7], func=AFT.Sqrt), inc="sc")
                p.wait("vector", c_sqrt)
                c_rs = p.op("vector", lambda e: e.reciprocal(out=stat[:, 3:4], in_=stat[:, 4:5]), inc="dv")
                p.wait("vector", c_rs)
                c_xn = p.op("vector", lambda e, b=b: e.tensor_scalar(
                    out=xnf[:], in0=b[:], scalar1=stat[:, 0:1], scalar2=stat[:, 3:4],
                    op0=ALU.subtract, op1=ALU.mult), inc="dv")
                xr_rel[t % 2] = c_xn
                p.wait("scalar", [c_xn, tp_n[par]])
                c_nb = p.op("scalar", lambda e, par=par: e.activation(out=xnb2[par][:], in_=xnf[:], func=AFT.Identity), inc="sc")
                c_q = emit_quant(p, qb, xnf[:], xqb2[par][:, 0:E], E, out_war=tp_q[par])
                prev = pend["v"]
                pend["v"] = None
                if prev is not None:
                    pend["v"], prev2 = prev, None
                    flush_p1()
                pend["v"] = (t, par, c_nb, c_q)
            flush_p1()


def phaseW(nc, p, ident, wgt_d, wup_d, wdn_d, wo_d, wgT_s, wuT_s, wdT_s, wo_s, t_id):
    p.barrier()
    with ExitStack() as sw:
        sbw = lambda n, s, d: sw.enter_context(nc.sbuf_tensor(n, s, d))
        raw = [sbw(f"wraw{i}", [128, IFF], F32) for i in range(2)]
        qb = QuantBufs(sbw, "pw", IFF, fp32=True)
        wqb = sbw("wqb", [128, IFF], BF16)
        wtp = sbw("wtp", [128, 1024], BF16)
        with nc.psum_tensor("ptw", [128, 1024], BF16) as ptw:
            raw_rel = [None, None]
            state = {"stores": [], "evac": None, "tp": None, "pend": None}

            def flush_pend():
                if state["pend"] is None:
                    return
                c_tp, w_s, r, n_in, col_base = state["pend"]
                state["pend"] = None
                p.wait("vector", [c_tp] + state["stores"])
                state["evac"] = p.op("vector", lambda e, n_in=n_in: e.tensor_copy(
                    out=wtp[:, 0:n_in * 128], in_=ptw[:, 0:n_in * 128]), inc="dv")
                p.wait("scalar", state["evac"])
                state["stores"] = [
                    p.dma("gpsimd", w_s[(col_base + cl) * 128:(col_base + cl + 1) * 128, r * 128:(r + 1) * 128],
                          wtp[:, cl * 128:(cl + 1) * 128]) for cl in range(n_in)]

            def tp_batch(w_s, r, n_in, col_base, c_q):
                flush_pend()
                p.wait("tensor", [c_q, state["evac"], t_id])
                for cl in range(n_in):
                    ic = col_base + cl
                    fn = lambda e, cl=cl, ic=ic: e.transpose(
                        out=ptw[:, cl * 128:(cl + 1) * 128],
                        in_=wqb[:, ic * 128:(ic + 1) * 128], identity=ident[:])
                    c_tp = p.op("tensor", fn, inc="te" if cl == n_in - 1 else None)
                state["tp"] = c_tp
                state["pend"] = (c_tp, w_s, r, n_in, col_base)

            for wi, (w_d, w_s) in enumerate(((wgt_d, wgT_s), (wup_d, wuT_s))):
                for r in range(ICH):
                    b = raw[r % 2]
                    p.wait("sync", raw_rel[r % 2])
                    c_w = p.dma("sync", b[:, 0:E], w_d[r * 128:(r + 1) * 128, :])
                    p.wait("scalar", c_w)
                    p.wait("vector", c_w)
                    c_q = emit_quant(p, qb, b[:, 0:E], wqb[:, 0:E], E, out_war=state["tp"])
                    raw_rel[r % 2] = c_q
                    tp_batch(w_s, r, ECH, 0, c_q)
            for r in range(ECH):
                b = raw[r % 2]
                p.wait("sync", raw_rel[r % 2])
                c_w = p.dma("sync", b[:], wdn_d[r * 128:(r + 1) * 128, :])
                p.wait("scalar", c_w)
                p.wait("vector", c_w)
                c_q = emit_quant(p, qb, b[:], wqb[:], IFF, out_war=state["tp"])
                raw_rel[r % 2] = c_q
                for grp in range(3):
                    tp_batch(wdT_s, r, WICH[grp], grp * 8, c_q)
            flush_pend()
            wo_tok = None
            for r in range(ECH):
                b = raw[r % 2]
                p.wait("sync", raw_rel[r % 2])
                c_w = p.dma("sync", b[:, 0:E], wo_d[r * 128:(r + 1) * 128, :])
                p.wait("scalar", [c_w, wo_tok, state["tp"]] + state["stores"])
                state["stores"] = []
                c_c = p.op("scalar", lambda e, b=b: e.activation(out=wqb[:, 0:E], in_=b[:, 0:E], func=AFT.Identity), inc="sc")
                raw_rel[r % 2] = c_c
                p.wait("scalar", c_c)
                wo_tok = p.dma("gpsimd", wo_s[r * 128:(r + 1) * 128, :], wqb[:, 0:E])
            p.wait("vector", wo_tok)


def phase2(nc, p, wq_d, wk_d, wv_d, wg1_d, wg2_d, ident, triu, ccb, ssb, xn_fm, attn_s, c_consts, t_tri, dbg_attn=None):
    p.barrier()
    with ExitStack() as s2:
        sb2 = lambda n, s, d: s2.enter_context(nc.sbuf_tensor(n, s, d))
        wq_r = sb2("wq_r", [128, ECH * 1024], BF16)
        wk_r = sb2("wk_r", [128, ECH * 1024], BF16)
        wv_r = sb2("wv_r", [128, ECH * 1024], BF16)
        wg1_r = sb2("wg1_r", [128, ECH * 64], BF16)
        wg2_r = sb2("wg2_r", [64, 1024], BF16)
        gt_b = sb2("gt_b", [64, S], BF16)
        qs = sb2("qs", [128, 1024], BF16)
        ks = sb2("ks", [128, 1024], BF16)
        vs = sb2("vs", [128, 1024], BF16)
        gs = sb2("gs", [128, 1024], BF16)
        qro = sb2("qro", [128, 1024], BF16)
        kro = sb2("kro", [128, 1024], BF16)
        qzA = sb2("qzA", [128, 1024], BF16)
        qzB = sb2("qzB", [128, 1024], BF16)
        kzA = sb2("kzA", [128, 1024], BF16)
        kzB = sb2("kzB", [128, 1024], BF16)
        em = sb2("em", [128, 1024], BF16)
        S_sb = sb2("S_sb", [128, 512], BF16)
        ab = [sb2(f"ab{i}", [128, 1024], BF16) for i in range(2)]

        raw_scope = ExitStack()
        raw = [raw_scope.enter_context(nc.sbuf_tensor(f"w2raw{i}", [128, 1024], F32)) for i in range(2)]
        raw_rel = [None, None]
        i = 0
        for w_d, w_r in ((wq_d, wq_r), (wk_d, wk_r), (wv_d, wv_r)):
            for c in range(ECH):
                b = raw[i % 2]
                p.wait("sync", raw_rel[i % 2])
                c_w = p.dma("sync", b[:], w_d[c * 128:(c + 1) * 128, :])
                p.wait("vector", c_w)
                c_c = p.op("vector", lambda e, b=b, w_r=w_r, c=c: e.tensor_copy(out=w_r[:, c * 1024:(c + 1) * 1024], in_=b[:]), inc="dv")
                raw_rel[i % 2] = c_c
                i += 1
        for c in range(ECH):
            b = raw[i % 2]
            p.wait("sync", raw_rel[i % 2])
            c_w = p.dma("sync", b[:, 0:D], wg1_d[c * 128:(c + 1) * 128, :])
            p.wait("vector", c_w)
            c_c = p.op("vector", lambda e, b=b, c=c: e.tensor_copy(out=wg1_r[:, c * 64:(c + 1) * 64], in_=b[:, 0:D]), inc="dv")
            raw_rel[i % 2] = c_c
            i += 1
        b = raw[i % 2]
        p.wait("sync", raw_rel[i % 2])
        c_w = p.dma("sync", b[0:64, 0:E], wg2_d[:, :])
        p.wait("vector", c_w)
        c_wcast = p.op("vector", lambda e, b=b: e.tensor_copy(out=wg2_r[:, :], in_=b[0:64, 0:E]), inc="dv")
        c_szero = p.op("vector", lambda e: e.memset(S_sb[:], 0.0), inc="dv")
        p.op("vector", lambda e: e.memset(qzA[64:128, :], 0.0))
        p.op("vector", lambda e: e.memset(qzB[0:64, :], 0.0))
        p.op("vector", lambda e: e.memset(kzA[64:128, :], 0.0))
        c_zpad = p.op("vector", lambda e: e.memset(kzB[0:64, :], 0.0), inc="dv")
        p.barrier()
        raw_scope.close()
        if P2_LEVEL <= 1:
            return

        with (nc.psum_tensor("pa", [128, 1024], F32) as pa,
              nc.psum_tensor("pb", [128, 1024], F32) as pb,
              nc.psum_tensor("po", [128, 1024], F32) as po,
              nc.psum_tensor("pss", [128, 512], F32) as pss,
              nc.psum_tensor("pt2", [128, 1024], BF16) as pt2):

            p.wait("tensor", [c_wcast, c_consts])
            c_gt = None
            for th in (range(4) if "gt" not in P2_SKIP else []):
                p.wait("tensor", c_gt)
                for tt in range(4):
                    t = th * 4 + tt
                    for c in range(ECH):
                        fn = lambda e, t=t, c=c, tt=tt: e.matmul(
                            out=pa[0:64, tt * 128:(tt + 1) * 128],
                            lhsT=wg1_r[:, c * 64:(c + 1) * 64],
                            rhs=xn_fm[:, t * 1024 + c * 128:t * 1024 + (c + 1) * 128],
                            start=(c == 0), stop=(c == ECH - 1))
                        c_mm = p.op("tensor", fn, inc="te" if (c == ECH - 1 and tt == 3) else None)
                p.wait("scalar", c_mm)
                c_gt = p.op("scalar", lambda e, th=th: e.activation(out=gt_b[:, th * 512:(th + 1) * 512], in_=pa[0:64, 0:512], func=AFT.Identity), inc="sc")
            p.wait("tensor", c_gt)
            if P2_LEVEL <= 2:
                return

            po_rel = None
            pt2_rel = None
            ab_rel = [None, None]
            pa_rel = None
            pb_rel = None
            c_qro_prev = None
            c_kro_prev = None
            c_smm_prev = None
            s_evac_prev = c_szero
            for n in range(P2_CHUNKS):
                p.wait("tensor", pa_rel)
                for c in (range(ECH) if "qkv" not in P2_SKIP else []):
                    for hcol in range(2):
                        fn = lambda e, n=n, c=c, hcol=hcol: e.matmul(
                            out=pa[:, hcol * 512:(hcol + 1) * 512],
                            lhsT=xn_fm[:, n * 1024 + c * 128:n * 1024 + (c + 1) * 128],
                            rhs=wq_r[:, c * 1024 + hcol * 512:c * 1024 + (hcol + 1) * 512],
                            start=(c == 0), stop=(c == ECH - 1))
                        c_qmm = p.op("tensor", fn, inc="te" if (c == ECH - 1 and hcol == 1) else None)
                if "qkv" not in P2_SKIP:
                    p.wait("scalar", [c_qmm, c_qro_prev])
                    c_qs = p.op("scalar", lambda e: e.activation(out=qs[:], in_=pa[:], func=AFT.Silu), inc="sc")
                else:
                    c_qs = None
                p.wait("tensor", pb_rel)
                for c in (range(ECH) if "qkv" not in P2_SKIP else []):
                    for hcol in range(2):
                        fn = lambda e, n=n, c=c, hcol=hcol: e.matmul(
                            out=pb[:, hcol * 512:(hcol + 1) * 512],
                            lhsT=xn_fm[:, n * 1024 + c * 128:n * 1024 + (c + 1) * 128],
                            rhs=wk_r[:, c * 1024 + hcol * 512:c * 1024 + (hcol + 1) * 512],
                            start=(c == 0), stop=(c == ECH - 1))
                        c_kmm = p.op("tensor", fn, inc="te" if (c == ECH - 1 and hcol == 1) else None)
                if "qkv" not in P2_SKIP:
                    p.wait("scalar", [c_kmm, c_kro_prev])
                    c_ks = p.op("scalar", lambda e: e.activation(out=ks[:], in_=pb[:], func=AFT.Silu), inc="sc")
                else:
                    c_ks = None
                p.wait("tensor", c_qs)
                for c in (range(ECH) if "qkv" not in P2_SKIP else []):
                    for hcol in range(2):
                        fn = lambda e, n=n, c=c, hcol=hcol: e.matmul(
                            out=pa[:, hcol * 512:(hcol + 1) * 512],
                            lhsT=xn_fm[:, n * 1024 + c * 128:n * 1024 + (c + 1) * 128],
                            rhs=wv_r[:, c * 1024 + hcol * 512:c * 1024 + (hcol + 1) * 512],
                            start=(c == 0), stop=(c == ECH - 1))
                        c_vmm = p.op("tensor", fn, inc="te" if (c == ECH - 1 and hcol == 1) else None)
                if "qkv" not in P2_SKIP:
                    p.wait("scalar", [c_vmm, c_smm_prev])
                    c_vs = p.op("scalar", lambda e: e.activation(out=vs[:], in_=pa[:], func=AFT.Identity), inc="sc")
                else:
                    c_vs = None
                p.wait("tensor", c_ks)
                for hcol in (range(2) if "gate" not in P2_SKIP else []):
                    fn = lambda e, n=n, hcol=hcol: e.matmul(
                        out=pb[:, hcol * 512:(hcol + 1) * 512],
                        lhsT=gt_b[:, n * 128:(n + 1) * 128],
                        rhs=wg2_r[:, hcol * 512:(hcol + 1) * 512],
                        start=True, stop=True)
                    c_gmm = p.op("tensor", fn, inc="te" if hcol == 1 else None)
                if "gate" not in P2_SKIP:
                    p.wait("scalar", [c_gmm, po_rel])
                    c_gs = p.op("scalar", lambda e: e.activation(out=gs[:], in_=pb[:], func=AFT.Sigmoid), inc="sc")
                else:
                    c_gs = None

                def rope(src, dst, c_src, n=n):
                    p.wait("vector", c_src)
                    s3 = lambda a: a[:].rearrange("q (h j) -> q h j", j=64)
                    cc_t = ccb[:, n * 64:(n + 1) * 64].rearrange("q (o j) -> q o j", o=1).broadcast_to([128, 16, 64])
                    ssA = ssb[:, n * 64:n * 64 + 32].rearrange("q (o j) -> q o j", o=1).broadcast_to([128, 16, 32]).rearrange("q h (o j) -> q h o j", o=1)
                    ssB = ssb[:, n * 64 + 32:n * 64 + 64].rearrange("q (o j) -> q o j", o=1).broadcast_to([128, 16, 32]).rearrange("q h (o j) -> q h o j", o=1)
                    src4 = src[:].rearrange("q (h k j) -> q h k j", k=2, j=32)
                    tmp4 = em[:, 0:1024].rearrange("q (h k j) -> q h k j", k=2, j=32)
                    dst4 = dst[:].rearrange("q (h k j) -> q h k j", k=2, j=32)
                    p.op("vector", lambda e: e.tensor_tensor(out=s3(em[:, 0:1024]), in0=s3(src), in1=cc_t, op=ALU.mult))
                    p.op("vector", lambda e: e.tensor_tensor(out=dst4[:, :, 0:1, :], in0=src4[:, :, 1:2, :], in1=ssA, op=ALU.mult))
                    p.op("vector", lambda e: e.tensor_tensor(out=dst4[:, :, 0:1, :], in0=dst4[:, :, 0:1, :], in1=tmp4[:, :, 0:1, :], op=ALU.add))
                    p.op("vector", lambda e: e.tensor_tensor(out=dst4[:, :, 1:2, :], in0=src4[:, :, 0:1, :], in1=ssB, op=ALU.mult))
                    return p.op("vector", lambda e: e.tensor_tensor(out=dst4[:, :, 1:2, :], in0=dst4[:, :, 1:2, :], in1=tmp4[:, :, 1:2, :], op=ALU.add), inc="dv")
                if "rope" not in P2_SKIP:
                    c_qro = rope(qs, qro, c_qs)
                    c_qro_prev = c_qro
                    p.wait("vector", c_smm_prev)
                    c_kro = rope(ks, kro, c_ks)
                    c_kro_prev = c_kro
                else:
                    c_qro = c_kro = None

                p.wait("tensor", [c_qro, pt2_rel])
                for cch in (range(ECH) if "tp" not in P2_SKIP else []):
                    fn = lambda e, cch=cch: e.transpose(
                        out=pt2[:, cch * 128:(cch + 1) * 128],
                        in_=qro[:, cch * 128:(cch + 1) * 128], identity=ident[:])
                    c_tp = p.op("tensor", fn, inc="te" if cch == ECH - 1 else None)
                if "tp" not in P2_SKIP:
                    p.wait("vector", c_tp)
                    p.op("vector", lambda e: e.tensor_copy(out=qzA[0:64, :], in_=pt2[0:64, :]))
                    c_qf = p.op("vector", lambda e: e.tensor_copy(out=qzB[64:128, :], in_=pt2[64:128, :]), inc="dv")
                else:
                    c_qf = None
                p.wait("tensor", [c_kro, c_qf])
                for cch in (range(ECH) if "tp" not in P2_SKIP else []):
                    fn = lambda e, cch=cch: e.transpose(
                        out=pt2[:, cch * 128:(cch + 1) * 128],
                        in_=kro[:, cch * 128:(cch + 1) * 128], identity=ident[:])
                    c_tp = p.op("tensor", fn, inc="te" if cch == ECH - 1 else None)
                if "tp" not in P2_SKIP:
                    p.wait("vector", c_tp)
                    p.op("vector", lambda e: e.tensor_copy(out=kzA[0:64, :], in_=pt2[0:64, :]))
                    c_kf = p.op("vector", lambda e: e.tensor_copy(out=kzB[64:128, :], in_=pt2[64:128, :]), inc="dv")
                    pt2_rel = c_kf
                else:
                    c_kf = None

                if P2_LEVEL <= 3:
                    pt2_rel = c_kf
                    pa_rel = c_vs
                    pb_rel = c_gs
                    c_qro_prev = c_qro
                    c_kro_prev = c_kro
                    continue
                p.wait("tensor", c_kf)
                c_e = [None, None]
                for half, pXX in ((0, pa), (1, pb)):
                    p.wait("tensor", c_vs if half == 0 else c_gs)
                    for hl in range(8):
                        h = half * 8 + hl
                        kz = kzA if h % 2 == 0 else kzB
                        qz = qzA if h % 2 == 0 else qzB
                        cc2 = h // 2
                        fn = lambda e, kz=kz, qz=qz, cc2=cc2, hl=hl, pXX=pXX: e.matmul(
                            out=pXX[:, hl * 128:(hl + 1) * 128],
                            lhsT=kz[:, cc2 * 128:(cc2 + 1) * 128],
                            rhs=qz[:, cc2 * 128:(cc2 + 1) * 128],
                            start=True, stop=True)
                        c_e[half] = p.op("tensor", fn, inc="te" if hl == 7 else None)
                tri8 = triu[:, :].rearrange("q (o j) -> q o j", o=1).broadcast_to([128, 8, 128])
                if P2_LEVEL <= 4 or P2_LEVEL in (35, 36, 41, 42, 43, 44):
                    if P2_LEVEL == 41: hsel = [0]
                    elif P2_LEVEL == 42: hsel = [1]
                    elif P2_LEVEL == 43: hsel = [0, 1]
                    elif P2_LEVEL == 44: hsel = [0, 2, 4, 6]
                    else: hsel = list(range(8))
                    for half, pXX in ((0, pa), (1, pb)):
                        p.wait("tensor", c_vs if half == 0 else c_gs)
                        for hl in hsel:
                            h = half * 8 + hl
                            kz = kzA if h % 2 == 0 else kzB
                            qz = qzA if h % 2 == 0 else qzB
                            cc2 = h // 2
                            fn = lambda e, kz=kz, qz=qz, cc2=cc2, hl=hl, pXX=pXX: e.matmul(
                                out=pXX[:, hl * 128:(hl + 1) * 128],
                                lhsT=kz[:, cc2 * 128:(cc2 + 1) * 128],
                                rhs=qz[:, cc2 * 128:(cc2 + 1) * 128],
                                start=True, stop=True)
                            c_eh = p.op("tensor", fn, inc="te" if hl == hsel[-1] else None)
                        p.wait("vector", [c_eh, t_tri])
                        if P2_LEVEL in (35, 41, 42, 43, 44):
                            c_em = p.op("vector", lambda e, pXX=pXX: e.tensor_copy(out=em[:], in_=pXX[:]), inc="dv")
                        elif P2_LEVEL == 36:
                            c_em = p.op("vector", lambda e, pXX=pXX: e.tensor_tensor(
                                out=em[:].rearrange("q (hh j) -> q hh j", j=128), in0=pXX[:].rearrange("q (hh j) -> q hh j", j=128), in1=tri8, op=ALU.mult), inc="dv")
                        else:
                            c_em = p.op("vector", lambda e, pXX=pXX: e.tensor_tensor(
                                out=em[:].rearrange("q (hh j) -> q hh j", j=128), in0=pXX[:].rearrange("q (hh j) -> q hh j", j=128), in1=tri8, op=ALU.mult), inc="dv")
                        if half == 0:
                            pa_rel = c_em
                        else:
                            pb_rel = c_em
                    pt2_rel = c_kf
                    c_qro_prev = c_qro
                    c_kro_prev = c_kro
                    continue
                c_omm = None
                for half, pXX in ((0, pa), (1, pb)):
                    p.wait("vector", [c_e[half], t_tri, c_omm])
                    c_em = p.op("vector", lambda e, pXX=pXX: e.tensor_tensor(
                        out=em[:].rearrange("q (hh j) -> q hh j", j=128), in0=pXX[:].rearrange("q (hh j) -> q hh j", j=128), in1=tri8, op=ALU.mult), inc="dv")
                    if half == 0:
                        pa_rel = c_em
                    else:
                        pb_rel = c_em
                    p.wait("tensor", [c_em, po_rel, s_evac_prev])
                    for hl in range(8):
                        h = half * 8 + hl
                        bp = 64 * (h % 2)
                        cc2 = h // 2
                        fn1 = lambda e, h=h, hl=hl: e.matmul(
                            out=po[:, h * 64:(h + 1) * 64],
                            lhsT=em[:, hl * 128:(hl + 1) * 128],
                            rhs=vs[:, h * 64:(h + 1) * 64],
                            start=True, stop=False)
                        p.op("tensor", fn1)
                        qz = qzA if h % 2 == 0 else qzB
                        fn2 = lambda e, h=h, qz=qz, cc2=cc2: e.matmul(
                            out=po[:, h * 64:(h + 1) * 64],
                            lhsT=qz[:, cc2 * 128:(cc2 + 1) * 128],
                            rhs=S_sb[:, cc2 * 64:(cc2 + 1) * 64],
                            start=False, stop=True)
                        c_omm = p.op("tensor", fn2, inc="te" if hl == 7 else None)

                for h in range(H):
                    bp = 64 * (h % 2)
                    fn = lambda e, h=h, bp=bp, n=n: e.matmul(
                        out=pss[bp:bp + 64, (h // 2) * 64:(h // 2 + 1) * 64],
                        lhsT=kro[:, h * 64:(h + 1) * 64],
                        rhs=vs[:, h * 64:(h + 1) * 64],
                        start=(n == 0 and h < 2), stop=(n == NT - 1 and h >= H - 2),
                        tile_position=(0, bp), skip_group_check=True)
                    c_smm = p.op("tensor", fn, inc="te" if h == H - 1 else None)
                c_smm_prev = c_smm
                p.wait("scalar", c_smm)
                s_evac_prev = p.op("scalar", lambda e: e.activation(out=S_sb[:], in_=pss[:], func=AFT.Identity), inc="sc")

                p.wait("vector", [c_omm, c_gs, ab_rel[n % 2]])
                po_rel = p.op("vector", lambda e, n=n: e.tensor_tensor(
                    out=ab[n % 2][:], in0=po[:], in1=gs[:], op=ALU.mult), inc="dv")
                p.wait("scalar", po_rel)
                ab_rel[n % 2] = p.dma("gpsimd", attn_s[n * 128:(n + 1) * 128, :], ab[n % 2][:])
                if dbg_attn is not None:
                    ab_rel[n % 2] = p.dma("gpsimd", dbg_attn[:, n * 1024:(n + 1) * 1024], ab[n % 2][:])


def phase3(nc, p, ident, attn_s, gamb, betb, ln2_fm):
    p.barrier()
    with ExitStack() as s3:
        sb3 = lambda n, s, d: s3.enter_context(nc.sbuf_tensor(n, s, d))
        stat = sb3("stat3", [128, 8], F32)
        n1 = sb3("n1", [128, E], BF16)
        sq = sb3("sq3", [128, E], F32)
        ln2b = sb3("ln2b", [128, E], BF16)
        abuf = [sb3(f"a3_{i}", [128, E], BF16) for i in range(2)]
        with nc.psum_tensor("pt3", [128, 1024], BF16) as pt3:
            ev_rel = None
            n1_rel = None
            a_rel = [None, None]
            for t in range(NT):
                p.wait("sync", a_rel[t % 2])
                c_a = p.dma("sync", abuf[t % 2][:], attn_s[t * 128:(t + 1) * 128, :])
                a = abuf[t % 2][:]
                p.wait("vector", c_a)
                p.wait("scalar", c_a)
                c = p.op("vector", lambda e, a=a: e.reduce_sum(out=stat[:, 0:1], in_=a, axis=AX.X), inc="dv")
                p.wait("vector", c)
                c_mu = p.op("vector", lambda e: e.tensor_scalar_mul(out=stat[:, 0:1], in0=stat[:, 0:1], scalar1=1.0 / E), inc="dv")
                c_sq = p.op("scalar", lambda e, a=a: e.activation(out=sq[:], in_=a, func=AFT.Square, accum_out=stat[:, 1:2]), inc="sc")
                p.wait("vector", [c_sq, c_mu])
                p.op("vector", lambda e: e.tensor_scalar_mul(out=stat[:, 2:3], in0=stat[:, 1:2], scalar1=1.0 / E))
                c = p.op("vector", lambda e: e.tensor_tensor(out=stat[:, 3:4], in0=stat[:, 0:1], in1=stat[:, 0:1], op=ALU.mult), inc="dv")
                p.wait("vector", c)
                c = p.op("vector", lambda e: e.tensor_tensor(out=stat[:, 2:3], in0=stat[:, 2:3], in1=stat[:, 3:4], op=ALU.subtract), inc="dv")
                p.wait("vector", c)
                c = p.op("vector", lambda e: e.tensor_scalar_add(out=stat[:, 6:7], in0=stat[:, 2:3], scalar1=1e-5), inc="dv")
                p.wait("scalar", c)
                c_sqrt = p.op("scalar", lambda e: e.activation(out=stat[:, 4:5], in_=stat[:, 6:7], func=AFT.Sqrt), inc="sc")
                p.wait("vector", c_sqrt)
                c_rs = p.op("vector", lambda e: e.reciprocal(out=stat[:, 3:4], in_=stat[:, 4:5]), inc="dv")
                p.wait("vector", [c_rs, n1_rel])
                c_n1 = p.op("vector", lambda e, a=a: e.tensor_scalar(
                    out=n1[:], in0=a, scalar1=stat[:, 0:1], scalar2=stat[:, 3:4],
                    op0=ALU.subtract, op1=ALU.mult), inc="dv")
                a_rel[t % 2] = c_n1
                p.wait("vector", c_n1)
                p.op("vector", lambda e: e.tensor_tensor(out=ln2b[:], in0=n1[:], in1=gamb[:], op=ALU.mult))
                c_l2 = p.op("vector", lambda e: e.tensor_tensor(out=ln2b[:], in0=ln2b[:], in1=betb[:], op=ALU.add), inc="dv")
                n1_rel = c_l2
                p.wait("tensor", [c_l2, ev_rel])
                for cch in range(ECH):
                    fn = lambda e, cch=cch: e.transpose(
                        out=pt3[:, cch * 128:(cch + 1) * 128],
                        in_=ln2b[:, cch * 128:(cch + 1) * 128], identity=ident[:])
                    c_tp = p.op("tensor", fn, inc="te" if cch == ECH - 1 else None)
                p.wait("vector", c_tp)
                ev_rel = p.op("vector", lambda e, t=t: e.tensor_copy(out=ln2_fm[:, t * 1024:(t + 1) * 1024], in_=pt3[:]), inc="dv")


def phase4(nc, p, x_d, xq_fm, ln2_fm, wgT_s, wuT_s, wdT_s, wo_s, out_d, ident):
    p.barrier()
    with ExitStack() as s4:
        sb4 = lambda n, s, d: s4.enter_context(nc.sbuf_tensor(n, s, d))
        ffn_acc = sb4("ffn_acc", [128, 8 * 1024], F32)
        wgs = sb4("wgs", [128, ECH * 1024], BF16)
        wus = sb4("wus", [128, ECH * 1024], BF16)
        wds = sb4("wds", [128, 8 * 1024], BF16)
        wos = wds
        sg = sb4("sg", [128, 1024], BF16)
        ub = sb4("ub", [128, 1024], BF16)
        hb = sb4("hb", [128, 1024], F32)
        hqb = sb4("hqb", [128, 1024], BF16)
        hqf = sb4("hqf", [128, 1024], BF16)
        qb = QuantBufs(sb4, "p4", 1024, fp32=True)
        xrt = sb4("xrt", [128, E], F32)
        outb = sb4("outb", [128, E], F32)
        with (nc.psum_tensor("pg", [128, 1024], F32) as pg,
              nc.psum_tensor("pu", [128, 1024], F32) as pu,
              nc.psum_tensor("pf", [128, 1024], F32) as pf,
              nc.psum_tensor("pt4", [128, 1024], BF16) as pt4):
            w_rel = None
            c_wo_prev = None
            pt4_rel = None
            hq_rel = None
            pf_rel = None
            x_rel = None
            out_rel = None
            for grp in range(2):
                for ip in range(3):
                    W = WIPW[ip]
                    NIC = WICH[ip]
                    p.wait("sync", [w_rel, c_wo_prev])
                    ldw = []
                    for c in range(ECH):
                        ldw.append(p.dma("sync", wgs[:, c * 1024:c * 1024 + W], wgT_s[c * 128:(c + 1) * 128, ip * 1024:ip * 1024 + W]))
                        ldw.append(p.dma("sync", wus[:, c * 1024:c * 1024 + W], wuT_s[c * 128:(c + 1) * 128, ip * 1024:ip * 1024 + W]))
                    for icl in range(NIC):
                        ic = ip * 8 + icl
                        ldw.append(p.dma("sync", wds[:, icl * 1024:(icl + 1) * 1024], wdT_s[ic * 128:(ic + 1) * 128, :]))
                    p.wait("tensor", ldw)
                    for tl in range(8):
                        t = grp * 8 + tl
                        for c in range(ECH):
                            for hcol in range((W + 511) // 512):
                                lo, hi = hcol * 512, min(W, (hcol + 1) * 512)
                                fn = lambda e, t=t, c=c, lo=lo, hi=hi: e.matmul(
                                    out=pg[:, lo:hi],
                                    lhsT=xq_fm[:, t * 1024 + c * 128:t * 1024 + (c + 1) * 128],
                                    rhs=wgs[:, c * 1024 + lo:c * 1024 + hi],
                                    start=(c == 0), stop=(c == ECH - 1))
                                p.op("tensor", fn)
                                fn = lambda e, t=t, c=c, lo=lo, hi=hi: e.matmul(
                                    out=pu[:, lo:hi],
                                    lhsT=xq_fm[:, t * 1024 + c * 128:t * 1024 + (c + 1) * 128],
                                    rhs=wus[:, c * 1024 + lo:c * 1024 + hi],
                                    start=(c == 0), stop=(c == ECH - 1))
                                c_gu = p.op("tensor", fn, inc="te" if (c == ECH - 1 and hi == W) else None)
                        p.wait("scalar", [c_gu, hq_rel])
                        p.op("scalar", lambda e, W=W: e.activation(out=sg[:, 0:W], in_=pg[:, 0:W], func=AFT.Silu))
                        c_ub = p.op("scalar", lambda e, W=W: e.activation(out=ub[:, 0:W], in_=pu[:, 0:W], func=AFT.Identity), inc="sc")
                        p.wait("vector", c_ub)
                        c_h = p.op("vector", lambda e, W=W: e.tensor_tensor(out=hb[:, 0:W], in0=sg[:, 0:W], in1=ub[:, 0:W], op=ALU.mult), inc="dv")
                        p.wait("scalar", c_h)
                        p.wait("vector", c_h)
                        c_hq = emit_quant(p, qb, hb[:, 0:W], hqb[:, 0:W], W, prev=hq_rel)
                        hq_rel = c_hq
                        p.wait("tensor", [c_hq, pt4_rel])
                        for icl in range(NIC):
                            fn = lambda e, icl=icl: e.transpose(
                                out=pt4[:, icl * 128:(icl + 1) * 128],
                                in_=hqb[:, icl * 128:(icl + 1) * 128], identity=ident[:])
                            c_tp = p.op("tensor", fn, inc="te" if icl == NIC - 1 else None)
                        p.wait("vector", c_tp)
                        c_hqf = p.op("vector", lambda e, NIC=NIC: e.tensor_copy(out=hqf[:, 0:NIC * 128], in_=pt4[:, 0:NIC * 128]), inc="dv")
                        pt4_rel = c_hqf
                        p.wait("tensor", [c_hqf, pf_rel])
                        for icl in range(NIC):
                            for hcol in range(2):
                                fn = lambda e, icl=icl, hcol=hcol, NIC=NIC: e.matmul(
                                    out=pf[:, hcol * 512:(hcol + 1) * 512],
                                    lhsT=hqf[:, icl * 128:(icl + 1) * 128],
                                    rhs=wds[:, icl * 1024 + hcol * 512:icl * 1024 + (hcol + 1) * 512],
                                    start=(icl == 0), stop=(icl == NIC - 1))
                                c_dn = p.op("tensor", fn, inc="te" if (icl == NIC - 1 and hcol == 1) else None)
                        p.wait("vector", c_dn)
                        if ip == 0:
                            pf_rel = p.op("vector", lambda e, tl=tl: e.tensor_copy(out=ffn_acc[:, tl * 1024:(tl + 1) * 1024], in_=pf[:]), inc="dv")
                        else:
                            pf_rel = p.op("vector", lambda e, tl=tl: e.tensor_tensor(out=ffn_acc[:, tl * 1024:(tl + 1) * 1024], in0=ffn_acc[:, tl * 1024:(tl + 1) * 1024], in1=pf[:], op=ALU.add), inc="dv")
                    w_rel = c_dn
                p.wait("sync", w_rel)
                ldo = []
                for c in range(ECH):
                    ldo.append(p.dma("sync", wos[:, c * 1024:(c + 1) * 1024], wo_s[c * 128:(c + 1) * 128, :]))
                p.wait("tensor", ldo)
                for tl in range(8):
                    t = grp * 8 + tl
                    p.wait("sync", x_rel)
                    c_x = p.dma("sync", xrt[:], x_d[t * 128:(t + 1) * 128, :])
                    p.wait("tensor", pf_rel)
                    for c in range(ECH):
                        for hcol in range(2):
                            fn = lambda e, t=t, c=c, hcol=hcol: e.matmul(
                                out=pf[:, hcol * 512:(hcol + 1) * 512],
                                lhsT=ln2_fm[:, t * 1024 + c * 128:t * 1024 + (c + 1) * 128],
                                rhs=wos[:, c * 1024 + hcol * 512:c * 1024 + (hcol + 1) * 512],
                                start=(c == 0), stop=(c == ECH - 1))
                            c_wo = p.op("tensor", fn, inc="te" if (c == ECH - 1 and hcol == 1) else None)
                    c_wo_prev = c_wo
                    p.wait("vector", [c_wo, c_x, out_rel])
                    c_o1 = p.op("vector", lambda e, tl=tl: e.tensor_tensor(out=outb[:], in0=pf[:], in1=ffn_acc[:, tl * 1024:(tl + 1) * 1024], op=ALU.add), inc="dv")
                    pf_rel = c_o1
                    p.wait("vector", c_o1)
                    c_o2 = p.op("vector", lambda e: e.tensor_tensor(out=outb[:], in0=outb[:], in1=xrt[:], op=ALU.add), inc="dv")
                    x_rel = c_o2
                    p.wait("scalar", c_o2)
                    out_rel = p.dma("gpsimd", out_d[t * 128:(t + 1) * 128, :], outb[:])


def make_consts():
    ident = np.eye(128, dtype=ml_dtypes.bfloat16)
    tri = np.triu(np.ones((128, 128), dtype=np.float32)).astype(ml_dtypes.bfloat16)
    return ident, tri


_NC_CACHE = {}

def _get_nc():
    if "nc" not in _NC_CACHE:
        _NC_CACHE["nc"] = build_nc(99)
    return _NC_CACHE["nc"]


def kernel(L0_attn_x, L0_attn_Wq, L0_attn_Wk, L0_attn_Wv, L0_attn_Wo,
           L0_attn_Wg1, L0_attn_Wg2, L0_attn_gamma, L0_attn_beta,
           L0_ffn_w_gate, L0_ffn_w_up, L0_ffn_w_down):
    x = np.ascontiguousarray(np.asarray(L0_attn_x, dtype=np.float32))
    ident, tri = make_consts()
    cc, ss = rope_tables()
    gam = np.ascontiguousarray(np.broadcast_to(np.asarray(L0_attn_gamma, np.float32), (128, E)))
    bet = np.ascontiguousarray(np.broadcast_to(np.asarray(L0_attn_beta, np.float32), (128, E)))
    common = {
        "wq": np.ascontiguousarray(np.asarray(L0_attn_Wq, np.float32)),
        "wk": np.ascontiguousarray(np.asarray(L0_attn_Wk, np.float32)),
        "wv": np.ascontiguousarray(np.asarray(L0_attn_Wv, np.float32)),
        "wo": np.ascontiguousarray(np.asarray(L0_attn_Wo, np.float32)),
        "wg1": np.ascontiguousarray(np.asarray(L0_attn_Wg1, np.float32)),
        "wg2": np.ascontiguousarray(np.asarray(L0_attn_Wg2, np.float32)),
        "wgate": np.ascontiguousarray(np.asarray(L0_ffn_w_gate, np.float32)),
        "wup": np.ascontiguousarray(np.asarray(L0_ffn_w_up, np.float32)),
        "wdown": np.ascontiguousarray(np.asarray(L0_ffn_w_down, np.float32)),
        "ident": ident, "triu": tri, "ctab": cc, "stab": ss,
        "gam": gam, "bet": bet,
    }
    nc = _get_nc()
    in_maps = [dict(common, x=np.ascontiguousarray(x[c])) for c in range(x.shape[0])]
    res = run_bass_kernel_spmd(nc, in_maps, core_ids=list(range(len(in_maps))))
    return np.stack([r["out"] for r in res.results], axis=0)


# version 13
# speedup vs baseline: 1.0221x; 1.0221x over previous
import numpy as np
import ml_dtypes
import concourse.bass as bass
import concourse.mybir as mybir
from concourse.bass_utils import run_bass_kernel_spmd
from contextlib import ExitStack

F32 = mybir.dt.float32
BF16 = mybir.dt.bfloat16
I16 = mybir.dt.int16
ALU = mybir.AluOpType
AX = mybir.AxisListType
AFT = mybir.ActivationFunctionType

S, E, IFF = 2048, 1024, 2816
NT = S // 128
ECH = E // 128
H, D = 16, 64
NBE = E // 16
ICH = IFF // 128
WICH = (8, 8, 6)
WIPW = (1024, 1024, 768)

DEBUG_PHASE = 99
P2_LEVEL = 5
P2_CHUNKS = NT
P2_SKIP = set()
N_SQ = 12
N_GQ = 8


class Prog:
    ENGINES = ("sync", "vector", "scalar", "tensor", "gpsimd")

    def __init__(self, nc, st):
        self.nc = nc
        names = ["dv", "sc", "te", "pl"]
        names += [f"sq{i}" for i in range(N_SQ)]
        names += [f"gq{i}" for i in range(N_GQ)]
        self.sems = {n: st.enter_context(nc.semaphore(n)) for n in names}
        self.cnt = {n: 0 for n in names}
        self.q = {e: [] for e in self.ENGINES}
        self.slot_rr = {"sync": 0, "gpsimd": 0}

    def emit(self, eng, fn):
        self.q[eng].append(fn)

    def wait(self, eng, tok):
        if tok is None:
            return
        if isinstance(tok, list):
            best = {}
            for t in tok:
                if t is None:
                    continue
                s, v = t
                best[s] = max(best.get(s, 0), v)
            for s, v in best.items():
                self._wait1(eng, s, v)
            return
        self._wait1(eng, tok[0], tok[1])

    def _wait1(self, eng, sem, val):
        if val <= 0:
            return
        h = self.sems[sem]
        self.emit(eng, lambda e, h=h, v=val: e.wait_ge(h, v))

    def op(self, eng, fn, inc=None, k=1):
        if inc is None:
            self.emit(eng, fn)
            return None
        h = self.sems[inc]
        self.cnt[inc] += k
        tgt = self.cnt[inc]
        self.emit(eng, lambda e, fn=fn, h=h, k=k: fn(e).then_inc(h, k))
        return (inc, tgt)

    def dma(self, eng, out, in_, tp=False):
        pool, npool = ("sq", N_SQ) if eng == "sync" else ("gq", N_GQ)
        i = self.slot_rr[eng] % npool
        self.slot_rr[eng] += 1
        iss = "scalar" if eng == "gpsimd" else eng
        sem = f"{pool}{i}"
        prev = self.cnt[sem]
        if prev > 0:
            self._wait1(iss, sem, prev)
        self.cnt[sem] += 16
        tgt = self.cnt[sem]
        h = self.sems[sem]
        if tp:
            self.emit(iss, lambda e, o=out, i_=in_, h=h: e.dma_start_transpose(out=o, in_=i_).then_inc(h, 16))
        else:
            self.emit(iss, lambda e, o=out, i_=in_, h=h: e.dma_start(out=o, in_=i_).then_inc(h, 16))
        return (sem, tgt)

    def barrier(self):
        snap = dict(self.cnt)
        for eng in self.ENGINES:
            for s, v in snap.items():
                if v > 0:
                    self._wait1(eng, s, v)

    def finalize(self, block):
        for eng in self.ENGINES:
            fns = self.q[eng]
            if not fns:
                continue
            getattr(block, eng)(lambda e, fns=fns: [f(e) for f in fns])


def rope_tables():
    try:
        import jax
        import jax.numpy as jnp
        with jax.default_device(jax.devices("cpu")[0]):
            inv_freq = 10000.0 ** (-(jnp.arange(D // 2, dtype=jnp.float32) * 2.0) / D)
            posj = jnp.arange(S, dtype=jnp.float32) + 0
            angj = posj[:, None] * inv_freq[None, :]
            cos = np.asarray(jnp.cos(angj))
            sin = np.asarray(jnp.sin(angj))
    except Exception:
        inv32 = (10000.0 ** (-(np.arange(D // 2, dtype=np.float64) * 2.0) / D)).astype(np.float32)
        pos = np.arange(S, dtype=np.float32)
        a64 = (pos[:, None] * inv32[None, :]).astype(np.float64)
        cos = np.cos(a64).astype(np.float32)
        sin = np.sin(a64).astype(np.float32)
    cc = np.concatenate([cos, cos], axis=1)
    ss = np.concatenate([-sin, sin], axis=1)
    cc = np.ascontiguousarray(cc.reshape(NT, 128, 64).transpose(1, 0, 2).reshape(128, NT * 64))
    ss = np.ascontiguousarray(ss.reshape(NT, 128, 64).transpose(1, 0, 2).reshape(128, NT * 64))
    return cc, ss


def sl3(ap, u=16):
    return ap.rearrange("q (nb u) -> q nb u", u=u)


def bc3(a, nb, u=16):
    return a.rearrange("q (nb o) -> q nb o", o=1).broadcast_to([128, nb, u])


class QuantBufs:
    def __init__(self, sb, tag, W, fp32=True):
        self.W = W
        nb = W // 16
        self.nb = nb
        f = F32 if fp32 else BF16
        self.uab = sb(f"{tag}_uab", [128, W], f)
        self.sgn = sb(f"{tag}_sgn", [128, W], f)
        self.ua = sb(f"{tag}_ua", [128, W], f)
        self.shs = sb(f"{tag}_shs", [128, W], f)
        self.rr = [sb(f"{tag}_rrA", [128, W], f),
                   sb(f"{tag}_rrB", [128, W], f)]
        self.fa = sb(f"{tag}_fa", [128, W], f)
        self.bmax = sb(f"{tag}_bmax", [128, nb], F32)
        self.bm2 = sb(f"{tag}_bm2", [128, nb], F32)
        self.hs = sb(f"{tag}_hs", [128, nb], F32)
        self.rcp = sb(f"{tag}_rcp", [128, nb], F32)
        self.par = 0
        self.g_sc = None
        self.g_rr = [None, None]


def emit_quant(p, qb, src_ap, out_b, W, prev=None, out_war=None):
    nb = W // 16
    par = qb.par
    qb.par ^= 1
    rr = qb.rr[par]
    p.wait("scalar", qb.g_sc)
    p.op("scalar", lambda e: e.activation(out=qb.uab[:, 0:W], in_=src_ap, func=AFT.Abs))
    csg = p.op("scalar", lambda e: e.activation(out=qb.sgn[:, 0:W], in_=src_ap, func=AFT.Sign), inc="sc")
    p.wait("vector", csg)
    c = p.op("vector", lambda e: e.reduce_max(out=qb.bmax[:, 0:nb], in_=sl3(qb.uab[:, 0:W]), axis=AX.X), inc="dv")
    p.wait("vector", c)
    c = p.op("vector", lambda e: e.tensor_scalar_max(out=qb.bm2[:, 0:nb], in0=qb.bmax[:, 0:nb], scalar1=6e-12), inc="dv")
    p.wait("vector", c)
    p.op("vector", lambda e: e.tensor_scalar_mul(out=qb.hs[:, 0:nb], in0=qb.bm2[:, 0:nb], scalar1=1.0 / 12))
    c = p.op("vector", lambda e: e.reciprocal(out=qb.rcp[:, 0:nb], in_=qb.bm2[:, 0:nb]), inc="dv")
    p.wait("vector", c)
    cn = p.op("vector", lambda e: e.tensor_scalar_mul(out=qb.rcp[:, 0:nb], in0=qb.rcp[:, 0:nb], scalar1=6.0), inc="dv")
    p.wait("vector", cn)
    c_ua = p.op("vector", lambda e: e.tensor_tensor(out=sl3(qb.ua[:, 0:W]), in0=sl3(qb.uab[:, 0:W]), in1=bc3(qb.rcp[:, 0:nb], nb), op=ALU.mult), inc="dv")
    p.op("vector", lambda e: e.tensor_single_scalar(out=qb.fa[:, 0:W], in_=qb.ua[:, 0:W], scalar=0.25, op=ALU.is_gt))
    for thr in (0.75, 1.25, 1.75):
        p.op("vector", lambda e, thr=thr: e.scalar_tensor_tensor(out=qb.fa[:, 0:W], in0=qb.ua[:, 0:W], scalar=thr, in1=qb.fa[:, 0:W], op0=ALU.is_gt, op1=ALU.add))
    p.wait("vector", qb.g_rr[par])
    p.op("vector", lambda e: e.tensor_single_scalar(out=rr[:, 0:W], in_=qb.ua[:, 0:W], scalar=2.5, op=ALU.is_gt))
    for thr in (3.5, 5.0, 5.0):
        p.op("vector", lambda e, thr=thr: e.scalar_tensor_tensor(out=rr[:, 0:W], in0=qb.ua[:, 0:W], scalar=thr, in1=rr[:, 0:W], op0=ALU.is_gt, op1=ALU.add))
    cm = p.op("vector", lambda e: e.scalar_tensor_tensor(out=rr[:, 0:W], in0=rr[:, 0:W], scalar=2.0, in1=qb.fa[:, 0:W], op0=ALU.mult, op1=ALU.add), inc="dv")
    p.wait("gpsimd", [csg, cn])
    cshs = p.op("gpsimd", lambda e: e.tensor_tensor(out=sl3(qb.shs[:, 0:W]), in0=sl3(qb.sgn[:, 0:W]), in1=bc3(qb.hs[:, 0:nb], nb), op=ALU.mult), inc="pl")
    p.wait("gpsimd", [cm, out_war])
    cq = p.op("gpsimd", lambda e: e.tensor_tensor(out=out_b, in0=rr[:, 0:W], in1=qb.shs[:, 0:W], op=ALU.mult), inc="pl")
    qb.g_sc = [c_ua, cshs]
    qb.g_rr[par] = cq
    return cq


def build_nc(debug_phase=DEBUG_PHASE, races=False):
    nc = bass.Bass("TRN2", target_bir_lowering=False, detect_race_conditions=races)
    dt = lambda n, s, d, k: nc.dram_tensor(n, s, d, kind=k).ap()
    x_d = dt("x", [S, E], F32, "ExternalInput")
    wq_d = dt("wq", [E, E], F32, "ExternalInput")
    wk_d = dt("wk", [E, E], F32, "ExternalInput")
    wv_d = dt("wv", [E, E], F32, "ExternalInput")
    wo_d = dt("wo", [E, E], F32, "ExternalInput")
    wg1_d = dt("wg1", [E, D], F32, "ExternalInput")
    wg2_d = dt("wg2", [D, E], F32, "ExternalInput")
    wgt_d = dt("wgate", [IFF, E], F32, "ExternalInput")
    wup_d = dt("wup", [IFF, E], F32, "ExternalInput")
    wdn_d = dt("wdown", [E, IFF], F32, "ExternalInput")
    id_d = dt("ident", [128, 128], BF16, "ExternalInput")
    tri_d = dt("triu", [128, 128], BF16, "ExternalInput")
    ct_d = dt("ctab", [128, NT * 64], F32, "ExternalInput")
    st_d = dt("stab", [128, NT * 64], F32, "ExternalInput")
    gam_d = dt("gam", [128, E], F32, "ExternalInput")
    bet_d = dt("bet", [128, E], F32, "ExternalInput")
    out_d = dt("out", [S, E], F32, "ExternalOutput")
    wg_q = dt("wg_q", [IFF, E], BF16, "Internal")
    wu_q = dt("wu_q", [IFF, E], BF16, "Internal")
    wd_q = dt("wd_q", [E, IFF], BF16, "Internal")
    ln2_s = dt("ln2_s", [128, NT * 1024], BF16, "Internal")
    wo_s = dt("wo_s", [E, E], BF16, "Internal")
    attn_s = dt("attn_s", [S, E], BF16, "Internal")
    dbg = {}
    if debug_phase == 2:
        dbg["attn"] = dt("dbg_attn", [128, NT * E], BF16, "ExternalOutput")
    if debug_phase == 3:
        dbg["ln2"] = dt("dbg_ln2", [128, ECH * S], BF16, "ExternalOutput")

    with ExitStack() as st:
        sb = lambda n, s, d: st.enter_context(nc.sbuf_tensor(n, s, d))
        p = Prog(nc, st)

        ident = sb("identt", [128, 128], BF16)
        triu = sb("triut", [128, 128], BF16)
        ccb = sb("ccb", [128, NT * 64], BF16)
        ssb = sb("ssb", [128, NT * 64], BF16)
        gamb = sb("gamb", [128, E], BF16)
        betb = sb("betb", [128, E], BF16)
        xq_fm = sb("xq_fm", [128, ECH * S], BF16)

        with nc.sbuf_tensor("cload", [128, NT * 64], F32) as cload:
            t_id = p.dma("sync", ident[:], id_d[:, :])
            t_tri = p.dma("sync", triu[:], tri_d[:, :])
            cv = None
            for buf, d, w in ((gamb, gam_d, E), (betb, bet_d, E),
                              (ccb, ct_d, NT * 64), (ssb, st_d, NT * 64)):
                p.wait("sync", cv)
                tok = p.dma("sync", cload[:, 0:w], d[:, :])
                p.wait("vector", tok)
                cv = p.op("vector", lambda e, buf=buf, w=w: e.tensor_copy(out=buf[:], in_=cload[:, 0:w]), inc="dv")
            c_consts = cv
            p.wait("sync", c_consts)

        ws_stack = ExitStack()
        ws = WSteps(nc, p, ws_stack, wgt_d, wup_d, wdn_d, wo_d, wg_q, wu_q, wd_q, wo_s)
        with nc.sbuf_tensor("xn_fm", [128, ECH * S], BF16) as xn_fm:
            phase1(nc, p, x_d, ident, xn_fm, xq_fm, t_id, ws=ws)
            phase2(nc, p, wq_d, wk_d, wv_d, wg1_d, wg2_d,
                   ident, triu, ccb, ssb, xn_fm, attn_s, c_consts, t_tri,
                   dbg.get("attn"), ws=ws)
        phase3(nc, p, ident, attn_s, gamb, betb, ln2_s, ws=ws)
        ws.drain()
        ws_stack.close()
        if debug_phase == 3:
            p.barrier()
            p.dma("gpsimd", dbg["ln2"][:, :], ln2_s[:, :])
        if debug_phase >= 4:
            phase4(nc, p, x_d, xq_fm, ln2_s, wg_q, wu_q, wd_q, wo_s, out_d, ident, ws.stores)

        if debug_phase < 4:
            p.barrier()
            with nc.sbuf_tensor("zz", [128, E], F32) as zz:
                cz = p.op("vector", lambda e: e.memset(zz[:], 0.0), inc="dv")
                p.wait("scalar", cz)
                for t in range(NT):
                    p.dma("gpsimd", out_d[t * 128:(t + 1) * 128, :], zz[:])
        p.barrier()

        with nc.Block() as block:
            p.finalize(block)
    return nc


def phase1(nc, p, x_d, ident, xn_fm, xq_fm, t_id, ws=None):
    p.barrier()
    with ExitStack() as s1:
        sb1 = lambda n, s, d: s1.enter_context(nc.sbuf_tensor(n, s, d))
        xr = [sb1(f"xr{i}", [128, E], F32) for i in range(2)]
        stat = sb1("stat", [128, 8], F32)
        xnf = sb1("xnf", [128, E], F32)
        xnb2 = [sb1(f"xnb{i}", [128, E], BF16) for i in range(2)]
        xqb2 = [sb1(f"xqb{i}", [128, E], BF16) for i in range(2)]
        qb = QuantBufs(sb1, "p1", E)
        with nc.psum_tensor("pt1", [128, 1024], BF16) as pt1:
            xr_rel = [None, None]
            tp_n = [None, None]
            tp_q = [None, None]
            evq_rel = None
            pend = {"v": None}

            def flush_p1():
                nonlocal evq_rel
                if pend["v"] is None:
                    return
                t, par, c_nb, c_q = pend["v"]
                pend["v"] = None
                p.wait("tensor", [c_nb, evq_rel, t_id])
                for cch in range(ECH):
                    fn = lambda e, cch=cch, par=par: e.transpose(
                        out=pt1[:, cch * 128:(cch + 1) * 128],
                        in_=xnb2[par][:, cch * 128:(cch + 1) * 128], identity=ident[:])
                    c_tp = p.op("tensor", fn, inc="te" if cch == ECH - 1 else None)
                tp_n[par] = c_tp
                p.wait("vector", c_tp)
                c_evn = p.op("vector", lambda e, t=t: e.tensor_copy(
                    out=xn_fm[:, t * 1024:(t + 1) * 1024], in_=pt1[:]), inc="dv")
                p.wait("tensor", [c_evn, c_q])
                for cch in range(ECH):
                    fn = lambda e, cch=cch, par=par: e.transpose(
                        out=pt1[:, cch * 128:(cch + 1) * 128],
                        in_=xqb2[par][:, cch * 128:(cch + 1) * 128], identity=ident[:])
                    c_tp = p.op("tensor", fn, inc="te" if cch == ECH - 1 else None)
                tp_q[par] = c_tp
                p.wait("vector", c_tp)
                evq_rel = p.op("vector", lambda e, t=t: e.tensor_copy(
                    out=xq_fm[:, t * 1024:(t + 1) * 1024], in_=pt1[:]), inc="dv")

            for t in range(NT):
                par = t % 2
                b = xr[t % 2]
                p.wait("sync", xr_rel[t % 2])
                c_x = p.dma("sync", b[:], x_d[t * 128:(t + 1) * 128, :])
                p.wait("vector", c_x)
                c = p.op("vector", lambda e, b=b: e.reduce_sum(out=stat[:, 0:1], in_=b[:], axis=AX.X), inc="dv")
                p.wait("vector", c)
                c_mu = p.op("vector", lambda e: e.tensor_scalar_mul(out=stat[:, 0:1], in0=stat[:, 0:1], scalar1=1.0 / E), inc="dv")
                p.wait("scalar", c_x)
                p.wait("scalar", qb.g_sc)
                c_sq = p.op("scalar", lambda e, b=b: e.activation(out=qb.uab[:], in_=b[:], func=AFT.Square, accum_out=stat[:, 1:2]), inc="sc")
                p.wait("vector", [c_sq, c_mu])
                p.op("vector", lambda e: e.tensor_scalar_mul(out=stat[:, 2:3], in0=stat[:, 1:2], scalar1=1.0 / E))
                c = p.op("vector", lambda e: e.tensor_tensor(out=stat[:, 3:4], in0=stat[:, 0:1], in1=stat[:, 0:1], op=ALU.mult), inc="dv")
                p.wait("vector", c)
                c = p.op("vector", lambda e: e.tensor_tensor(out=stat[:, 2:3], in0=stat[:, 2:3], in1=stat[:, 3:4], op=ALU.subtract), inc="dv")
                p.wait("vector", c)
                c = p.op("vector", lambda e: e.tensor_scalar_add(out=stat[:, 6:7], in0=stat[:, 2:3], scalar1=1e-5), inc="dv")
                p.wait("scalar", c)
                c_sqrt = p.op("scalar", lambda e: e.activation(out=stat[:, 4:5], in_=stat[:, 6:7], func=AFT.Sqrt), inc="sc")
                p.wait("vector", c_sqrt)
                c_rs = p.op("vector", lambda e: e.reciprocal(out=stat[:, 3:4], in_=stat[:, 4:5]), inc="dv")
                p.wait("vector", c_rs)
                c_xn = p.op("vector", lambda e, b=b: e.tensor_scalar(
                    out=xnf[:], in0=b[:], scalar1=stat[:, 0:1], scalar2=stat[:, 3:4],
                    op0=ALU.subtract, op1=ALU.mult), inc="dv")
                xr_rel[t % 2] = c_xn
                p.wait("scalar", [c_xn, tp_n[par]])
                c_nb = p.op("scalar", lambda e, par=par: e.activation(out=xnb2[par][:], in_=xnf[:], func=AFT.Identity), inc="sc")
                c_q = emit_quant(p, qb, xnf[:], xqb2[par][:, 0:E], E, out_war=tp_q[par])
                prev = pend["v"]
                pend["v"] = None
                if prev is not None:
                    pend["v"], prev2 = prev, None
                    flush_p1()
                pend["v"] = (t, par, c_nb, c_q)
                if ws is not None:
                    ws.pump(1)
            flush_p1()


class WSteps:

    def __init__(self, nc, p, stack, wgt_d, wup_d, wdn_d, wo_d, wg_q, wu_q, wd_q, wo_s):
        sbw = lambda n, s, d: stack.enter_context(nc.sbuf_tensor(n, s, d))
        self.p = p
        self.raw = [sbw(f"wraw{i}", [128, 1024], F32) for i in range(2)]
        self.wqb = [sbw(f"wqbo{i}", [128, 1024], BF16) for i in range(2)]
        self.qb = QuantBufs(sbw, "pw", 1024, fp32=True)
        self.raw_rel = [None, None]
        self.store_war = [None, None]
        self.stores = []
        self.steps = []
        self.k = 0
        self.i = 0

        def quant_step(w_d, dst, r0, c0, w):
            def do():
                p = self.p
                li = self.k
                self.k += 1
                b = self.raw[li % 2]
                ob = self.wqb[li % 2]
                p.wait("sync", self.raw_rel[li % 2])
                c_w = p.dma("sync", b[:, 0:w], w_d[r0:r0 + 128, c0:c0 + w])
                p.wait("scalar", c_w)
                cq = emit_quant(p, self.qb, b[:, 0:w], ob[:, 0:w], w, out_war=self.store_war[li % 2])
                self.raw_rel[li % 2] = cq
                p.wait("scalar", cq)
                tok = p.dma("gpsimd", dst[r0:r0 + 128, c0:c0 + w], ob[:, 0:w])
                self.store_war[li % 2] = tok
                self.stores.append(tok)
            return do

        def wo_step(r):
            def do():
                p = self.p
                li = self.k
                self.k += 1
                b = self.raw[li % 2]
                ob = self.wqb[li % 2]
                p.wait("sync", self.raw_rel[li % 2])
                c_w = p.dma("sync", b[:, 0:1024], wo_d[r * 128:(r + 1) * 128, :])
                p.wait("scalar", [c_w, self.store_war[li % 2]])
                cc = p.op("scalar", lambda e, b=b, ob=ob: e.activation(out=ob[:, 0:1024], in_=b[:, 0:1024], func=AFT.Identity), inc="sc")
                self.raw_rel[li % 2] = cc
                p.wait("scalar", cc)
                tok = p.dma("gpsimd", wo_s[r * 128:(r + 1) * 128, :], ob[:, 0:1024])
                self.store_war[li % 2] = tok
                self.stores.append(tok)
            return do

        for w_d, dst in ((wgt_d, wg_q), (wup_d, wu_q)):
            for r in range(ICH):
                self.steps.append(quant_step(w_d, dst, r * 128, 0, E))
        for r in range(ECH):
            for c0, w in ((0, 1024), (1024, 1024), (2048, 768)):
                self.steps.append(quant_step(wdn_d, wd_q, r * 128, c0, w))
        for r in range(ECH):
            self.steps.append(wo_step(r))

    def pump(self, k):
        while k > 0 and self.i < len(self.steps):
            self.steps[self.i]()
            self.i += 1
            k -= 1

    def drain(self):
        self.pump(len(self.steps))


def phase2(nc, p, wq_d, wk_d, wv_d, wg1_d, wg2_d, ident, triu, ccb, ssb, xn_fm, attn_s, c_consts, t_tri, dbg_attn=None, ws=None):
    p.barrier()
    with ExitStack() as s2:
        sb2 = lambda n, s, d: s2.enter_context(nc.sbuf_tensor(n, s, d))
        wq_r = sb2("wq_r", [128, ECH * 1024], BF16)
        wk_r = sb2("wk_r", [128, ECH * 1024], BF16)
        wv_r = sb2("wv_r", [128, ECH * 1024], BF16)
        wg1_r = sb2("wg1_r", [128, ECH * 64], BF16)
        wg2_r = sb2("wg2_r", [64, 1024], BF16)
        gt_b = sb2("gt_b", [64, S], BF16)
        qs = sb2("qs", [128, 1024], BF16)
        ks = sb2("ks", [128, 1024], BF16)
        vs = sb2("vs", [128, 1024], BF16)
        gs = sb2("gs", [128, 1024], BF16)
        qro = sb2("qro", [128, 1024], BF16)
        kro = sb2("kro", [128, 1024], BF16)
        qzA = sb2("qzA", [128, 1024], BF16)
        qzB = sb2("qzB", [128, 1024], BF16)
        kzA = sb2("kzA", [128, 1024], BF16)
        kzB = sb2("kzB", [128, 1024], BF16)
        em = sb2("em", [128, 1024], BF16)
        S_sb = sb2("S_sb", [128, 512], BF16)
        ab = [sb2(f"ab{i}", [128, 1024], BF16) for i in range(2)]

        raw_scope = ExitStack()
        raw = [raw_scope.enter_context(nc.sbuf_tensor(f"w2raw{i}", [128, 1024], F32)) for i in range(2)]
        raw_rel = [None, None]
        i = 0
        for w_d, w_r in ((wq_d, wq_r), (wk_d, wk_r), (wv_d, wv_r)):
            for c in range(ECH):
                b = raw[i % 2]
                p.wait("sync", raw_rel[i % 2])
                c_w = p.dma("sync", b[:], w_d[c * 128:(c + 1) * 128, :])
                p.wait("vector", c_w)
                c_c = p.op("vector", lambda e, b=b, w_r=w_r, c=c: e.tensor_copy(out=w_r[:, c * 1024:(c + 1) * 1024], in_=b[:]), inc="dv")
                raw_rel[i % 2] = c_c
                i += 1
        for c in range(ECH):
            b = raw[i % 2]
            p.wait("sync", raw_rel[i % 2])
            c_w = p.dma("sync", b[:, 0:D], wg1_d[c * 128:(c + 1) * 128, :])
            p.wait("vector", c_w)
            c_c = p.op("vector", lambda e, b=b, c=c: e.tensor_copy(out=wg1_r[:, c * 64:(c + 1) * 64], in_=b[:, 0:D]), inc="dv")
            raw_rel[i % 2] = c_c
            i += 1
        b = raw[i % 2]
        p.wait("sync", raw_rel[i % 2])
        c_w = p.dma("sync", b[0:64, 0:E], wg2_d[:, :])
        p.wait("vector", c_w)
        c_wcast = p.op("vector", lambda e, b=b: e.tensor_copy(out=wg2_r[:, :], in_=b[0:64, 0:E]), inc="dv")
        c_szero = p.op("vector", lambda e: e.memset(S_sb[:], 0.0), inc="dv")
        p.op("vector", lambda e: e.memset(qzA[64:128, :], 0.0))
        p.op("vector", lambda e: e.memset(qzB[0:64, :], 0.0))
        p.op("vector", lambda e: e.memset(kzA[64:128, :], 0.0))
        c_zpad = p.op("vector", lambda e: e.memset(kzB[0:64, :], 0.0), inc="dv")
        p.barrier()
        raw_scope.close()
        if ws is not None:
            ws.pump(4)
        if P2_LEVEL <= 1:
            return

        with (nc.psum_tensor("pa", [128, 1024], F32) as pa,
              nc.psum_tensor("pb", [128, 1024], F32) as pb,
              nc.psum_tensor("po", [128, 1024], F32) as po,
              nc.psum_tensor("pss", [128, 512], F32) as pss,
              nc.psum_tensor("pt2", [128, 1024], BF16) as pt2):

            p.wait("tensor", [c_wcast, c_consts])
            c_gt = None
            for th in (range(4) if "gt" not in P2_SKIP else []):
                p.wait("tensor", c_gt)
                for tt in range(4):
                    t = th * 4 + tt
                    for c in range(ECH):
                        fn = lambda e, t=t, c=c, tt=tt: e.matmul(
                            out=pa[0:64, tt * 128:(tt + 1) * 128],
                            lhsT=wg1_r[:, c * 64:(c + 1) * 64],
                            rhs=xn_fm[:, t * 1024 + c * 128:t * 1024 + (c + 1) * 128],
                            start=(c == 0), stop=(c == ECH - 1))
                        c_mm = p.op("tensor", fn, inc="te" if (c == ECH - 1 and tt == 3) else None)
                p.wait("scalar", c_mm)
                c_gt = p.op("scalar", lambda e, th=th: e.activation(out=gt_b[:, th * 512:(th + 1) * 512], in_=pa[0:64, 0:512], func=AFT.Identity), inc="sc")
            p.wait("tensor", c_gt)
            if P2_LEVEL <= 2:
                return

            po_rel = None
            pt2_rel = None
            ab_rel = [None, None]
            pa_rel = None
            pb_rel = None
            c_qro_prev = None
            c_kro_prev = None
            c_smm_prev = None
            s_evac_prev = c_szero
            for n in range(P2_CHUNKS):
                p.wait("tensor", pa_rel)
                for c in (range(ECH) if "qkv" not in P2_SKIP else []):
                    for hcol in range(2):
                        fn = lambda e, n=n, c=c, hcol=hcol: e.matmul(
                            out=pa[:, hcol * 512:(hcol + 1) * 512],
                            lhsT=xn_fm[:, n * 1024 + c * 128:n * 1024 + (c + 1) * 128],
                            rhs=wq_r[:, c * 1024 + hcol * 512:c * 1024 + (hcol + 1) * 512],
                            start=(c == 0), stop=(c == ECH - 1))
                        c_qmm = p.op("tensor", fn, inc="te" if (c == ECH - 1 and hcol == 1) else None)
                if "qkv" not in P2_SKIP:
                    p.wait("scalar", [c_qmm, c_qro_prev])
                    c_qs = p.op("scalar", lambda e: e.activation(out=qs[:], in_=pa[:], func=AFT.Silu), inc="sc")
                else:
                    c_qs = None
                p.wait("tensor", pb_rel)
                for c in (range(ECH) if "qkv" not in P2_SKIP else []):
                    for hcol in range(2):
                        fn = lambda e, n=n, c=c, hcol=hcol: e.matmul(
                            out=pb[:, hcol * 512:(hcol + 1) * 512],
                            lhsT=xn_fm[:, n * 1024 + c * 128:n * 1024 + (c + 1) * 128],
                            rhs=wk_r[:, c * 1024 + hcol * 512:c * 1024 + (hcol + 1) * 512],
                            start=(c == 0), stop=(c == ECH - 1))
                        c_kmm = p.op("tensor", fn, inc="te" if (c == ECH - 1 and hcol == 1) else None)
                if "qkv" not in P2_SKIP:
                    p.wait("scalar", [c_kmm, c_kro_prev])
                    c_ks = p.op("scalar", lambda e: e.activation(out=ks[:], in_=pb[:], func=AFT.Silu), inc="sc")
                else:
                    c_ks = None
                p.wait("tensor", c_qs)
                for c in (range(ECH) if "qkv" not in P2_SKIP else []):
                    for hcol in range(2):
                        fn = lambda e, n=n, c=c, hcol=hcol: e.matmul(
                            out=pa[:, hcol * 512:(hcol + 1) * 512],
                            lhsT=xn_fm[:, n * 1024 + c * 128:n * 1024 + (c + 1) * 128],
                            rhs=wv_r[:, c * 1024 + hcol * 512:c * 1024 + (hcol + 1) * 512],
                            start=(c == 0), stop=(c == ECH - 1))
                        c_vmm = p.op("tensor", fn, inc="te" if (c == ECH - 1 and hcol == 1) else None)
                if "qkv" not in P2_SKIP:
                    p.wait("scalar", [c_vmm, c_smm_prev])
                    c_vs = p.op("scalar", lambda e: e.activation(out=vs[:], in_=pa[:], func=AFT.Identity), inc="sc")
                else:
                    c_vs = None
                p.wait("tensor", c_ks)
                for hcol in (range(2) if "gate" not in P2_SKIP else []):
                    fn = lambda e, n=n, hcol=hcol: e.matmul(
                        out=pb[:, hcol * 512:(hcol + 1) * 512],
                        lhsT=gt_b[:, n * 128:(n + 1) * 128],
                        rhs=wg2_r[:, hcol * 512:(hcol + 1) * 512],
                        start=True, stop=True)
                    c_gmm = p.op("tensor", fn, inc="te" if hcol == 1 else None)
                if "gate" not in P2_SKIP:
                    p.wait("scalar", [c_gmm, po_rel])
                    c_gs = p.op("scalar", lambda e: e.activation(out=gs[:], in_=pb[:], func=AFT.Sigmoid), inc="sc")
                else:
                    c_gs = None

                def rope(src, dst, c_src, n=n):
                    p.wait("vector", c_src)
                    s3 = lambda a: a[:].rearrange("q (h j) -> q h j", j=64)
                    cc_t = ccb[:, n * 64:(n + 1) * 64].rearrange("q (o j) -> q o j", o=1).broadcast_to([128, 16, 64])
                    ssA = ssb[:, n * 64:n * 64 + 32].rearrange("q (o j) -> q o j", o=1).broadcast_to([128, 16, 32]).rearrange("q h (o j) -> q h o j", o=1)
                    ssB = ssb[:, n * 64 + 32:n * 64 + 64].rearrange("q (o j) -> q o j", o=1).broadcast_to([128, 16, 32]).rearrange("q h (o j) -> q h o j", o=1)
                    src4 = src[:].rearrange("q (h k j) -> q h k j", k=2, j=32)
                    tmp4 = em[:, 0:1024].rearrange("q (h k j) -> q h k j", k=2, j=32)
                    dst4 = dst[:].rearrange("q (h k j) -> q h k j", k=2, j=32)
                    p.op("vector", lambda e: e.tensor_tensor(out=s3(em[:, 0:1024]), in0=s3(src), in1=cc_t, op=ALU.mult))
                    p.op("vector", lambda e: e.tensor_tensor(out=dst4[:, :, 0:1, :], in0=src4[:, :, 1:2, :], in1=ssA, op=ALU.mult))
                    p.op("vector", lambda e: e.tensor_tensor(out=dst4[:, :, 0:1, :], in0=dst4[:, :, 0:1, :], in1=tmp4[:, :, 0:1, :], op=ALU.add))
                    p.op("vector", lambda e: e.tensor_tensor(out=dst4[:, :, 1:2, :], in0=src4[:, :, 0:1, :], in1=ssB, op=ALU.mult))
                    return p.op("vector", lambda e: e.tensor_tensor(out=dst4[:, :, 1:2, :], in0=dst4[:, :, 1:2, :], in1=tmp4[:, :, 1:2, :], op=ALU.add), inc="dv")
                if "rope" not in P2_SKIP:
                    c_qro = rope(qs, qro, c_qs)
                    c_qro_prev = c_qro
                    p.wait("vector", c_smm_prev)
                    c_kro = rope(ks, kro, c_ks)
                    c_kro_prev = c_kro
                else:
                    c_qro = c_kro = None

                p.wait("tensor", [c_qro, pt2_rel])
                for cch in (range(ECH) if "tp" not in P2_SKIP else []):
                    fn = lambda e, cch=cch: e.transpose(
                        out=pt2[:, cch * 128:(cch + 1) * 128],
                        in_=qro[:, cch * 128:(cch + 1) * 128], identity=ident[:])
                    c_tp = p.op("tensor", fn, inc="te" if cch == ECH - 1 else None)
                if "tp" not in P2_SKIP:
                    p.wait("vector", c_tp)
                    p.op("vector", lambda e: e.tensor_copy(out=qzA[0:64, :], in_=pt2[0:64, :]))
                    c_qf = p.op("vector", lambda e: e.tensor_copy(out=qzB[64:128, :], in_=pt2[64:128, :]), inc="dv")
                else:
                    c_qf = None
                p.wait("tensor", [c_kro, c_qf])
                for cch in (range(ECH) if "tp" not in P2_SKIP else []):
                    fn = lambda e, cch=cch: e.transpose(
                        out=pt2[:, cch * 128:(cch + 1) * 128],
                        in_=kro[:, cch * 128:(cch + 1) * 128], identity=ident[:])
                    c_tp = p.op("tensor", fn, inc="te" if cch == ECH - 1 else None)
                if "tp" not in P2_SKIP:
                    p.wait("vector", c_tp)
                    p.op("vector", lambda e: e.tensor_copy(out=kzA[0:64, :], in_=pt2[0:64, :]))
                    c_kf = p.op("vector", lambda e: e.tensor_copy(out=kzB[64:128, :], in_=pt2[64:128, :]), inc="dv")
                    pt2_rel = c_kf
                else:
                    c_kf = None

                if P2_LEVEL <= 3:
                    pt2_rel = c_kf
                    pa_rel = c_vs
                    pb_rel = c_gs
                    c_qro_prev = c_qro
                    c_kro_prev = c_kro
                    continue
                p.wait("tensor", c_kf)
                c_e = [None, None]
                for half, pXX in ((0, pa), (1, pb)):
                    p.wait("tensor", c_vs if half == 0 else c_gs)
                    for hl in range(8):
                        h = half * 8 + hl
                        kz = kzA if h % 2 == 0 else kzB
                        qz = qzA if h % 2 == 0 else qzB
                        cc2 = h // 2
                        fn = lambda e, kz=kz, qz=qz, cc2=cc2, hl=hl, pXX=pXX: e.matmul(
                            out=pXX[:, hl * 128:(hl + 1) * 128],
                            lhsT=kz[:, cc2 * 128:(cc2 + 1) * 128],
                            rhs=qz[:, cc2 * 128:(cc2 + 1) * 128],
                            start=True, stop=True)
                        c_e[half] = p.op("tensor", fn, inc="te" if hl == 7 else None)
                tri8 = triu[:, :].rearrange("q (o j) -> q o j", o=1).broadcast_to([128, 8, 128])
                if P2_LEVEL <= 4 or P2_LEVEL in (35, 36, 41, 42, 43, 44):
                    if P2_LEVEL == 41: hsel = [0]
                    elif P2_LEVEL == 42: hsel = [1]
                    elif P2_LEVEL == 43: hsel = [0, 1]
                    elif P2_LEVEL == 44: hsel = [0, 2, 4, 6]
                    else: hsel = list(range(8))
                    for half, pXX in ((0, pa), (1, pb)):
                        p.wait("tensor", c_vs if half == 0 else c_gs)
                        for hl in hsel:
                            h = half * 8 + hl
                            kz = kzA if h % 2 == 0 else kzB
                            qz = qzA if h % 2 == 0 else qzB
                            cc2 = h // 2
                            fn = lambda e, kz=kz, qz=qz, cc2=cc2, hl=hl, pXX=pXX: e.matmul(
                                out=pXX[:, hl * 128:(hl + 1) * 128],
                                lhsT=kz[:, cc2 * 128:(cc2 + 1) * 128],
                                rhs=qz[:, cc2 * 128:(cc2 + 1) * 128],
                                start=True, stop=True)
                            c_eh = p.op("tensor", fn, inc="te" if hl == hsel[-1] else None)
                        p.wait("vector", [c_eh, t_tri])
                        if P2_LEVEL in (35, 41, 42, 43, 44):
                            c_em = p.op("vector", lambda e, pXX=pXX: e.tensor_copy(out=em[:], in_=pXX[:]), inc="dv")
                        elif P2_LEVEL == 36:
                            c_em = p.op("vector", lambda e, pXX=pXX: e.tensor_tensor(
                                out=em[:].rearrange("q (hh j) -> q hh j", j=128), in0=pXX[:].rearrange("q (hh j) -> q hh j", j=128), in1=tri8, op=ALU.mult), inc="dv")
                        else:
                            c_em = p.op("vector", lambda e, pXX=pXX: e.tensor_tensor(
                                out=em[:].rearrange("q (hh j) -> q hh j", j=128), in0=pXX[:].rearrange("q (hh j) -> q hh j", j=128), in1=tri8, op=ALU.mult), inc="dv")
                        if half == 0:
                            pa_rel = c_em
                        else:
                            pb_rel = c_em
                    pt2_rel = c_kf
                    c_qro_prev = c_qro
                    c_kro_prev = c_kro
                    continue
                c_omm = None
                for half, pXX in ((0, pa), (1, pb)):
                    p.wait("vector", [c_e[half], t_tri, c_omm])
                    c_em = p.op("vector", lambda e, pXX=pXX: e.tensor_tensor(
                        out=em[:].rearrange("q (hh j) -> q hh j", j=128), in0=pXX[:].rearrange("q (hh j) -> q hh j", j=128), in1=tri8, op=ALU.mult), inc="dv")
                    if half == 0:
                        pa_rel = c_em
                    else:
                        pb_rel = c_em
                    p.wait("tensor", [c_em, po_rel, s_evac_prev])
                    for hl in range(8):
                        h = half * 8 + hl
                        bp = 64 * (h % 2)
                        cc2 = h // 2
                        fn1 = lambda e, h=h, hl=hl: e.matmul(
                            out=po[:, h * 64:(h + 1) * 64],
                            lhsT=em[:, hl * 128:(hl + 1) * 128],
                            rhs=vs[:, h * 64:(h + 1) * 64],
                            start=True, stop=False)
                        p.op("tensor", fn1)
                        qz = qzA if h % 2 == 0 else qzB
                        fn2 = lambda e, h=h, qz=qz, cc2=cc2: e.matmul(
                            out=po[:, h * 64:(h + 1) * 64],
                            lhsT=qz[:, cc2 * 128:(cc2 + 1) * 128],
                            rhs=S_sb[:, cc2 * 64:(cc2 + 1) * 64],
                            start=False, stop=True)
                        c_omm = p.op("tensor", fn2, inc="te" if hl == 7 else None)

                for h in range(H):
                    bp = 64 * (h % 2)
                    fn = lambda e, h=h, bp=bp, n=n: e.matmul(
                        out=pss[bp:bp + 64, (h // 2) * 64:(h // 2 + 1) * 64],
                        lhsT=kro[:, h * 64:(h + 1) * 64],
                        rhs=vs[:, h * 64:(h + 1) * 64],
                        start=(n == 0 and h < 2), stop=(n == NT - 1 and h >= H - 2),
                        tile_position=(0, bp), skip_group_check=True)
                    c_smm = p.op("tensor", fn, inc="te" if h == H - 1 else None)
                c_smm_prev = c_smm
                p.wait("scalar", c_smm)
                s_evac_prev = p.op("scalar", lambda e: e.activation(out=S_sb[:], in_=pss[:], func=AFT.Identity), inc="sc")

                p.wait("vector", [c_omm, c_gs, ab_rel[n % 2]])
                po_rel = p.op("vector", lambda e, n=n: e.tensor_tensor(
                    out=ab[n % 2][:], in0=po[:], in1=gs[:], op=ALU.mult), inc="dv")
                p.wait("scalar", po_rel)
                ab_rel[n % 2] = p.dma("gpsimd", attn_s[n * 128:(n + 1) * 128, :], ab[n % 2][:])
                if dbg_attn is not None:
                    ab_rel[n % 2] = p.dma("gpsimd", dbg_attn[:, n * 1024:(n + 1) * 1024], ab[n % 2][:])
                if ws is not None:
                    ws.pump(3)


def phase3(nc, p, ident, attn_s, gamb, betb, ln2_s, ws=None):
    p.barrier()
    with ExitStack() as s3:
        sb3 = lambda n, s, d: s3.enter_context(nc.sbuf_tensor(n, s, d))
        stat = sb3("stat3", [128, 8], F32)
        n1 = sb3("n1", [128, E], BF16)
        sq = sb3("sq3", [128, E], F32)
        ln2b = sb3("ln2b", [128, E], BF16)
        lnst = [sb3(f"lnst{i}", [128, E], BF16) for i in range(2)]
        abuf = [sb3(f"a3_{i}", [128, E], BF16) for i in range(2)]
        with nc.psum_tensor("pt3", [128, 1024], BF16) as pt3:
            ev_rel = None
            n1_rel = None
            a_rel = [None, None]
            ln_rel = [None, None]
            for t in range(NT):
                p.wait("sync", a_rel[t % 2])
                c_a = p.dma("sync", abuf[t % 2][:], attn_s[t * 128:(t + 1) * 128, :])
                a = abuf[t % 2][:]
                p.wait("vector", c_a)
                p.wait("scalar", c_a)
                c = p.op("vector", lambda e, a=a: e.reduce_sum(out=stat[:, 0:1], in_=a, axis=AX.X), inc="dv")
                p.wait("vector", c)
                c_mu = p.op("vector", lambda e: e.tensor_scalar_mul(out=stat[:, 0:1], in0=stat[:, 0:1], scalar1=1.0 / E), inc="dv")
                c_sq = p.op("scalar", lambda e, a=a: e.activation(out=sq[:], in_=a, func=AFT.Square, accum_out=stat[:, 1:2]), inc="sc")
                p.wait("vector", [c_sq, c_mu])
                p.op("vector", lambda e: e.tensor_scalar_mul(out=stat[:, 2:3], in0=stat[:, 1:2], scalar1=1.0 / E))
                c = p.op("vector", lambda e: e.tensor_tensor(out=stat[:, 3:4], in0=stat[:, 0:1], in1=stat[:, 0:1], op=ALU.mult), inc="dv")
                p.wait("vector", c)
                c = p.op("vector", lambda e: e.tensor_tensor(out=stat[:, 2:3], in0=stat[:, 2:3], in1=stat[:, 3:4], op=ALU.subtract), inc="dv")
                p.wait("vector", c)
                c = p.op("vector", lambda e: e.tensor_scalar_add(out=stat[:, 6:7], in0=stat[:, 2:3], scalar1=1e-5), inc="dv")
                p.wait("scalar", c)
                c_sqrt = p.op("scalar", lambda e: e.activation(out=stat[:, 4:5], in_=stat[:, 6:7], func=AFT.Sqrt), inc="sc")
                p.wait("vector", c_sqrt)
                c_rs = p.op("vector", lambda e: e.reciprocal(out=stat[:, 3:4], in_=stat[:, 4:5]), inc="dv")
                p.wait("vector", [c_rs, n1_rel])
                c_n1 = p.op("vector", lambda e, a=a: e.tensor_scalar(
                    out=n1[:], in0=a, scalar1=stat[:, 0:1], scalar2=stat[:, 3:4],
                    op0=ALU.subtract, op1=ALU.mult), inc="dv")
                a_rel[t % 2] = c_n1
                p.wait("vector", c_n1)
                p.op("vector", lambda e: e.tensor_tensor(out=ln2b[:], in0=n1[:], in1=gamb[:], op=ALU.mult))
                c_l2 = p.op("vector", lambda e: e.tensor_tensor(out=ln2b[:], in0=ln2b[:], in1=betb[:], op=ALU.add), inc="dv")
                n1_rel = c_l2
                p.wait("tensor", [c_l2, ev_rel])
                for cch in range(ECH):
                    fn = lambda e, cch=cch: e.transpose(
                        out=pt3[:, cch * 128:(cch + 1) * 128],
                        in_=ln2b[:, cch * 128:(cch + 1) * 128], identity=ident[:])
                    c_tp = p.op("tensor", fn, inc="te" if cch == ECH - 1 else None)
                p.wait("vector", [c_tp, ln_rel[t % 2]])
                ev_rel = p.op("vector", lambda e, t=t: e.tensor_copy(out=lnst[t % 2][:], in_=pt3[:]), inc="dv")
                p.wait("scalar", ev_rel)
                ln_rel[t % 2] = p.dma("gpsimd", ln2_s[:, t * 1024:(t + 1) * 1024], lnst[t % 2][:])
                if ws is not None:
                    ws.pump(1)


def phase4(nc, p, x_d, xq_fm, ln2_s, wg_q, wu_q, wd_q, wo_s, out_d, ident, w_fence):
    p.barrier()
    with ExitStack() as s4:
        sb4 = lambda n, s, d: s4.enter_context(nc.sbuf_tensor(n, s, d))
        ffn_acc = sb4("ffn_acc", [128, 8 * 1024], F32)
        wgs = sb4("wgs", [128, ECH * 1024], BF16)
        wus = sb4("wus", [128, ECH * 1024], BF16)
        wds = sb4("wds", [128, 8 * 1024], BF16)
        wos = wds
        sg = sb4("sg", [128, 1024], BF16)
        ub = sb4("ub", [128, 1024], BF16)
        hb = sb4("hb", [128, 1024], F32)
        hqb = sb4("hqb", [128, 1024], BF16)
        hqf = sb4("hqf", [128, 1024], BF16)
        qb = QuantBufs(sb4, "p4", 1024, fp32=True)
        xrt = sb4("xrt", [128, E], F32)
        outb = sb4("outb", [128, E], F32)
        lnt = [sb4(f"lnt{i}", [128, E], BF16) for i in range(2)]
        with (nc.psum_tensor("pg", [128, 1024], F32) as pg,
              nc.psum_tensor("pu", [128, 1024], F32) as pu,
              nc.psum_tensor("pf", [128, 1024], F32) as pf,
              nc.psum_tensor("pt4", [128, 1024], BF16) as pt4):
            p.wait("sync", w_fence)
            w_rel = None
            c_wo_prev = None
            pt4_rel = None
            hq_rel = None
            pf_rel = None
            x_rel = None
            out_rel = None
            ln_war = [None, None]
            for grp in range(2):
                for ip in range(3):
                    W = WIPW[ip]
                    NIC = WICH[ip]
                    p.wait("sync", [w_rel, c_wo_prev])
                    ldw = []
                    for c in range(ECH):
                        ldw.append(p.dma("sync", wgs[:, c * 1024:c * 1024 + W], wg_q[ip * 1024:ip * 1024 + W, c * 128:(c + 1) * 128], tp=True))
                        ldw.append(p.dma("sync", wus[:, c * 1024:c * 1024 + W], wu_q[ip * 1024:ip * 1024 + W, c * 128:(c + 1) * 128], tp=True))
                    for icl in range(NIC):
                        ic = ip * 8 + icl
                        ldw.append(p.dma("sync", wds[:, icl * 1024:(icl + 1) * 1024], wd_q[:, ic * 128:(ic + 1) * 128], tp=True))
                    p.wait("tensor", ldw)
                    for tl in range(8):
                        t = grp * 8 + tl
                        for c in range(ECH):
                            for hcol in range((W + 511) // 512):
                                lo, hi = hcol * 512, min(W, (hcol + 1) * 512)
                                fn = lambda e, t=t, c=c, lo=lo, hi=hi: e.matmul(
                                    out=pg[:, lo:hi],
                                    lhsT=xq_fm[:, t * 1024 + c * 128:t * 1024 + (c + 1) * 128],
                                    rhs=wgs[:, c * 1024 + lo:c * 1024 + hi],
                                    start=(c == 0), stop=(c == ECH - 1))
                                p.op("tensor", fn)
                                fn = lambda e, t=t, c=c, lo=lo, hi=hi: e.matmul(
                                    out=pu[:, lo:hi],
                                    lhsT=xq_fm[:, t * 1024 + c * 128:t * 1024 + (c + 1) * 128],
                                    rhs=wus[:, c * 1024 + lo:c * 1024 + hi],
                                    start=(c == 0), stop=(c == ECH - 1))
                                c_gu = p.op("tensor", fn, inc="te" if (c == ECH - 1 and hi == W) else None)
                        p.wait("scalar", [c_gu, hq_rel])
                        p.op("scalar", lambda e, W=W: e.activation(out=sg[:, 0:W], in_=pg[:, 0:W], func=AFT.Silu))
                        c_ub = p.op("scalar", lambda e, W=W: e.activation(out=ub[:, 0:W], in_=pu[:, 0:W], func=AFT.Identity), inc="sc")
                        p.wait("vector", c_ub)
                        c_h = p.op("vector", lambda e, W=W: e.tensor_tensor(out=hb[:, 0:W], in0=sg[:, 0:W], in1=ub[:, 0:W], op=ALU.mult), inc="dv")
                        p.wait("scalar", c_h)
                        p.wait("vector", c_h)
                        c_hq = emit_quant(p, qb, hb[:, 0:W], hqb[:, 0:W], W, prev=hq_rel)
                        hq_rel = c_hq
                        p.wait("tensor", [c_hq, pt4_rel])
                        for icl in range(NIC):
                            fn = lambda e, icl=icl: e.transpose(
                                out=pt4[:, icl * 128:(icl + 1) * 128],
                                in_=hqb[:, icl * 128:(icl + 1) * 128], identity=ident[:])
                            c_tp = p.op("tensor", fn, inc="te" if icl == NIC - 1 else None)
                        p.wait("vector", c_tp)
                        c_hqf = p.op("vector", lambda e, NIC=NIC: e.tensor_copy(out=hqf[:, 0:NIC * 128], in_=pt4[:, 0:NIC * 128]), inc="dv")
                        pt4_rel = c_hqf
                        p.wait("tensor", [c_hqf, pf_rel])
                        for icl in range(NIC):
                            for hcol in range(2):
                                fn = lambda e, icl=icl, hcol=hcol, NIC=NIC: e.matmul(
                                    out=pf[:, hcol * 512:(hcol + 1) * 512],
                                    lhsT=hqf[:, icl * 128:(icl + 1) * 128],
                                    rhs=wds[:, icl * 1024 + hcol * 512:icl * 1024 + (hcol + 1) * 512],
                                    start=(icl == 0), stop=(icl == NIC - 1))
                                c_dn = p.op("tensor", fn, inc="te" if (icl == NIC - 1 and hcol == 1) else None)
                        p.wait("vector", c_dn)
                        if ip == 0:
                            pf_rel = p.op("vector", lambda e, tl=tl: e.tensor_copy(out=ffn_acc[:, tl * 1024:(tl + 1) * 1024], in_=pf[:]), inc="dv")
                        else:
                            pf_rel = p.op("vector", lambda e, tl=tl: e.tensor_tensor(out=ffn_acc[:, tl * 1024:(tl + 1) * 1024], in0=ffn_acc[:, tl * 1024:(tl + 1) * 1024], in1=pf[:], op=ALU.add), inc="dv")
                    w_rel = c_dn
                p.wait("sync", w_rel)
                ldo = []
                for c in range(ECH):
                    ldo.append(p.dma("sync", wos[:, c * 1024:(c + 1) * 1024], wo_s[c * 128:(c + 1) * 128, :]))
                p.wait("tensor", ldo)
                for tl in range(8):
                    t = grp * 8 + tl
                    p.wait("sync", [x_rel, ln_war[tl % 2]])
                    c_x = p.dma("sync", xrt[:], x_d[t * 128:(t + 1) * 128, :])
                    c_ln = p.dma("sync", lnt[tl % 2][:], ln2_s[:, t * 1024:(t + 1) * 1024])
                    p.wait("tensor", [pf_rel, c_ln])
                    for c in range(ECH):
                        for hcol in range(2):
                            fn = lambda e, t=t, c=c, hcol=hcol, tl=tl: e.matmul(
                                out=pf[:, hcol * 512:(hcol + 1) * 512],
                                lhsT=lnt[tl % 2][:, c * 128:(c + 1) * 128],
                                rhs=wos[:, c * 1024 + hcol * 512:c * 1024 + (hcol + 1) * 512],
                                start=(c == 0), stop=(c == ECH - 1))
                            c_wo = p.op("tensor", fn, inc="te" if (c == ECH - 1 and hcol == 1) else None)
                    c_wo_prev = c_wo
                    ln_war[tl % 2] = c_wo
                    p.wait("vector", [c_wo, c_x, out_rel])
                    c_o1 = p.op("vector", lambda e, tl=tl: e.tensor_tensor(out=outb[:], in0=pf[:], in1=ffn_acc[:, tl * 1024:(tl + 1) * 1024], op=ALU.add), inc="dv")
                    pf_rel = c_o1
                    p.wait("vector", c_o1)
                    c_o2 = p.op("vector", lambda e: e.tensor_tensor(out=outb[:], in0=outb[:], in1=xrt[:], op=ALU.add), inc="dv")
                    x_rel = c_o2
                    p.wait("scalar", c_o2)
                    out_rel = p.dma("gpsimd", out_d[t * 128:(t + 1) * 128, :], outb[:])


def make_consts():
    ident = np.eye(128, dtype=ml_dtypes.bfloat16)
    tri = np.triu(np.ones((128, 128), dtype=np.float32)).astype(ml_dtypes.bfloat16)
    return ident, tri


_NC_CACHE = {}

def _get_nc():
    if "nc" not in _NC_CACHE:
        _NC_CACHE["nc"] = build_nc(99)
    return _NC_CACHE["nc"]


def kernel(L0_attn_x, L0_attn_Wq, L0_attn_Wk, L0_attn_Wv, L0_attn_Wo,
           L0_attn_Wg1, L0_attn_Wg2, L0_attn_gamma, L0_attn_beta,
           L0_ffn_w_gate, L0_ffn_w_up, L0_ffn_w_down):
    x = np.ascontiguousarray(np.asarray(L0_attn_x, dtype=np.float32))
    ident, tri = make_consts()
    cc, ss = rope_tables()
    gam = np.ascontiguousarray(np.broadcast_to(np.asarray(L0_attn_gamma, np.float32), (128, E)))
    bet = np.ascontiguousarray(np.broadcast_to(np.asarray(L0_attn_beta, np.float32), (128, E)))
    common = {
        "wq": np.ascontiguousarray(np.asarray(L0_attn_Wq, np.float32)),
        "wk": np.ascontiguousarray(np.asarray(L0_attn_Wk, np.float32)),
        "wv": np.ascontiguousarray(np.asarray(L0_attn_Wv, np.float32)),
        "wo": np.ascontiguousarray(np.asarray(L0_attn_Wo, np.float32)),
        "wg1": np.ascontiguousarray(np.asarray(L0_attn_Wg1, np.float32)),
        "wg2": np.ascontiguousarray(np.asarray(L0_attn_Wg2, np.float32)),
        "wgate": np.ascontiguousarray(np.asarray(L0_ffn_w_gate, np.float32)),
        "wup": np.ascontiguousarray(np.asarray(L0_ffn_w_up, np.float32)),
        "wdown": np.ascontiguousarray(np.asarray(L0_ffn_w_down, np.float32)),
        "ident": ident, "triu": tri, "ctab": cc, "stab": ss,
        "gam": gam, "bet": bet,
    }
    nc = _get_nc()
    in_maps = [dict(common, x=np.ascontiguousarray(x[c])) for c in range(x.shape[0])]
    res = run_bass_kernel_spmd(nc, in_maps, core_ids=list(range(len(in_maps))))
    return np.stack([r["out"] for r in res.results], axis=0)


# version 14
# speedup vs baseline: 1.0620x; 1.0390x over previous
import numpy as np
import ml_dtypes
import concourse.bass as bass
import concourse.mybir as mybir
from concourse.bass_utils import run_bass_kernel_spmd
from contextlib import ExitStack

F32 = mybir.dt.float32
BF16 = mybir.dt.bfloat16
I16 = mybir.dt.int16
ALU = mybir.AluOpType
AX = mybir.AxisListType
AFT = mybir.ActivationFunctionType

S, E, IFF = 2048, 1024, 2816
NT = S // 128
ECH = E // 128
H, D = 16, 64
NBE = E // 16
ICH = IFF // 128
WICH = (8, 8, 6)
WIPW = (1024, 1024, 768)

DEBUG_PHASE = 99
P2_LEVEL = 5
P2_CHUNKS = NT
P2_SKIP = set()
N_SQ = 12
N_GQ = 8


class Prog:
    ENGINES = ("sync", "vector", "scalar", "tensor", "gpsimd")

    def __init__(self, nc, st):
        self.nc = nc
        names = ["dv", "sc", "te", "pl"]
        names += [f"sq{i}" for i in range(N_SQ)]
        names += [f"gq{i}" for i in range(N_GQ)]
        self.sems = {n: st.enter_context(nc.semaphore(n)) for n in names}
        self.cnt = {n: 0 for n in names}
        self.q = {e: [] for e in self.ENGINES}
        self.slot_rr = {"sync": 0, "gpsimd": 0}

    def emit(self, eng, fn):
        self.q[eng].append(fn)

    def wait(self, eng, tok):
        if tok is None:
            return
        if isinstance(tok, list):
            best = {}
            for t in tok:
                if t is None:
                    continue
                s, v = t
                best[s] = max(best.get(s, 0), v)
            for s, v in best.items():
                self._wait1(eng, s, v)
            return
        self._wait1(eng, tok[0], tok[1])

    def _wait1(self, eng, sem, val):
        if val <= 0:
            return
        h = self.sems[sem]
        self.emit(eng, lambda e, h=h, v=val: e.wait_ge(h, v))

    def op(self, eng, fn, inc=None, k=1):
        if inc is None:
            self.emit(eng, fn)
            return None
        h = self.sems[inc]
        self.cnt[inc] += k
        tgt = self.cnt[inc]
        self.emit(eng, lambda e, fn=fn, h=h, k=k: fn(e).then_inc(h, k))
        return (inc, tgt)

    def dma(self, eng, out, in_, tp=False):
        pool, npool = ("sq", N_SQ) if eng == "sync" else ("gq", N_GQ)
        i = self.slot_rr[eng] % npool
        self.slot_rr[eng] += 1
        iss = "scalar" if eng == "gpsimd" else eng
        sem = f"{pool}{i}"
        prev = self.cnt[sem]
        if prev > 0:
            self._wait1(iss, sem, prev)
        self.cnt[sem] += 16
        tgt = self.cnt[sem]
        h = self.sems[sem]
        if tp:
            self.emit(iss, lambda e, o=out, i_=in_, h=h: e.dma_start_transpose(out=o, in_=i_).then_inc(h, 16))
        else:
            self.emit(iss, lambda e, o=out, i_=in_, h=h: e.dma_start(out=o, in_=i_).then_inc(h, 16))
        return (sem, tgt)

    def barrier(self):
        snap = dict(self.cnt)
        for eng in self.ENGINES:
            for s, v in snap.items():
                if v > 0:
                    self._wait1(eng, s, v)

    def finalize(self, block):
        for eng in self.ENGINES:
            fns = self.q[eng]
            if not fns:
                continue
            getattr(block, eng)(lambda e, fns=fns: [f(e) for f in fns])


def rope_tables():
    try:
        import jax
        import jax.numpy as jnp
        with jax.default_device(jax.devices("cpu")[0]):
            inv_freq = 10000.0 ** (-(jnp.arange(D // 2, dtype=jnp.float32) * 2.0) / D)
            posj = jnp.arange(S, dtype=jnp.float32) + 0
            angj = posj[:, None] * inv_freq[None, :]
            cos = np.asarray(jnp.cos(angj))
            sin = np.asarray(jnp.sin(angj))
    except Exception:
        inv32 = (10000.0 ** (-(np.arange(D // 2, dtype=np.float64) * 2.0) / D)).astype(np.float32)
        pos = np.arange(S, dtype=np.float32)
        a64 = (pos[:, None] * inv32[None, :]).astype(np.float64)
        cos = np.cos(a64).astype(np.float32)
        sin = np.sin(a64).astype(np.float32)
    cc = np.concatenate([cos, cos], axis=1)
    ss = np.concatenate([-sin, sin], axis=1)
    cc = np.ascontiguousarray(cc.reshape(NT, 128, 64).transpose(1, 0, 2).reshape(128, NT * 64))
    ss = np.ascontiguousarray(ss.reshape(NT, 128, 64).transpose(1, 0, 2).reshape(128, NT * 64))
    return cc, ss


def sl3(ap, u=16):
    return ap.rearrange("q (nb u) -> q nb u", u=u)


def bc3(a, nb, u=16):
    return a.rearrange("q (nb o) -> q nb o", o=1).broadcast_to([128, nb, u])


class QuantBufs:
    def __init__(self, sb, tag, W, fp32=True):
        self.W = W
        nb = W // 16
        self.nb = nb
        f = F32 if fp32 else BF16
        self.uab = sb(f"{tag}_uab", [128, W], f)
        self.sgn = sb(f"{tag}_sgn", [128, W], f)
        self.ua = sb(f"{tag}_ua", [128, W], f)
        self.shs = sb(f"{tag}_shs", [128, W], f)
        self.rr = [sb(f"{tag}_rrA", [128, W], f),
                   sb(f"{tag}_rrB", [128, W], f)]
        self.fa = sb(f"{tag}_fa", [128, W], f)
        self.bmax = sb(f"{tag}_bmax", [128, nb], F32)
        self.bm2 = sb(f"{tag}_bm2", [128, nb], F32)
        self.hs = sb(f"{tag}_hs", [128, nb], F32)
        self.rcp = sb(f"{tag}_rcp", [128, nb], F32)
        self.par = 0
        self.g_sc = None
        self.g_rr = [None, None]


def emit_quant(p, qb, src_ap, out_b, W, prev=None, out_war=None):
    nb = W // 16
    par = qb.par
    qb.par ^= 1
    rr = qb.rr[par]
    p.wait("scalar", qb.g_sc)
    p.op("scalar", lambda e: e.activation(out=qb.uab[:, 0:W], in_=src_ap, func=AFT.Abs))
    csg = p.op("scalar", lambda e: e.activation(out=qb.sgn[:, 0:W], in_=src_ap, func=AFT.Sign), inc="sc")
    p.wait("vector", csg)
    c = p.op("vector", lambda e: e.reduce_max(out=qb.bmax[:, 0:nb], in_=sl3(qb.uab[:, 0:W]), axis=AX.X), inc="dv")
    p.wait("vector", c)
    c = p.op("vector", lambda e: e.tensor_scalar_max(out=qb.bm2[:, 0:nb], in0=qb.bmax[:, 0:nb], scalar1=6e-12), inc="dv")
    p.wait("vector", c)
    p.op("vector", lambda e: e.tensor_scalar_mul(out=qb.hs[:, 0:nb], in0=qb.bm2[:, 0:nb], scalar1=1.0 / 12))
    c = p.op("vector", lambda e: e.reciprocal(out=qb.rcp[:, 0:nb], in_=qb.bm2[:, 0:nb]), inc="dv")
    p.wait("vector", c)
    cn = p.op("vector", lambda e: e.tensor_scalar_mul(out=qb.rcp[:, 0:nb], in0=qb.rcp[:, 0:nb], scalar1=6.0), inc="dv")
    p.wait("vector", cn)
    c_ua = p.op("vector", lambda e: e.tensor_tensor(out=sl3(qb.ua[:, 0:W]), in0=sl3(qb.uab[:, 0:W]), in1=bc3(qb.rcp[:, 0:nb], nb), op=ALU.mult), inc="dv")
    p.op("vector", lambda e: e.tensor_single_scalar(out=qb.fa[:, 0:W], in_=qb.ua[:, 0:W], scalar=0.25, op=ALU.is_gt))
    for thr in (0.75, 1.25, 1.75):
        p.op("vector", lambda e, thr=thr: e.scalar_tensor_tensor(out=qb.fa[:, 0:W], in0=qb.ua[:, 0:W], scalar=thr, in1=qb.fa[:, 0:W], op0=ALU.is_gt, op1=ALU.add))
    p.wait("vector", qb.g_rr[par])
    p.op("vector", lambda e: e.tensor_single_scalar(out=rr[:, 0:W], in_=qb.ua[:, 0:W], scalar=2.5, op=ALU.is_gt))
    for thr in (3.5, 5.0, 5.0):
        p.op("vector", lambda e, thr=thr: e.scalar_tensor_tensor(out=rr[:, 0:W], in0=qb.ua[:, 0:W], scalar=thr, in1=rr[:, 0:W], op0=ALU.is_gt, op1=ALU.add))
    cm = p.op("vector", lambda e: e.scalar_tensor_tensor(out=rr[:, 0:W], in0=rr[:, 0:W], scalar=2.0, in1=qb.fa[:, 0:W], op0=ALU.mult, op1=ALU.add), inc="dv")
    p.wait("gpsimd", [csg, cn])
    cshs = p.op("gpsimd", lambda e: e.tensor_tensor(out=sl3(qb.shs[:, 0:W]), in0=sl3(qb.sgn[:, 0:W]), in1=bc3(qb.hs[:, 0:nb], nb), op=ALU.mult), inc="pl")
    p.wait("gpsimd", [cm, out_war])
    cq = p.op("gpsimd", lambda e: e.tensor_tensor(out=out_b, in0=rr[:, 0:W], in1=qb.shs[:, 0:W], op=ALU.mult), inc="pl")
    qb.g_sc = [c_ua, cshs]
    qb.g_rr[par] = cq
    return cq


def build_nc(debug_phase=DEBUG_PHASE, races=False):
    nc = bass.Bass("TRN2", target_bir_lowering=False, detect_race_conditions=races)
    dt = lambda n, s, d, k: nc.dram_tensor(n, s, d, kind=k).ap()
    x_d = dt("x", [S, E], F32, "ExternalInput")
    wq_d = dt("wq", [E, E], F32, "ExternalInput")
    wk_d = dt("wk", [E, E], F32, "ExternalInput")
    wv_d = dt("wv", [E, E], F32, "ExternalInput")
    wo_d = dt("wo", [E, E], F32, "ExternalInput")
    wg1_d = dt("wg1", [E, D], F32, "ExternalInput")
    wg2_d = dt("wg2", [D, E], F32, "ExternalInput")
    wgt_d = dt("wgate", [IFF, E], F32, "ExternalInput")
    wup_d = dt("wup", [IFF, E], F32, "ExternalInput")
    wdn_d = dt("wdown", [E, IFF], F32, "ExternalInput")
    id_d = dt("ident", [128, 128], BF16, "ExternalInput")
    tri_d = dt("triu", [128, 128], BF16, "ExternalInput")
    ct_d = dt("ctab", [128, NT * 64], F32, "ExternalInput")
    st_d = dt("stab", [128, NT * 64], F32, "ExternalInput")
    gam_d = dt("gam", [128, E], F32, "ExternalInput")
    bet_d = dt("bet", [128, E], F32, "ExternalInput")
    out_d = dt("out", [S, E], F32, "ExternalOutput")
    wg_q = dt("wg_q", [IFF, E], BF16, "Internal")
    wu_q = dt("wu_q", [IFF, E], BF16, "Internal")
    wd_q = dt("wd_q", [E, IFF], BF16, "Internal")
    ln2_s = dt("ln2_s", [128, NT * 1024], BF16, "Internal")
    wo_s = dt("wo_s", [E, E], BF16, "Internal")
    attn_s = dt("attn_s", [S, E], BF16, "Internal")
    dbg = {}
    if debug_phase == 2:
        dbg["attn"] = dt("dbg_attn", [128, NT * E], BF16, "ExternalOutput")
    if debug_phase == 3:
        dbg["ln2"] = dt("dbg_ln2", [128, ECH * S], BF16, "ExternalOutput")

    with ExitStack() as st:
        sb = lambda n, s, d: st.enter_context(nc.sbuf_tensor(n, s, d))
        p = Prog(nc, st)

        ident = sb("identt", [128, 128], BF16)
        triu = sb("triut", [128, 128], BF16)
        ccb = sb("ccb", [128, NT * 64], BF16)
        ssb = sb("ssb", [128, NT * 64], BF16)
        gamb = sb("gamb", [128, E], BF16)
        betb = sb("betb", [128, E], BF16)
        xq_fm = sb("xq_fm", [128, ECH * S], BF16)

        with nc.sbuf_tensor("cload", [128, NT * 64], F32) as cload:
            t_id = p.dma("sync", ident[:], id_d[:, :])
            t_tri = p.dma("sync", triu[:], tri_d[:, :])
            cv = None
            for buf, d, w in ((gamb, gam_d, E), (betb, bet_d, E),
                              (ccb, ct_d, NT * 64), (ssb, st_d, NT * 64)):
                p.wait("sync", cv)
                tok = p.dma("sync", cload[:, 0:w], d[:, :])
                p.wait("vector", tok)
                cv = p.op("vector", lambda e, buf=buf, w=w: e.tensor_copy(out=buf[:], in_=cload[:, 0:w]), inc="dv")
            c_consts = cv
            p.wait("sync", c_consts)

        ws_stack = ExitStack()
        ws = WSteps(nc, p, ws_stack, wgt_d, wup_d, wdn_d, wo_d, wg_q, wu_q, wd_q, wo_s)
        with nc.sbuf_tensor("xn_fm", [128, ECH * S], BF16) as xn_fm:
            phase1(nc, p, x_d, ident, xn_fm, xq_fm, t_id, ws=ws)
            phase2(nc, p, wq_d, wk_d, wv_d, wg1_d, wg2_d,
                   ident, triu, ccb, ssb, xn_fm, attn_s, c_consts, t_tri,
                   dbg.get("attn"), ws=ws)
        phase3(nc, p, ident, attn_s, gamb, betb, ln2_s, ws=ws)
        ws.drain()
        ws_stack.close()
        if debug_phase == 3:
            p.barrier()
            p.dma("gpsimd", dbg["ln2"][:, :], ln2_s[:, :])
        if debug_phase >= 4:
            phase4(nc, p, x_d, xq_fm, ln2_s, wg_q, wu_q, wd_q, wo_s, out_d, ident, ws.stores)

        if debug_phase < 4:
            p.barrier()
            with nc.sbuf_tensor("zz", [128, E], F32) as zz:
                cz = p.op("vector", lambda e: e.memset(zz[:], 0.0), inc="dv")
                p.wait("scalar", cz)
                for t in range(NT):
                    p.dma("gpsimd", out_d[t * 128:(t + 1) * 128, :], zz[:])
        p.barrier()

        with nc.Block() as block:
            p.finalize(block)
    return nc


def phase1(nc, p, x_d, ident, xn_fm, xq_fm, t_id, ws=None):
    p.barrier()
    with ExitStack() as s1:
        sb1 = lambda n, s, d: s1.enter_context(nc.sbuf_tensor(n, s, d))
        xr = [sb1(f"xr{i}", [128, E], F32) for i in range(2)]
        stat = sb1("stat", [128, 8], F32)
        xnf = sb1("xnf", [128, E], F32)
        xnb2 = [sb1(f"xnb{i}", [128, E], BF16) for i in range(2)]
        xqb2 = [sb1(f"xqb{i}", [128, E], BF16) for i in range(2)]
        qb = QuantBufs(sb1, "p1", E)
        with nc.psum_tensor("pt1", [128, 1024], BF16) as pt1:
            xr_rel = [None, None]
            tp_n = [None, None]
            tp_q = [None, None]
            evq_rel = None
            pend = {"v": None}

            def flush_p1():
                nonlocal evq_rel
                if pend["v"] is None:
                    return
                t, par, c_nb, c_q = pend["v"]
                pend["v"] = None
                p.wait("tensor", [c_nb, evq_rel, t_id])
                for cch in range(ECH):
                    fn = lambda e, cch=cch, par=par: e.transpose(
                        out=pt1[:, cch * 128:(cch + 1) * 128],
                        in_=xnb2[par][:, cch * 128:(cch + 1) * 128], identity=ident[:])
                    c_tp = p.op("tensor", fn, inc="te" if cch == ECH - 1 else None)
                tp_n[par] = c_tp
                p.wait("vector", c_tp)
                c_evn = p.op("vector", lambda e, t=t: e.tensor_copy(
                    out=xn_fm[:, t * 1024:(t + 1) * 1024], in_=pt1[:]), inc="dv")
                p.wait("tensor", [c_evn, c_q])
                for cch in range(ECH):
                    fn = lambda e, cch=cch, par=par: e.transpose(
                        out=pt1[:, cch * 128:(cch + 1) * 128],
                        in_=xqb2[par][:, cch * 128:(cch + 1) * 128], identity=ident[:])
                    c_tp = p.op("tensor", fn, inc="te" if cch == ECH - 1 else None)
                tp_q[par] = c_tp
                p.wait("vector", c_tp)
                evq_rel = p.op("vector", lambda e, t=t: e.tensor_copy(
                    out=xq_fm[:, t * 1024:(t + 1) * 1024], in_=pt1[:]), inc="dv")

            for t in range(NT):
                par = t % 2
                b = xr[t % 2]
                p.wait("sync", xr_rel[t % 2])
                c_x = p.dma("sync", b[:], x_d[t * 128:(t + 1) * 128, :])
                p.wait("vector", c_x)
                c = p.op("vector", lambda e, b=b: e.reduce_sum(out=stat[:, 0:1], in_=b[:], axis=AX.X), inc="dv")
                p.wait("vector", c)
                c_mu = p.op("vector", lambda e: e.tensor_scalar_mul(out=stat[:, 0:1], in0=stat[:, 0:1], scalar1=1.0 / E), inc="dv")
                p.wait("scalar", c_x)
                p.wait("scalar", qb.g_sc)
                c_sq = p.op("scalar", lambda e, b=b: e.activation(out=qb.uab[:], in_=b[:], func=AFT.Square, accum_out=stat[:, 1:2]), inc="sc")
                p.wait("vector", [c_sq, c_mu])
                p.op("vector", lambda e: e.tensor_scalar_mul(out=stat[:, 2:3], in0=stat[:, 1:2], scalar1=1.0 / E))
                c = p.op("vector", lambda e: e.tensor_tensor(out=stat[:, 3:4], in0=stat[:, 0:1], in1=stat[:, 0:1], op=ALU.mult), inc="dv")
                p.wait("vector", c)
                c = p.op("vector", lambda e: e.tensor_tensor(out=stat[:, 2:3], in0=stat[:, 2:3], in1=stat[:, 3:4], op=ALU.subtract), inc="dv")
                p.wait("vector", c)
                c = p.op("vector", lambda e: e.tensor_scalar_add(out=stat[:, 6:7], in0=stat[:, 2:3], scalar1=1e-5), inc="dv")
                p.wait("scalar", c)
                c_sqrt = p.op("scalar", lambda e: e.activation(out=stat[:, 4:5], in_=stat[:, 6:7], func=AFT.Sqrt), inc="sc")
                p.wait("vector", c_sqrt)
                c_rs = p.op("vector", lambda e: e.reciprocal(out=stat[:, 3:4], in_=stat[:, 4:5]), inc="dv")
                p.wait("vector", c_rs)
                c_xn = p.op("vector", lambda e, b=b: e.tensor_scalar(
                    out=xnf[:], in0=b[:], scalar1=stat[:, 0:1], scalar2=stat[:, 3:4],
                    op0=ALU.subtract, op1=ALU.mult), inc="dv")
                xr_rel[t % 2] = c_xn
                p.wait("scalar", [c_xn, tp_n[par]])
                c_nb = p.op("scalar", lambda e, par=par: e.activation(out=xnb2[par][:], in_=xnf[:], func=AFT.Identity), inc="sc")
                c_q = emit_quant(p, qb, xnf[:], xqb2[par][:, 0:E], E, out_war=tp_q[par])
                prev = pend["v"]
                pend["v"] = None
                if prev is not None:
                    pend["v"], prev2 = prev, None
                    flush_p1()
                pend["v"] = (t, par, c_nb, c_q)
                if ws is not None:
                    ws.pump(1)
            flush_p1()


class WSteps:

    def __init__(self, nc, p, stack, wgt_d, wup_d, wdn_d, wo_d, wg_q, wu_q, wd_q, wo_s):
        sbw = lambda n, s, d: stack.enter_context(nc.sbuf_tensor(n, s, d))
        self.p = p
        self.raw = [sbw(f"wraw{i}", [128, 1024], F32) for i in range(2)]
        self.wqb = [sbw(f"wqbo{i}", [128, 1024], BF16) for i in range(2)]
        self.qb = QuantBufs(sbw, "pw", 1024, fp32=True)
        self.raw_rel = [None, None]
        self.store_war = [None, None]
        self.stores = []
        self.steps = []
        self.k = 0
        self.i = 0

        def quant_step(w_d, dst, r0, c0, w):
            def do():
                p = self.p
                li = self.k
                self.k += 1
                b = self.raw[li % 2]
                ob = self.wqb[li % 2]
                p.wait("sync", self.raw_rel[li % 2])
                c_w = p.dma("sync", b[:, 0:w], w_d[r0:r0 + 128, c0:c0 + w])
                p.wait("scalar", c_w)
                cq = emit_quant(p, self.qb, b[:, 0:w], ob[:, 0:w], w, out_war=self.store_war[li % 2])
                self.raw_rel[li % 2] = cq
                p.wait("scalar", cq)
                tok = p.dma("gpsimd", dst[r0:r0 + 128, c0:c0 + w], ob[:, 0:w])
                self.store_war[li % 2] = tok
                self.stores.append(tok)
            return do

        def wo_step(r):
            def do():
                p = self.p
                li = self.k
                self.k += 1
                b = self.raw[li % 2]
                ob = self.wqb[li % 2]
                p.wait("sync", self.raw_rel[li % 2])
                c_w = p.dma("sync", b[:, 0:1024], wo_d[r * 128:(r + 1) * 128, :])
                p.wait("scalar", [c_w, self.store_war[li % 2]])
                cc = p.op("scalar", lambda e, b=b, ob=ob: e.activation(out=ob[:, 0:1024], in_=b[:, 0:1024], func=AFT.Identity), inc="sc")
                self.raw_rel[li % 2] = cc
                p.wait("scalar", cc)
                tok = p.dma("gpsimd", wo_s[r * 128:(r + 1) * 128, :], ob[:, 0:1024])
                self.store_war[li % 2] = tok
                self.stores.append(tok)
            return do

        for w_d, dst in ((wgt_d, wg_q), (wup_d, wu_q)):
            for r in range(ICH):
                self.steps.append(quant_step(w_d, dst, r * 128, 0, E))
        for r in range(ECH):
            for c0, w in ((0, 1024), (1024, 1024), (2048, 768)):
                self.steps.append(quant_step(wdn_d, wd_q, r * 128, c0, w))
        for r in range(ECH):
            self.steps.append(wo_step(r))

    def pump(self, k):
        while k > 0 and self.i < len(self.steps):
            self.steps[self.i]()
            self.i += 1
            k -= 1

    def drain(self):
        self.pump(len(self.steps))


def phase2(nc, p, wq_d, wk_d, wv_d, wg1_d, wg2_d, ident, triu, ccb, ssb, xn_fm, attn_s, c_consts, t_tri, dbg_attn=None, ws=None):
    p.barrier()
    with ExitStack() as s2:
        sb2 = lambda n, s, d: s2.enter_context(nc.sbuf_tensor(n, s, d))
        wq_r = sb2("wq_r", [128, ECH * 1024], BF16)
        wk_r = sb2("wk_r", [128, ECH * 1024], BF16)
        wv_r = sb2("wv_r", [128, ECH * 1024], BF16)
        wg1_r = sb2("wg1_r", [128, ECH * 64], BF16)
        wg2_r = sb2("wg2_r", [64, 1024], BF16)
        gt_b = sb2("gt_b", [64, S], BF16)
        qs = sb2("qs", [128, 1024], BF16)
        ks = sb2("ks", [128, 1024], BF16)
        vs = sb2("vs", [128, 1024], BF16)
        gs = sb2("gs", [128, 1024], BF16)
        qro = sb2("qro", [128, 1024], BF16)
        kro = sb2("kro", [128, 1024], BF16)
        qzA = sb2("qzA", [128, 1024], BF16)
        qzB = sb2("qzB", [128, 1024], BF16)
        kzA = sb2("kzA", [128, 1024], BF16)
        kzB = sb2("kzB", [128, 1024], BF16)
        em = sb2("em", [128, 1024], BF16)
        S_sb = sb2("S_sb", [128, 512], BF16)
        ab = [sb2(f"ab{i}", [128, 1024], BF16) for i in range(2)]

        raw_scope = ExitStack()
        raw = [raw_scope.enter_context(nc.sbuf_tensor(f"w2raw{i}", [128, 1024], F32)) for i in range(2)]
        raw_rel = [None, None]
        i = 0
        for w_d, w_r in ((wq_d, wq_r), (wk_d, wk_r), (wv_d, wv_r)):
            for c in range(ECH):
                b = raw[i % 2]
                p.wait("sync", raw_rel[i % 2])
                c_w = p.dma("sync", b[:], w_d[c * 128:(c + 1) * 128, :])
                p.wait("vector", c_w)
                c_c = p.op("vector", lambda e, b=b, w_r=w_r, c=c: e.tensor_copy(out=w_r[:, c * 1024:(c + 1) * 1024], in_=b[:]), inc="dv")
                raw_rel[i % 2] = c_c
                i += 1
        for c in range(ECH):
            b = raw[i % 2]
            p.wait("sync", raw_rel[i % 2])
            c_w = p.dma("sync", b[:, 0:D], wg1_d[c * 128:(c + 1) * 128, :])
            p.wait("vector", c_w)
            c_c = p.op("vector", lambda e, b=b, c=c: e.tensor_copy(out=wg1_r[:, c * 64:(c + 1) * 64], in_=b[:, 0:D]), inc="dv")
            raw_rel[i % 2] = c_c
            i += 1
        b = raw[i % 2]
        p.wait("sync", raw_rel[i % 2])
        c_w = p.dma("sync", b[0:64, 0:E], wg2_d[:, :])
        p.wait("vector", c_w)
        c_wcast = p.op("vector", lambda e, b=b: e.tensor_copy(out=wg2_r[:, :], in_=b[0:64, 0:E]), inc="dv")
        c_szero = p.op("vector", lambda e: e.memset(S_sb[:], 0.0), inc="dv")
        p.op("vector", lambda e: e.memset(qzA[64:128, :], 0.0))
        p.op("vector", lambda e: e.memset(qzB[0:64, :], 0.0))
        p.op("vector", lambda e: e.memset(kzA[64:128, :], 0.0))
        c_zpad = p.op("vector", lambda e: e.memset(kzB[0:64, :], 0.0), inc="dv")
        p.barrier()
        raw_scope.close()
        if ws is not None:
            ws.pump(4)
        if P2_LEVEL <= 1:
            return

        with (nc.psum_tensor("pa", [128, 1024], F32) as pa,
              nc.psum_tensor("pb", [128, 1024], F32) as pb,
              nc.psum_tensor("po", [128, 1024], F32) as po,
              nc.psum_tensor("pss", [128, 512], F32) as pss,
              nc.psum_tensor("pt2", [128, 1024], BF16) as pt2):

            p.wait("tensor", [c_wcast, c_consts])
            c_gt = None
            for th in (range(4) if "gt" not in P2_SKIP else []):
                p.wait("tensor", c_gt)
                for tt in range(4):
                    t = th * 4 + tt
                    for c in range(ECH):
                        fn = lambda e, t=t, c=c, tt=tt: e.matmul(
                            out=pa[0:64, tt * 128:(tt + 1) * 128],
                            lhsT=wg1_r[:, c * 64:(c + 1) * 64],
                            rhs=xn_fm[:, t * 1024 + c * 128:t * 1024 + (c + 1) * 128],
                            start=(c == 0), stop=(c == ECH - 1))
                        c_mm = p.op("tensor", fn, inc="te" if (c == ECH - 1 and tt == 3) else None)
                p.wait("scalar", c_mm)
                c_gt = p.op("scalar", lambda e, th=th: e.activation(out=gt_b[:, th * 512:(th + 1) * 512], in_=pa[0:64, 0:512], func=AFT.Identity), inc="sc")
            p.wait("tensor", c_gt)
            if P2_LEVEL <= 2:
                return

            po_rel = None
            pt2_rel = None
            ab_rel = [None, None]
            pa_rel = None
            pb_rel = None
            c_qro_prev = None
            c_kro_prev = None
            c_smm_prev = None
            s_evac_prev = c_szero
            for n in range(P2_CHUNKS):
                p.wait("tensor", pa_rel)
                for c in (range(ECH) if "qkv" not in P2_SKIP else []):
                    for hcol in range(2):
                        fn = lambda e, n=n, c=c, hcol=hcol: e.matmul(
                            out=pa[:, hcol * 512:(hcol + 1) * 512],
                            lhsT=xn_fm[:, n * 1024 + c * 128:n * 1024 + (c + 1) * 128],
                            rhs=wq_r[:, c * 1024 + hcol * 512:c * 1024 + (hcol + 1) * 512],
                            start=(c == 0), stop=(c == ECH - 1))
                        c_qmm = p.op("tensor", fn, inc="te" if (c == ECH - 1 and hcol == 1) else None)
                if "qkv" not in P2_SKIP:
                    p.wait("scalar", [c_qmm, c_qro_prev])
                    c_qs = p.op("scalar", lambda e: e.activation(out=qs[:], in_=pa[:], func=AFT.Silu), inc="sc")
                else:
                    c_qs = None
                p.wait("tensor", pb_rel)
                for c in (range(ECH) if "qkv" not in P2_SKIP else []):
                    for hcol in range(2):
                        fn = lambda e, n=n, c=c, hcol=hcol: e.matmul(
                            out=pb[:, hcol * 512:(hcol + 1) * 512],
                            lhsT=xn_fm[:, n * 1024 + c * 128:n * 1024 + (c + 1) * 128],
                            rhs=wk_r[:, c * 1024 + hcol * 512:c * 1024 + (hcol + 1) * 512],
                            start=(c == 0), stop=(c == ECH - 1))
                        c_kmm = p.op("tensor", fn, inc="te" if (c == ECH - 1 and hcol == 1) else None)
                if "qkv" not in P2_SKIP:
                    p.wait("scalar", [c_kmm, c_kro_prev])
                    c_ks = p.op("scalar", lambda e: e.activation(out=ks[:], in_=pb[:], func=AFT.Silu), inc="sc")
                else:
                    c_ks = None
                p.wait("tensor", c_qs)
                for c in (range(ECH) if "qkv" not in P2_SKIP else []):
                    for hcol in range(2):
                        fn = lambda e, n=n, c=c, hcol=hcol: e.matmul(
                            out=pa[:, hcol * 512:(hcol + 1) * 512],
                            lhsT=xn_fm[:, n * 1024 + c * 128:n * 1024 + (c + 1) * 128],
                            rhs=wv_r[:, c * 1024 + hcol * 512:c * 1024 + (hcol + 1) * 512],
                            start=(c == 0), stop=(c == ECH - 1))
                        c_vmm = p.op("tensor", fn, inc="te" if (c == ECH - 1 and hcol == 1) else None)
                if "qkv" not in P2_SKIP:
                    p.wait("scalar", [c_vmm, c_smm_prev])
                    c_vs = p.op("scalar", lambda e: e.activation(out=vs[:], in_=pa[:], func=AFT.Identity), inc="sc")
                else:
                    c_vs = None
                p.wait("tensor", c_ks)
                for hcol in (range(2) if "gate" not in P2_SKIP else []):
                    fn = lambda e, n=n, hcol=hcol: e.matmul(
                        out=pb[:, hcol * 512:(hcol + 1) * 512],
                        lhsT=gt_b[:, n * 128:(n + 1) * 128],
                        rhs=wg2_r[:, hcol * 512:(hcol + 1) * 512],
                        start=True, stop=True)
                    c_gmm = p.op("tensor", fn, inc="te" if hcol == 1 else None)
                if "gate" not in P2_SKIP:
                    p.wait("scalar", [c_gmm, po_rel])
                    c_gs = p.op("scalar", lambda e: e.activation(out=gs[:], in_=pb[:], func=AFT.Sigmoid), inc="sc")
                else:
                    c_gs = None

                def rope(src, dst, c_src, n=n):
                    p.wait("vector", c_src)
                    s3 = lambda a: a[:].rearrange("q (h j) -> q h j", j=64)
                    cc_t = ccb[:, n * 64:(n + 1) * 64].rearrange("q (o j) -> q o j", o=1).broadcast_to([128, 16, 64])
                    ssA = ssb[:, n * 64:n * 64 + 32].rearrange("q (o j) -> q o j", o=1).broadcast_to([128, 16, 32]).rearrange("q h (o j) -> q h o j", o=1)
                    ssB = ssb[:, n * 64 + 32:n * 64 + 64].rearrange("q (o j) -> q o j", o=1).broadcast_to([128, 16, 32]).rearrange("q h (o j) -> q h o j", o=1)
                    src4 = src[:].rearrange("q (h k j) -> q h k j", k=2, j=32)
                    tmp4 = em[:, 0:1024].rearrange("q (h k j) -> q h k j", k=2, j=32)
                    dst4 = dst[:].rearrange("q (h k j) -> q h k j", k=2, j=32)
                    p.op("vector", lambda e: e.tensor_tensor(out=s3(em[:, 0:1024]), in0=s3(src), in1=cc_t, op=ALU.mult))
                    p.op("vector", lambda e: e.tensor_tensor(out=dst4[:, :, 0:1, :], in0=src4[:, :, 1:2, :], in1=ssA, op=ALU.mult))
                    p.op("vector", lambda e: e.tensor_tensor(out=dst4[:, :, 0:1, :], in0=dst4[:, :, 0:1, :], in1=tmp4[:, :, 0:1, :], op=ALU.add))
                    p.op("vector", lambda e: e.tensor_tensor(out=dst4[:, :, 1:2, :], in0=src4[:, :, 0:1, :], in1=ssB, op=ALU.mult))
                    return p.op("vector", lambda e: e.tensor_tensor(out=dst4[:, :, 1:2, :], in0=dst4[:, :, 1:2, :], in1=tmp4[:, :, 1:2, :], op=ALU.add), inc="dv")
                if "rope" not in P2_SKIP:
                    c_qro = rope(qs, qro, c_qs)
                    c_qro_prev = c_qro
                    p.wait("vector", c_smm_prev)
                    c_kro = rope(ks, kro, c_ks)
                    c_kro_prev = c_kro
                else:
                    c_qro = c_kro = None

                p.wait("tensor", [c_qro, pt2_rel])
                for cch in (range(ECH) if "tp" not in P2_SKIP else []):
                    fn = lambda e, cch=cch: e.transpose(
                        out=pt2[:, cch * 128:(cch + 1) * 128],
                        in_=qro[:, cch * 128:(cch + 1) * 128], identity=ident[:])
                    c_tp = p.op("tensor", fn, inc="te" if cch == ECH - 1 else None)
                if "tp" not in P2_SKIP:
                    p.wait("vector", c_tp)
                    p.op("vector", lambda e: e.tensor_copy(out=qzA[0:64, :], in_=pt2[0:64, :]))
                    c_qf = p.op("vector", lambda e: e.tensor_copy(out=qzB[64:128, :], in_=pt2[64:128, :]), inc="dv")
                else:
                    c_qf = None
                p.wait("tensor", [c_kro, c_qf])
                for cch in (range(ECH) if "tp" not in P2_SKIP else []):
                    fn = lambda e, cch=cch: e.transpose(
                        out=pt2[:, cch * 128:(cch + 1) * 128],
                        in_=kro[:, cch * 128:(cch + 1) * 128], identity=ident[:])
                    c_tp = p.op("tensor", fn, inc="te" if cch == ECH - 1 else None)
                if "tp" not in P2_SKIP:
                    p.wait("vector", c_tp)
                    p.op("vector", lambda e: e.tensor_copy(out=kzA[0:64, :], in_=pt2[0:64, :]))
                    c_kf = p.op("vector", lambda e: e.tensor_copy(out=kzB[64:128, :], in_=pt2[64:128, :]), inc="dv")
                    pt2_rel = c_kf
                else:
                    c_kf = None

                if P2_LEVEL <= 3:
                    pt2_rel = c_kf
                    pa_rel = c_vs
                    pb_rel = c_gs
                    c_qro_prev = c_qro
                    c_kro_prev = c_kro
                    continue
                p.wait("tensor", c_kf)
                c_e = [None, None]
                for half, pXX in ((0, pa), (1, pb)):
                    p.wait("tensor", c_vs if half == 0 else c_gs)
                    for hl in range(8):
                        h = half * 8 + hl
                        kz = kzA if h % 2 == 0 else kzB
                        qz = qzA if h % 2 == 0 else qzB
                        cc2 = h // 2
                        fn = lambda e, kz=kz, qz=qz, cc2=cc2, hl=hl, pXX=pXX: e.matmul(
                            out=pXX[:, hl * 128:(hl + 1) * 128],
                            lhsT=kz[:, cc2 * 128:(cc2 + 1) * 128],
                            rhs=qz[:, cc2 * 128:(cc2 + 1) * 128],
                            start=True, stop=True)
                        c_e[half] = p.op("tensor", fn, inc="te" if hl == 7 else None)
                tri8 = triu[:, :].rearrange("q (o j) -> q o j", o=1).broadcast_to([128, 8, 128])
                if P2_LEVEL <= 4 or P2_LEVEL in (35, 36, 41, 42, 43, 44):
                    if P2_LEVEL == 41: hsel = [0]
                    elif P2_LEVEL == 42: hsel = [1]
                    elif P2_LEVEL == 43: hsel = [0, 1]
                    elif P2_LEVEL == 44: hsel = [0, 2, 4, 6]
                    else: hsel = list(range(8))
                    for half, pXX in ((0, pa), (1, pb)):
                        p.wait("tensor", c_vs if half == 0 else c_gs)
                        for hl in hsel:
                            h = half * 8 + hl
                            kz = kzA if h % 2 == 0 else kzB
                            qz = qzA if h % 2 == 0 else qzB
                            cc2 = h // 2
                            fn = lambda e, kz=kz, qz=qz, cc2=cc2, hl=hl, pXX=pXX: e.matmul(
                                out=pXX[:, hl * 128:(hl + 1) * 128],
                                lhsT=kz[:, cc2 * 128:(cc2 + 1) * 128],
                                rhs=qz[:, cc2 * 128:(cc2 + 1) * 128],
                                start=True, stop=True)
                            c_eh = p.op("tensor", fn, inc="te" if hl == hsel[-1] else None)
                        p.wait("vector", [c_eh, t_tri])
                        if P2_LEVEL in (35, 41, 42, 43, 44):
                            c_em = p.op("vector", lambda e, pXX=pXX: e.tensor_copy(out=em[:], in_=pXX[:]), inc="dv")
                        elif P2_LEVEL == 36:
                            c_em = p.op("vector", lambda e, pXX=pXX: e.tensor_tensor(
                                out=em[:].rearrange("q (hh j) -> q hh j", j=128), in0=pXX[:].rearrange("q (hh j) -> q hh j", j=128), in1=tri8, op=ALU.mult), inc="dv")
                        else:
                            c_em = p.op("vector", lambda e, pXX=pXX: e.tensor_tensor(
                                out=em[:].rearrange("q (hh j) -> q hh j", j=128), in0=pXX[:].rearrange("q (hh j) -> q hh j", j=128), in1=tri8, op=ALU.mult), inc="dv")
                        if half == 0:
                            pa_rel = c_em
                        else:
                            pb_rel = c_em
                    pt2_rel = c_kf
                    c_qro_prev = c_qro
                    c_kro_prev = c_kro
                    continue
                c_omm = None
                for half, pXX in ((0, pa), (1, pb)):
                    p.wait("vector", [c_e[half], t_tri, c_omm])
                    c_em = p.op("vector", lambda e, pXX=pXX: e.tensor_tensor(
                        out=em[:].rearrange("q (hh j) -> q hh j", j=128), in0=pXX[:].rearrange("q (hh j) -> q hh j", j=128), in1=tri8, op=ALU.mult), inc="dv")
                    if half == 0:
                        pa_rel = c_em
                    else:
                        pb_rel = c_em
                    p.wait("tensor", [c_em, po_rel, s_evac_prev])
                    for hl in range(8):
                        h = half * 8 + hl
                        bp = 64 * (h % 2)
                        cc2 = h // 2
                        fn1 = lambda e, h=h, hl=hl: e.matmul(
                            out=po[:, h * 64:(h + 1) * 64],
                            lhsT=em[:, hl * 128:(hl + 1) * 128],
                            rhs=vs[:, h * 64:(h + 1) * 64],
                            start=True, stop=False)
                        p.op("tensor", fn1)
                        qz = qzA if h % 2 == 0 else qzB
                        fn2 = lambda e, h=h, qz=qz, cc2=cc2: e.matmul(
                            out=po[:, h * 64:(h + 1) * 64],
                            lhsT=qz[:, cc2 * 128:(cc2 + 1) * 128],
                            rhs=S_sb[:, cc2 * 64:(cc2 + 1) * 64],
                            start=False, stop=True)
                        c_omm = p.op("tensor", fn2, inc="te" if hl == 7 else None)

                for h in range(H):
                    bp = 64 * (h % 2)
                    fn = lambda e, h=h, bp=bp, n=n: e.matmul(
                        out=pss[bp:bp + 64, (h // 2) * 64:(h // 2 + 1) * 64],
                        lhsT=kro[:, h * 64:(h + 1) * 64],
                        rhs=vs[:, h * 64:(h + 1) * 64],
                        start=(n == 0 and h < 2), stop=(n == NT - 1 and h >= H - 2),
                        tile_position=(0, bp), skip_group_check=True)
                    c_smm = p.op("tensor", fn, inc="te" if h == H - 1 else None)
                c_smm_prev = c_smm
                p.wait("scalar", c_smm)
                s_evac_prev = p.op("scalar", lambda e: e.activation(out=S_sb[:], in_=pss[:], func=AFT.Identity), inc="sc")

                p.wait("vector", [c_omm, c_gs, ab_rel[n % 2]])
                po_rel = p.op("vector", lambda e, n=n: e.tensor_tensor(
                    out=ab[n % 2][:], in0=po[:], in1=gs[:], op=ALU.mult), inc="dv")
                p.wait("scalar", po_rel)
                ab_rel[n % 2] = p.dma("gpsimd", attn_s[n * 128:(n + 1) * 128, :], ab[n % 2][:])
                if dbg_attn is not None:
                    ab_rel[n % 2] = p.dma("gpsimd", dbg_attn[:, n * 1024:(n + 1) * 1024], ab[n % 2][:])
                if ws is not None:
                    ws.pump(3)


def phase3(nc, p, ident, attn_s, gamb, betb, ln2_s, ws=None):
    p.barrier()
    with ExitStack() as s3:
        sb3 = lambda n, s, d: s3.enter_context(nc.sbuf_tensor(n, s, d))
        stat = sb3("stat3", [128, 8], F32)
        n1 = sb3("n1", [128, E], BF16)
        sq = sb3("sq3", [128, E], F32)
        ln2b = sb3("ln2b", [128, E], BF16)
        lnst = [sb3(f"lnst{i}", [128, E], BF16) for i in range(2)]
        abuf = [sb3(f"a3_{i}", [128, E], BF16) for i in range(2)]
        with nc.psum_tensor("pt3", [128, 1024], BF16) as pt3:
            ev_rel = None
            n1_rel = None
            a_rel = [None, None]
            ln_rel = [None, None]
            for t in range(NT):
                p.wait("sync", a_rel[t % 2])
                c_a = p.dma("sync", abuf[t % 2][:], attn_s[t * 128:(t + 1) * 128, :])
                a = abuf[t % 2][:]
                p.wait("vector", c_a)
                p.wait("scalar", c_a)
                c = p.op("vector", lambda e, a=a: e.reduce_sum(out=stat[:, 0:1], in_=a, axis=AX.X), inc="dv")
                p.wait("vector", c)
                c_mu = p.op("vector", lambda e: e.tensor_scalar_mul(out=stat[:, 0:1], in0=stat[:, 0:1], scalar1=1.0 / E), inc="dv")
                c_sq = p.op("scalar", lambda e, a=a: e.activation(out=sq[:], in_=a, func=AFT.Square, accum_out=stat[:, 1:2]), inc="sc")
                p.wait("vector", [c_sq, c_mu])
                p.op("vector", lambda e: e.tensor_scalar_mul(out=stat[:, 2:3], in0=stat[:, 1:2], scalar1=1.0 / E))
                c = p.op("vector", lambda e: e.tensor_tensor(out=stat[:, 3:4], in0=stat[:, 0:1], in1=stat[:, 0:1], op=ALU.mult), inc="dv")
                p.wait("vector", c)
                c = p.op("vector", lambda e: e.tensor_tensor(out=stat[:, 2:3], in0=stat[:, 2:3], in1=stat[:, 3:4], op=ALU.subtract), inc="dv")
                p.wait("vector", c)
                c = p.op("vector", lambda e: e.tensor_scalar_add(out=stat[:, 6:7], in0=stat[:, 2:3], scalar1=1e-5), inc="dv")
                p.wait("scalar", c)
                c_sqrt = p.op("scalar", lambda e: e.activation(out=stat[:, 4:5], in_=stat[:, 6:7], func=AFT.Sqrt), inc="sc")
                p.wait("vector", c_sqrt)
                c_rs = p.op("vector", lambda e: e.reciprocal(out=stat[:, 3:4], in_=stat[:, 4:5]), inc="dv")
                p.wait("vector", [c_rs, n1_rel])
                c_n1 = p.op("vector", lambda e, a=a: e.tensor_scalar(
                    out=n1[:], in0=a, scalar1=stat[:, 0:1], scalar2=stat[:, 3:4],
                    op0=ALU.subtract, op1=ALU.mult), inc="dv")
                a_rel[t % 2] = c_n1
                p.wait("vector", c_n1)
                p.op("vector", lambda e: e.tensor_tensor(out=ln2b[:], in0=n1[:], in1=gamb[:], op=ALU.mult))
                c_l2 = p.op("vector", lambda e: e.tensor_tensor(out=ln2b[:], in0=ln2b[:], in1=betb[:], op=ALU.add), inc="dv")
                n1_rel = c_l2
                p.wait("tensor", [c_l2, ev_rel])
                for cch in range(ECH):
                    fn = lambda e, cch=cch: e.transpose(
                        out=pt3[:, cch * 128:(cch + 1) * 128],
                        in_=ln2b[:, cch * 128:(cch + 1) * 128], identity=ident[:])
                    c_tp = p.op("tensor", fn, inc="te" if cch == ECH - 1 else None)
                p.wait("vector", [c_tp, ln_rel[t % 2]])
                ev_rel = p.op("vector", lambda e, t=t: e.tensor_copy(out=lnst[t % 2][:], in_=pt3[:]), inc="dv")
                p.wait("scalar", ev_rel)
                ln_rel[t % 2] = p.dma("gpsimd", ln2_s[:, t * 1024:(t + 1) * 1024], lnst[t % 2][:])
                if ws is not None:
                    ws.pump(1)


def phase4(nc, p, x_d, xq_fm, ln2_s, wg_q, wu_q, wd_q, wo_s, out_d, ident, w_fence):
    p.barrier()
    with ExitStack() as s4:
        sb4 = lambda n, s, d: s4.enter_context(nc.sbuf_tensor(n, s, d))
        ffn_acc = sb4("ffn_acc", [128, 8 * 1024], F32)
        wgs = [sb4(f"wgs{i}", [128, ECH * 1024], BF16) for i in range(2)]
        wus = [sb4(f"wus{i}", [128, ECH * 1024], BF16) for i in range(2)]
        wds = sb4("wds", [128, 8 * 1024], BF16)
        wos = wds
        sg = sb4("sg", [128, 1024], BF16)
        ub = sb4("ub", [128, 1024], BF16)
        hb = sb4("hb", [128, 1024], F32)
        hqb = sb4("hqb", [128, 1024], BF16)
        hqf = sb4("hqf", [128, 1024], BF16)
        qb = QuantBufs(sb4, "p4", 1024, fp32=True)
        xrt = sb4("xrt", [128, E], F32)
        outb = sb4("outb", [128, E], F32)
        lnt = [sb4(f"lnt{i}", [128, E], BF16) for i in range(2)]
        with (nc.psum_tensor("pg", [128, 1024], F32) as pg,
              nc.psum_tensor("pu", [128, 1024], F32) as pu,
              nc.psum_tensor("pf", [128, 1024], F32) as pf,
              nc.psum_tensor("pt4", [128, 1024], BF16) as pt4):
            p.wait("sync", w_fence)
            w_rel = None
            c_wo_prev = None
            pt4_rel = None
            hq_rel = None
            pf_rel = None
            x_rel = None
            out_rel = None
            ln_war = [None, None]
            gu_war = [None, None]

            def issue_gu(k):
                grp_, ip_ = divmod(k, 3)
                W_ = WIPW[ip_]
                par = k % 2
                p.wait("sync", gu_war[par])
                toks = []
                for c in range(ECH):
                    toks.append(p.dma("sync", wgs[par][:, c * 1024:c * 1024 + W_], wg_q[ip_ * 1024:ip_ * 1024 + W_, c * 128:(c + 1) * 128], tp=True))
                    toks.append(p.dma("sync", wus[par][:, c * 1024:c * 1024 + W_], wu_q[ip_ * 1024:ip_ * 1024 + W_, c * 128:(c + 1) * 128], tp=True))
                return toks

            gu_toks = issue_gu(0)
            for grp in range(2):
                for ip in range(3):
                    W = WIPW[ip]
                    NIC = WICH[ip]
                    k = grp * 3 + ip
                    par = k % 2
                    p.wait("sync", [w_rel, c_wo_prev])
                    ldd = []
                    for icl in range(NIC):
                        ic = ip * 8 + icl
                        ldd.append(p.dma("sync", wds[:, icl * 1024:(icl + 1) * 1024], wd_q[:, ic * 128:(ic + 1) * 128], tp=True))
                    nxt = issue_gu(k + 1) if k + 1 < 6 else []
                    p.wait("tensor", gu_toks)
                    gu_toks = nxt
                    for tl in range(8):
                        t = grp * 8 + tl
                        for c in range(ECH):
                            for hcol in range((W + 511) // 512):
                                lo, hi = hcol * 512, min(W, (hcol + 1) * 512)
                                fn = lambda e, t=t, c=c, lo=lo, hi=hi, par=par: e.matmul(
                                    out=pg[:, lo:hi],
                                    lhsT=xq_fm[:, t * 1024 + c * 128:t * 1024 + (c + 1) * 128],
                                    rhs=wgs[par][:, c * 1024 + lo:c * 1024 + hi],
                                    start=(c == 0), stop=(c == ECH - 1))
                                p.op("tensor", fn)
                                fn = lambda e, t=t, c=c, lo=lo, hi=hi, par=par: e.matmul(
                                    out=pu[:, lo:hi],
                                    lhsT=xq_fm[:, t * 1024 + c * 128:t * 1024 + (c + 1) * 128],
                                    rhs=wus[par][:, c * 1024 + lo:c * 1024 + hi],
                                    start=(c == 0), stop=(c == ECH - 1))
                                c_gu = p.op("tensor", fn, inc="te" if (c == ECH - 1 and hi == W) else None)
                        p.wait("scalar", [c_gu, hq_rel])
                        p.op("scalar", lambda e, W=W: e.activation(out=sg[:, 0:W], in_=pg[:, 0:W], func=AFT.Silu))
                        c_ub = p.op("scalar", lambda e, W=W: e.activation(out=ub[:, 0:W], in_=pu[:, 0:W], func=AFT.Identity), inc="sc")
                        p.wait("vector", c_ub)
                        c_h = p.op("vector", lambda e, W=W: e.tensor_tensor(out=hb[:, 0:W], in0=sg[:, 0:W], in1=ub[:, 0:W], op=ALU.mult), inc="dv")
                        p.wait("scalar", c_h)
                        p.wait("vector", c_h)
                        c_hq = emit_quant(p, qb, hb[:, 0:W], hqb[:, 0:W], W, prev=hq_rel)
                        hq_rel = c_hq
                        p.wait("tensor", [c_hq, pt4_rel])
                        for icl in range(NIC):
                            fn = lambda e, icl=icl: e.transpose(
                                out=pt4[:, icl * 128:(icl + 1) * 128],
                                in_=hqb[:, icl * 128:(icl + 1) * 128], identity=ident[:])
                            c_tp = p.op("tensor", fn, inc="te" if icl == NIC - 1 else None)
                        p.wait("vector", c_tp)
                        c_hqf = p.op("vector", lambda e, NIC=NIC: e.tensor_copy(out=hqf[:, 0:NIC * 128], in_=pt4[:, 0:NIC * 128]), inc="dv")
                        pt4_rel = c_hqf
                        p.wait("tensor", [c_hqf, pf_rel] + (ldd if tl == 0 else []))
                        for icl in range(NIC):
                            for hcol in range(2):
                                fn = lambda e, icl=icl, hcol=hcol, NIC=NIC: e.matmul(
                                    out=pf[:, hcol * 512:(hcol + 1) * 512],
                                    lhsT=hqf[:, icl * 128:(icl + 1) * 128],
                                    rhs=wds[:, icl * 1024 + hcol * 512:icl * 1024 + (hcol + 1) * 512],
                                    start=(icl == 0), stop=(icl == NIC - 1))
                                c_dn = p.op("tensor", fn, inc="te" if (icl == NIC - 1 and hcol == 1) else None)
                        p.wait("vector", c_dn)
                        if ip == 0:
                            pf_rel = p.op("vector", lambda e, tl=tl: e.tensor_copy(out=ffn_acc[:, tl * 1024:(tl + 1) * 1024], in_=pf[:]), inc="dv")
                        else:
                            pf_rel = p.op("vector", lambda e, tl=tl: e.tensor_tensor(out=ffn_acc[:, tl * 1024:(tl + 1) * 1024], in0=ffn_acc[:, tl * 1024:(tl + 1) * 1024], in1=pf[:], op=ALU.add), inc="dv")
                    w_rel = c_dn
                    gu_war[par] = c_gu
                p.wait("sync", w_rel)
                ldo = []
                for c in range(ECH):
                    ldo.append(p.dma("sync", wos[:, c * 1024:(c + 1) * 1024], wo_s[c * 128:(c + 1) * 128, :]))
                p.wait("tensor", ldo)
                for tl in range(8):
                    t = grp * 8 + tl
                    p.wait("sync", [x_rel, ln_war[tl % 2]])
                    c_x = p.dma("sync", xrt[:], x_d[t * 128:(t + 1) * 128, :])
                    c_ln = p.dma("sync", lnt[tl % 2][:], ln2_s[:, t * 1024:(t + 1) * 1024])
                    p.wait("tensor", [pf_rel, c_ln])
                    for c in range(ECH):
                        for hcol in range(2):
                            fn = lambda e, t=t, c=c, hcol=hcol, tl=tl: e.matmul(
                                out=pf[:, hcol * 512:(hcol + 1) * 512],
                                lhsT=lnt[tl % 2][:, c * 128:(c + 1) * 128],
                                rhs=wos[:, c * 1024 + hcol * 512:c * 1024 + (hcol + 1) * 512],
                                start=(c == 0), stop=(c == ECH - 1))
                            c_wo = p.op("tensor", fn, inc="te" if (c == ECH - 1 and hcol == 1) else None)
                    c_wo_prev = c_wo
                    ln_war[tl % 2] = c_wo
                    p.wait("vector", [c_wo, c_x, out_rel])
                    c_o1 = p.op("vector", lambda e, tl=tl: e.tensor_tensor(out=outb[:], in0=pf[:], in1=ffn_acc[:, tl * 1024:(tl + 1) * 1024], op=ALU.add), inc="dv")
                    pf_rel = c_o1
                    p.wait("vector", c_o1)
                    c_o2 = p.op("vector", lambda e: e.tensor_tensor(out=outb[:], in0=outb[:], in1=xrt[:], op=ALU.add), inc="dv")
                    x_rel = c_o2
                    p.wait("scalar", c_o2)
                    out_rel = p.dma("gpsimd", out_d[t * 128:(t + 1) * 128, :], outb[:])


def make_consts():
    ident = np.eye(128, dtype=ml_dtypes.bfloat16)
    tri = np.triu(np.ones((128, 128), dtype=np.float32)).astype(ml_dtypes.bfloat16)
    return ident, tri


_NC_CACHE = {}

def _get_nc():
    if "nc" not in _NC_CACHE:
        _NC_CACHE["nc"] = build_nc(99)
    return _NC_CACHE["nc"]


def kernel(L0_attn_x, L0_attn_Wq, L0_attn_Wk, L0_attn_Wv, L0_attn_Wo,
           L0_attn_Wg1, L0_attn_Wg2, L0_attn_gamma, L0_attn_beta,
           L0_ffn_w_gate, L0_ffn_w_up, L0_ffn_w_down):
    x = np.ascontiguousarray(np.asarray(L0_attn_x, dtype=np.float32))
    ident, tri = make_consts()
    cc, ss = rope_tables()
    gam = np.ascontiguousarray(np.broadcast_to(np.asarray(L0_attn_gamma, np.float32), (128, E)))
    bet = np.ascontiguousarray(np.broadcast_to(np.asarray(L0_attn_beta, np.float32), (128, E)))
    common = {
        "wq": np.ascontiguousarray(np.asarray(L0_attn_Wq, np.float32)),
        "wk": np.ascontiguousarray(np.asarray(L0_attn_Wk, np.float32)),
        "wv": np.ascontiguousarray(np.asarray(L0_attn_Wv, np.float32)),
        "wo": np.ascontiguousarray(np.asarray(L0_attn_Wo, np.float32)),
        "wg1": np.ascontiguousarray(np.asarray(L0_attn_Wg1, np.float32)),
        "wg2": np.ascontiguousarray(np.asarray(L0_attn_Wg2, np.float32)),
        "wgate": np.ascontiguousarray(np.asarray(L0_ffn_w_gate, np.float32)),
        "wup": np.ascontiguousarray(np.asarray(L0_ffn_w_up, np.float32)),
        "wdown": np.ascontiguousarray(np.asarray(L0_ffn_w_down, np.float32)),
        "ident": ident, "triu": tri, "ctab": cc, "stab": ss,
        "gam": gam, "bet": bet,
    }
    nc = _get_nc()
    in_maps = [dict(common, x=np.ascontiguousarray(x[c])) for c in range(x.shape[0])]
    res = run_bass_kernel_spmd(nc, in_maps, core_ids=list(range(len(in_maps))))
    return np.stack([r["out"] for r in res.results], axis=0)


# version 15
# speedup vs baseline: 1.1176x; 1.0524x over previous
import numpy as np
import ml_dtypes
import concourse.bass as bass
import concourse.mybir as mybir
from concourse.bass_utils import run_bass_kernel_spmd
from contextlib import ExitStack

F32 = mybir.dt.float32
BF16 = mybir.dt.bfloat16
I16 = mybir.dt.int16
ALU = mybir.AluOpType
AX = mybir.AxisListType
AFT = mybir.ActivationFunctionType

S, E, IFF = 2048, 1024, 2816
NT = S // 128
ECH = E // 128
H, D = 16, 64
NBE = E // 16
ICH = IFF // 128
WICH = (8, 8, 6)
WIPW = (1024, 1024, 768)

DEBUG_PHASE = 99
P2_LEVEL = 5
P2_CHUNKS = NT
P2_SKIP = set()
N_SQ = 12
N_GQ = 8


class Prog:
    ENGINES = ("sync", "vector", "scalar", "tensor", "gpsimd")

    def __init__(self, nc, st):
        self.nc = nc
        names = ["dv", "sc", "te", "pl"]
        names += [f"sq{i}" for i in range(N_SQ)]
        names += [f"gq{i}" for i in range(N_GQ)]
        self.sems = {n: st.enter_context(nc.semaphore(n)) for n in names}
        self.cnt = {n: 0 for n in names}
        self.q = {e: [] for e in self.ENGINES}
        self.slot_rr = {"sync": 0, "gpsimd": 0}

    def emit(self, eng, fn):
        self.q[eng].append(fn)

    def wait(self, eng, tok):
        if tok is None:
            return
        if isinstance(tok, list):
            best = {}
            for t in tok:
                if t is None:
                    continue
                s, v = t
                best[s] = max(best.get(s, 0), v)
            for s, v in best.items():
                self._wait1(eng, s, v)
            return
        self._wait1(eng, tok[0], tok[1])

    def _wait1(self, eng, sem, val):
        if val <= 0:
            return
        h = self.sems[sem]
        self.emit(eng, lambda e, h=h, v=val: e.wait_ge(h, v))

    def op(self, eng, fn, inc=None, k=1):
        if inc is None:
            self.emit(eng, fn)
            return None
        h = self.sems[inc]
        self.cnt[inc] += k
        tgt = self.cnt[inc]
        self.emit(eng, lambda e, fn=fn, h=h, k=k: fn(e).then_inc(h, k))
        return (inc, tgt)

    def dma(self, eng, out, in_, tp=False):
        pool, npool = ("sq", N_SQ) if eng == "sync" else ("gq", N_GQ)
        i = self.slot_rr[eng] % npool
        self.slot_rr[eng] += 1
        iss = "scalar" if eng == "gpsimd" else eng
        sem = f"{pool}{i}"
        prev = self.cnt[sem]
        if prev > 0:
            self._wait1(iss, sem, prev)
        self.cnt[sem] += 16
        tgt = self.cnt[sem]
        h = self.sems[sem]
        if tp:
            self.emit(iss, lambda e, o=out, i_=in_, h=h: e.dma_start_transpose(out=o, in_=i_).then_inc(h, 16))
        else:
            self.emit(iss, lambda e, o=out, i_=in_, h=h: e.dma_start(out=o, in_=i_).then_inc(h, 16))
        return (sem, tgt)

    def barrier(self):
        snap = dict(self.cnt)
        for eng in self.ENGINES:
            for s, v in snap.items():
                if v > 0:
                    self._wait1(eng, s, v)

    def finalize(self, block):
        for eng in self.ENGINES:
            fns = self.q[eng]
            if not fns:
                continue
            getattr(block, eng)(lambda e, fns=fns: [f(e) for f in fns])


def rope_tables():
    try:
        import jax
        import jax.numpy as jnp
        with jax.default_device(jax.devices("cpu")[0]):
            inv_freq = 10000.0 ** (-(jnp.arange(D // 2, dtype=jnp.float32) * 2.0) / D)
            posj = jnp.arange(S, dtype=jnp.float32) + 0
            angj = posj[:, None] * inv_freq[None, :]
            cos = np.asarray(jnp.cos(angj))
            sin = np.asarray(jnp.sin(angj))
    except Exception:
        inv32 = (10000.0 ** (-(np.arange(D // 2, dtype=np.float64) * 2.0) / D)).astype(np.float32)
        pos = np.arange(S, dtype=np.float32)
        a64 = (pos[:, None] * inv32[None, :]).astype(np.float64)
        cos = np.cos(a64).astype(np.float32)
        sin = np.sin(a64).astype(np.float32)
    cc = np.concatenate([cos, cos], axis=1)
    ss = np.concatenate([-sin, sin], axis=1)
    cc = np.ascontiguousarray(cc.reshape(NT, 128, 64).transpose(1, 0, 2).reshape(128, NT * 64))
    ss = np.ascontiguousarray(ss.reshape(NT, 128, 64).transpose(1, 0, 2).reshape(128, NT * 64))
    return cc, ss


def sl3(ap, u=16):
    return ap.rearrange("q (nb u) -> q nb u", u=u)


def bc3(a, nb, u=16):
    return a.rearrange("q (nb o) -> q nb o", o=1).broadcast_to([128, nb, u])


class QuantBufs:
    def __init__(self, sb, tag, W, fp32=True):
        self.W = W
        nb = W // 16
        self.nb = nb
        f = F32 if fp32 else BF16
        self.uab = sb(f"{tag}_uab", [128, W], f)
        self.sgn = sb(f"{tag}_sgn", [128, W], f)
        self.ua = sb(f"{tag}_ua", [128, W], f)
        self.shs = sb(f"{tag}_shs", [128, W], f)
        self.rr = [sb(f"{tag}_rrA", [128, W], f),
                   sb(f"{tag}_rrB", [128, W], f)]
        self.fa = sb(f"{tag}_fa", [128, W], f)
        self.bmax = sb(f"{tag}_bmax", [128, nb], F32)
        self.bm2 = sb(f"{tag}_bm2", [128, nb], F32)
        self.hs = sb(f"{tag}_hs", [128, nb], F32)
        self.rcp = sb(f"{tag}_rcp", [128, nb], F32)
        self.par = 0
        self.g_sc = None
        self.g_rr = [None, None]


def emit_quant(p, qb, src_ap, out_b, W, prev=None, out_war=None):
    nb = W // 16
    par = qb.par
    qb.par ^= 1
    rr = qb.rr[par]
    p.wait("scalar", qb.g_sc)
    p.op("scalar", lambda e: e.activation(out=qb.uab[:, 0:W], in_=src_ap, func=AFT.Abs))
    csg = p.op("scalar", lambda e: e.activation(out=qb.sgn[:, 0:W], in_=src_ap, func=AFT.Sign), inc="sc")
    p.wait("vector", csg)
    c = p.op("vector", lambda e: e.reduce_max(out=qb.bmax[:, 0:nb], in_=sl3(qb.uab[:, 0:W]), axis=AX.X), inc="dv")
    p.wait("vector", c)
    c = p.op("vector", lambda e: e.tensor_scalar_max(out=qb.bm2[:, 0:nb], in0=qb.bmax[:, 0:nb], scalar1=6e-12), inc="dv")
    p.wait("vector", c)
    p.op("vector", lambda e: e.tensor_scalar_mul(out=qb.hs[:, 0:nb], in0=qb.bm2[:, 0:nb], scalar1=1.0 / 12))
    c = p.op("vector", lambda e: e.reciprocal(out=qb.rcp[:, 0:nb], in_=qb.bm2[:, 0:nb]), inc="dv")
    p.wait("vector", c)
    cn = p.op("vector", lambda e: e.tensor_scalar_mul(out=qb.rcp[:, 0:nb], in0=qb.rcp[:, 0:nb], scalar1=6.0), inc="dv")
    p.wait("vector", cn)
    c_ua = p.op("vector", lambda e: e.tensor_tensor(out=sl3(qb.ua[:, 0:W]), in0=sl3(qb.uab[:, 0:W]), in1=bc3(qb.rcp[:, 0:nb], nb), op=ALU.mult), inc="dv")
    p.op("vector", lambda e: e.tensor_single_scalar(out=qb.fa[:, 0:W], in_=qb.ua[:, 0:W], scalar=0.25, op=ALU.is_gt))
    for thr in (0.75, 1.25, 1.75):
        p.op("vector", lambda e, thr=thr: e.scalar_tensor_tensor(out=qb.fa[:, 0:W], in0=qb.ua[:, 0:W], scalar=thr, in1=qb.fa[:, 0:W], op0=ALU.is_gt, op1=ALU.add))
    p.wait("vector", qb.g_rr[par])
    p.op("vector", lambda e: e.tensor_single_scalar(out=rr[:, 0:W], in_=qb.ua[:, 0:W], scalar=2.5, op=ALU.is_gt))
    for thr in (3.5, 5.0, 5.0):
        p.op("vector", lambda e, thr=thr: e.scalar_tensor_tensor(out=rr[:, 0:W], in0=qb.ua[:, 0:W], scalar=thr, in1=rr[:, 0:W], op0=ALU.is_gt, op1=ALU.add))
    cm = p.op("vector", lambda e: e.scalar_tensor_tensor(out=rr[:, 0:W], in0=rr[:, 0:W], scalar=2.0, in1=qb.fa[:, 0:W], op0=ALU.mult, op1=ALU.add), inc="dv")
    p.wait("gpsimd", [csg, cn])
    cshs = p.op("gpsimd", lambda e: e.tensor_tensor(out=sl3(qb.shs[:, 0:W]), in0=sl3(qb.sgn[:, 0:W]), in1=bc3(qb.hs[:, 0:nb], nb), op=ALU.mult), inc="pl")
    p.wait("gpsimd", [cm, out_war])
    cq = p.op("gpsimd", lambda e: e.tensor_tensor(out=out_b, in0=rr[:, 0:W], in1=qb.shs[:, 0:W], op=ALU.mult), inc="pl")
    qb.g_sc = [c_ua, cshs]
    qb.g_rr[par] = cq
    return cq


def build_nc(debug_phase=DEBUG_PHASE, races=False):
    nc = bass.Bass("TRN2", target_bir_lowering=False, detect_race_conditions=races)
    dt = lambda n, s, d, k: nc.dram_tensor(n, s, d, kind=k).ap()
    x_d = dt("x", [S, E], F32, "ExternalInput")
    wq_d = dt("wq", [E, E], F32, "ExternalInput")
    wk_d = dt("wk", [E, E], F32, "ExternalInput")
    wv_d = dt("wv", [E, E], F32, "ExternalInput")
    wo_d = dt("wo", [E, E], F32, "ExternalInput")
    wg1_d = dt("wg1", [E, D], F32, "ExternalInput")
    wg2_d = dt("wg2", [D, E], F32, "ExternalInput")
    wgt_d = dt("wgate", [IFF, E], F32, "ExternalInput")
    wup_d = dt("wup", [IFF, E], F32, "ExternalInput")
    wdn_d = dt("wdown", [E, IFF], F32, "ExternalInput")
    id_d = dt("ident", [128, 128], BF16, "ExternalInput")
    tri_d = dt("triu", [128, 128], BF16, "ExternalInput")
    ct_d = dt("ctab", [128, NT * 64], F32, "ExternalInput")
    st_d = dt("stab", [128, NT * 64], F32, "ExternalInput")
    gam_d = dt("gam", [128, E], F32, "ExternalInput")
    bet_d = dt("bet", [128, E], F32, "ExternalInput")
    out_d = dt("out", [S, E], F32, "ExternalOutput")
    wg_q = dt("wg_q", [IFF, E], BF16, "Internal")
    wu_q = dt("wu_q", [IFF, E], BF16, "Internal")
    wd_q = dt("wd_q", [E, IFF], BF16, "Internal")
    ln2_s = dt("ln2_s", [128, NT * 1024], BF16, "Internal")
    wo_s = dt("wo_s", [E, E], BF16, "Internal")
    attn_s = dt("attn_s", [S, E], BF16, "Internal")
    dbg = {}
    if debug_phase == 2:
        dbg["attn"] = dt("dbg_attn", [128, NT * E], BF16, "ExternalOutput")
    if debug_phase == 3:
        dbg["ln2"] = dt("dbg_ln2", [128, ECH * S], BF16, "ExternalOutput")

    with ExitStack() as st:
        sb = lambda n, s, d: st.enter_context(nc.sbuf_tensor(n, s, d))
        p = Prog(nc, st)

        ident = sb("identt", [128, 128], BF16)
        triu = sb("triut", [128, 128], BF16)
        ccb = sb("ccb", [128, NT * 64], BF16)
        ssb = sb("ssb", [128, NT * 64], BF16)
        gamb = sb("gamb", [128, E], BF16)
        betb = sb("betb", [128, E], BF16)
        xq_fm = sb("xq_fm", [128, ECH * S], BF16)

        with nc.sbuf_tensor("cload", [128, NT * 64], F32) as cload:
            t_id = p.dma("sync", ident[:], id_d[:, :])
            t_tri = p.dma("sync", triu[:], tri_d[:, :])
            cv = None
            for buf, d, w in ((gamb, gam_d, E), (betb, bet_d, E),
                              (ccb, ct_d, NT * 64), (ssb, st_d, NT * 64)):
                p.wait("sync", cv)
                tok = p.dma("sync", cload[:, 0:w], d[:, :])
                p.wait("vector", tok)
                cv = p.op("vector", lambda e, buf=buf, w=w: e.tensor_copy(out=buf[:], in_=cload[:, 0:w]), inc="dv")
            c_consts = cv
            p.wait("sync", c_consts)

        ws_stack = ExitStack()
        ws = WSteps(nc, p, ws_stack, wgt_d, wup_d, wdn_d, wo_d, wg_q, wu_q, wd_q, wo_s)
        with nc.sbuf_tensor("xn_fm", [128, ECH * S], BF16) as xn_fm:
            phase1(nc, p, x_d, ident, xn_fm, xq_fm, t_id, ws=ws)
            phase2(nc, p, wq_d, wk_d, wv_d, wg1_d, wg2_d,
                   ident, triu, ccb, ssb, xn_fm, attn_s, c_consts, t_tri,
                   dbg.get("attn"), ws=ws)
        phase3(nc, p, ident, attn_s, gamb, betb, ln2_s, ws=ws)
        ws.drain()
        ws_stack.close()
        if debug_phase == 3:
            p.barrier()
            p.dma("gpsimd", dbg["ln2"][:, :], ln2_s[:, :])
        if debug_phase >= 4:
            phase4(nc, p, x_d, xq_fm, ln2_s, wg_q, wu_q, wd_q, wo_s, out_d, ident, ws.stores)

        if debug_phase < 4:
            p.barrier()
            with nc.sbuf_tensor("zz", [128, E], F32) as zz:
                cz = p.op("vector", lambda e: e.memset(zz[:], 0.0), inc="dv")
                p.wait("scalar", cz)
                for t in range(NT):
                    p.dma("gpsimd", out_d[t * 128:(t + 1) * 128, :], zz[:])
        p.barrier()

        with nc.Block() as block:
            p.finalize(block)
    return nc


def phase1(nc, p, x_d, ident, xn_fm, xq_fm, t_id, ws=None):
    p.barrier()
    with ExitStack() as s1:
        sb1 = lambda n, s, d: s1.enter_context(nc.sbuf_tensor(n, s, d))
        xr = [sb1(f"xr{i}", [128, E], F32) for i in range(2)]
        stat = sb1("stat", [128, 8], F32)
        xnf = sb1("xnf", [128, E], F32)
        xnb2 = [sb1(f"xnb{i}", [128, E], BF16) for i in range(2)]
        xqb2 = [sb1(f"xqb{i}", [128, E], BF16) for i in range(2)]
        qb = QuantBufs(sb1, "p1", E)
        with nc.psum_tensor("pt1", [128, 1024], BF16) as pt1:
            xr_rel = [None, None]
            tp_n = [None, None]
            tp_q = [None, None]
            evq_rel = None
            pend = {"v": None}

            def flush_p1():
                nonlocal evq_rel
                if pend["v"] is None:
                    return
                t, par, c_nb, c_q = pend["v"]
                pend["v"] = None
                p.wait("tensor", [c_nb, evq_rel, t_id])
                for cch in range(ECH):
                    fn = lambda e, cch=cch, par=par: e.transpose(
                        out=pt1[:, cch * 128:(cch + 1) * 128],
                        in_=xnb2[par][:, cch * 128:(cch + 1) * 128], identity=ident[:])
                    c_tp = p.op("tensor", fn, inc="te" if cch == ECH - 1 else None)
                tp_n[par] = c_tp
                p.wait("vector", c_tp)
                c_evn = p.op("vector", lambda e, t=t: e.tensor_copy(
                    out=xn_fm[:, t * 1024:(t + 1) * 1024], in_=pt1[:]), inc="dv")
                p.wait("tensor", [c_evn, c_q])
                for cch in range(ECH):
                    fn = lambda e, cch=cch, par=par: e.transpose(
                        out=pt1[:, cch * 128:(cch + 1) * 128],
                        in_=xqb2[par][:, cch * 128:(cch + 1) * 128], identity=ident[:])
                    c_tp = p.op("tensor", fn, inc="te" if cch == ECH - 1 else None)
                tp_q[par] = c_tp
                p.wait("vector", c_tp)
                evq_rel = p.op("vector", lambda e, t=t: e.tensor_copy(
                    out=xq_fm[:, t * 1024:(t + 1) * 1024], in_=pt1[:]), inc="dv")

            for t in range(NT):
                par = t % 2
                b = xr[t % 2]
                p.wait("sync", xr_rel[t % 2])
                c_x = p.dma("sync", b[:], x_d[t * 128:(t + 1) * 128, :])
                p.wait("vector", c_x)
                c = p.op("vector", lambda e, b=b: e.reduce_sum(out=stat[:, 0:1], in_=b[:], axis=AX.X), inc="dv")
                p.wait("vector", c)
                c_mu = p.op("vector", lambda e: e.tensor_scalar_mul(out=stat[:, 0:1], in0=stat[:, 0:1], scalar1=1.0 / E), inc="dv")
                p.wait("scalar", c_x)
                p.wait("scalar", qb.g_sc)
                c_sq = p.op("scalar", lambda e, b=b: e.activation(out=qb.uab[:], in_=b[:], func=AFT.Square, accum_out=stat[:, 1:2]), inc="sc")
                p.wait("vector", [c_sq, c_mu])
                p.op("vector", lambda e: e.tensor_scalar_mul(out=stat[:, 2:3], in0=stat[:, 1:2], scalar1=1.0 / E))
                c = p.op("vector", lambda e: e.tensor_tensor(out=stat[:, 3:4], in0=stat[:, 0:1], in1=stat[:, 0:1], op=ALU.mult), inc="dv")
                p.wait("vector", c)
                c = p.op("vector", lambda e: e.tensor_tensor(out=stat[:, 2:3], in0=stat[:, 2:3], in1=stat[:, 3:4], op=ALU.subtract), inc="dv")
                p.wait("vector", c)
                c = p.op("vector", lambda e: e.tensor_scalar_add(out=stat[:, 6:7], in0=stat[:, 2:3], scalar1=1e-5), inc="dv")
                p.wait("scalar", c)
                c_sqrt = p.op("scalar", lambda e: e.activation(out=stat[:, 4:5], in_=stat[:, 6:7], func=AFT.Sqrt), inc="sc")
                p.wait("vector", c_sqrt)
                c_rs = p.op("vector", lambda e: e.reciprocal(out=stat[:, 3:4], in_=stat[:, 4:5]), inc="dv")
                p.wait("vector", c_rs)
                c_xn = p.op("vector", lambda e, b=b: e.tensor_scalar(
                    out=xnf[:], in0=b[:], scalar1=stat[:, 0:1], scalar2=stat[:, 3:4],
                    op0=ALU.subtract, op1=ALU.mult), inc="dv")
                xr_rel[t % 2] = c_xn
                p.wait("scalar", [c_xn, tp_n[par]])
                c_nb = p.op("scalar", lambda e, par=par: e.activation(out=xnb2[par][:], in_=xnf[:], func=AFT.Identity), inc="sc")
                c_q = emit_quant(p, qb, xnf[:], xqb2[par][:, 0:E], E, out_war=tp_q[par])
                prev = pend["v"]
                pend["v"] = None
                if prev is not None:
                    pend["v"], prev2 = prev, None
                    flush_p1()
                pend["v"] = (t, par, c_nb, c_q)
                if ws is not None:
                    ws.pump(1)
            flush_p1()


class WSteps:

    def __init__(self, nc, p, stack, wgt_d, wup_d, wdn_d, wo_d, wg_q, wu_q, wd_q, wo_s):
        sbw = lambda n, s, d: stack.enter_context(nc.sbuf_tensor(n, s, d))
        self.p = p
        self.raw = [sbw(f"wraw{i}", [128, 1024], F32) for i in range(2)]
        self.wqb = [sbw(f"wqbo{i}", [128, 1024], BF16) for i in range(2)]
        self.qb = QuantBufs(sbw, "pw", 1024, fp32=True)
        self.raw_rel = [None, None]
        self.store_war = [None, None]
        self.stores = []
        self.steps = []
        self.k = 0
        self.i = 0

        def quant_step(w_d, dst, r0, c0, w):
            def do():
                p = self.p
                li = self.k
                self.k += 1
                b = self.raw[li % 2]
                ob = self.wqb[li % 2]
                p.wait("sync", self.raw_rel[li % 2])
                c_w = p.dma("sync", b[:, 0:w], w_d[r0:r0 + 128, c0:c0 + w])
                p.wait("scalar", c_w)
                cq = emit_quant(p, self.qb, b[:, 0:w], ob[:, 0:w], w, out_war=self.store_war[li % 2])
                self.raw_rel[li % 2] = cq
                p.wait("scalar", cq)
                tok = p.dma("gpsimd", dst[r0:r0 + 128, c0:c0 + w], ob[:, 0:w])
                self.store_war[li % 2] = tok
                self.stores.append(tok)
            return do

        def wo_step(r):
            def do():
                p = self.p
                li = self.k
                self.k += 1
                b = self.raw[li % 2]
                ob = self.wqb[li % 2]
                p.wait("sync", self.raw_rel[li % 2])
                c_w = p.dma("sync", b[:, 0:1024], wo_d[r * 128:(r + 1) * 128, :])
                p.wait("scalar", [c_w, self.store_war[li % 2]])
                cc = p.op("scalar", lambda e, b=b, ob=ob: e.activation(out=ob[:, 0:1024], in_=b[:, 0:1024], func=AFT.Identity), inc="sc")
                self.raw_rel[li % 2] = cc
                p.wait("scalar", cc)
                tok = p.dma("gpsimd", wo_s[r * 128:(r + 1) * 128, :], ob[:, 0:1024])
                self.store_war[li % 2] = tok
                self.stores.append(tok)
            return do

        for w_d, dst in ((wgt_d, wg_q), (wup_d, wu_q)):
            for r in range(ICH):
                self.steps.append(quant_step(w_d, dst, r * 128, 0, E))
        for r in range(ECH):
            for c0, w in ((0, 1024), (1024, 1024), (2048, 768)):
                self.steps.append(quant_step(wdn_d, wd_q, r * 128, c0, w))
        for r in range(ECH):
            self.steps.append(wo_step(r))

    def pump(self, k):
        while k > 0 and self.i < len(self.steps):
            self.steps[self.i]()
            self.i += 1
            k -= 1

    def drain(self):
        self.pump(len(self.steps))


def phase2(nc, p, wq_d, wk_d, wv_d, wg1_d, wg2_d, ident, triu, ccb, ssb, xn_fm, attn_s, c_consts, t_tri, dbg_attn=None, ws=None):
    p.barrier()
    with ExitStack() as s2:
        sb2 = lambda n, s, d: s2.enter_context(nc.sbuf_tensor(n, s, d))
        wq_r = sb2("wq_r", [128, ECH * 1024], BF16)
        wk_r = sb2("wk_r", [128, ECH * 1024], BF16)
        wv_r = sb2("wv_r", [128, ECH * 1024], BF16)
        wg1_r = sb2("wg1_r", [128, ECH * 64], BF16)
        wg2_r = sb2("wg2_r", [64, 1024], BF16)
        gt_b = sb2("gt_b", [64, S], BF16)
        qs = sb2("qs", [128, 1024], BF16)
        ks = sb2("ks", [128, 1024], BF16)
        vs = sb2("vs", [128, 1024], BF16)
        gs = sb2("gs", [128, 1024], BF16)
        qro = sb2("qro", [128, 1024], BF16)
        kro = sb2("kro", [128, 1024], BF16)
        qzA = sb2("qzA", [128, 1024], BF16)
        qzB = sb2("qzB", [128, 1024], BF16)
        kzA = sb2("kzA", [128, 1024], BF16)
        kzB = sb2("kzB", [128, 1024], BF16)
        em = sb2("em", [128, 1024], BF16)
        S_sb = sb2("S_sb", [128, 512], BF16)
        ab = [sb2(f"ab{i}", [128, 1024], BF16) for i in range(2)]

        raw_scope = ExitStack()
        raw = [raw_scope.enter_context(nc.sbuf_tensor(f"w2raw{i}", [128, 1024], F32)) for i in range(2)]
        raw_rel = [None, None]
        i = 0
        for w_d, w_r in ((wq_d, wq_r), (wk_d, wk_r), (wv_d, wv_r)):
            for c in range(ECH):
                b = raw[i % 2]
                p.wait("sync", raw_rel[i % 2])
                c_w = p.dma("sync", b[:], w_d[c * 128:(c + 1) * 128, :])
                p.wait("vector", c_w)
                c_c = p.op("vector", lambda e, b=b, w_r=w_r, c=c: e.tensor_copy(out=w_r[:, c * 1024:(c + 1) * 1024], in_=b[:]), inc="dv")
                raw_rel[i % 2] = c_c
                i += 1
        for c in range(ECH):
            b = raw[i % 2]
            p.wait("sync", raw_rel[i % 2])
            c_w = p.dma("sync", b[:, 0:D], wg1_d[c * 128:(c + 1) * 128, :])
            p.wait("vector", c_w)
            c_c = p.op("vector", lambda e, b=b, c=c: e.tensor_copy(out=wg1_r[:, c * 64:(c + 1) * 64], in_=b[:, 0:D]), inc="dv")
            raw_rel[i % 2] = c_c
            i += 1
        b = raw[i % 2]
        p.wait("sync", raw_rel[i % 2])
        c_w = p.dma("sync", b[0:64, 0:E], wg2_d[:, :])
        p.wait("vector", c_w)
        c_wcast = p.op("vector", lambda e, b=b: e.tensor_copy(out=wg2_r[:, :], in_=b[0:64, 0:E]), inc="dv")
        c_szero = p.op("vector", lambda e: e.memset(S_sb[:], 0.0), inc="dv")
        p.op("vector", lambda e: e.memset(qzA[64:128, :], 0.0))
        p.op("vector", lambda e: e.memset(qzB[0:64, :], 0.0))
        p.op("vector", lambda e: e.memset(kzA[64:128, :], 0.0))
        c_zpad = p.op("vector", lambda e: e.memset(kzB[0:64, :], 0.0), inc="dv")
        p.barrier()
        raw_scope.close()
        if ws is not None:
            ws.pump(4)
        if P2_LEVEL <= 1:
            return

        with (nc.psum_tensor("pa", [128, 1024], F32) as pa,
              nc.psum_tensor("pb", [128, 1024], F32) as pb,
              nc.psum_tensor("po", [128, 1024], F32) as po,
              nc.psum_tensor("pss", [128, 512], F32) as pss,
              nc.psum_tensor("pt2", [128, 1024], BF16) as pt2):

            p.wait("tensor", [c_wcast, c_consts])
            c_gt = None
            for th in (range(4) if "gt" not in P2_SKIP else []):
                p.wait("tensor", c_gt)
                for tt in range(4):
                    t = th * 4 + tt
                    for c in range(ECH):
                        fn = lambda e, t=t, c=c, tt=tt: e.matmul(
                            out=pa[0:64, tt * 128:(tt + 1) * 128],
                            lhsT=wg1_r[:, c * 64:(c + 1) * 64],
                            rhs=xn_fm[:, t * 1024 + c * 128:t * 1024 + (c + 1) * 128],
                            start=(c == 0), stop=(c == ECH - 1))
                        c_mm = p.op("tensor", fn, inc="te" if (c == ECH - 1 and tt == 3) else None)
                p.wait("scalar", c_mm)
                c_gt = p.op("scalar", lambda e, th=th: e.activation(out=gt_b[:, th * 512:(th + 1) * 512], in_=pa[0:64, 0:512], func=AFT.Identity), inc="sc")
            p.wait("tensor", c_gt)
            if P2_LEVEL <= 2:
                return

            po_rel = None
            pt2_rel = None
            ab_rel = [None, None]
            pa_rel = None
            pb_rel = None
            c_qro_prev = None
            c_kro_prev = None
            c_smm_prev = None
            s_evac_prev = c_szero
            for n in range(P2_CHUNKS):
                p.wait("tensor", pa_rel)
                for c in (range(ECH) if "qkv" not in P2_SKIP else []):
                    for hcol in range(2):
                        fn = lambda e, n=n, c=c, hcol=hcol: e.matmul(
                            out=pa[:, hcol * 512:(hcol + 1) * 512],
                            lhsT=xn_fm[:, n * 1024 + c * 128:n * 1024 + (c + 1) * 128],
                            rhs=wq_r[:, c * 1024 + hcol * 512:c * 1024 + (hcol + 1) * 512],
                            start=(c == 0), stop=(c == ECH - 1))
                        c_qmm = p.op("tensor", fn, inc="te" if (c == ECH - 1 and hcol == 1) else None)
                if "qkv" not in P2_SKIP:
                    p.wait("scalar", [c_qmm, c_qro_prev])
                    c_qs = p.op("scalar", lambda e: e.activation(out=qs[:], in_=pa[:], func=AFT.Silu), inc="sc")
                else:
                    c_qs = None
                p.wait("tensor", pb_rel)
                for c in (range(ECH) if "qkv" not in P2_SKIP else []):
                    for hcol in range(2):
                        fn = lambda e, n=n, c=c, hcol=hcol: e.matmul(
                            out=pb[:, hcol * 512:(hcol + 1) * 512],
                            lhsT=xn_fm[:, n * 1024 + c * 128:n * 1024 + (c + 1) * 128],
                            rhs=wk_r[:, c * 1024 + hcol * 512:c * 1024 + (hcol + 1) * 512],
                            start=(c == 0), stop=(c == ECH - 1))
                        c_kmm = p.op("tensor", fn, inc="te" if (c == ECH - 1 and hcol == 1) else None)
                if "qkv" not in P2_SKIP:
                    p.wait("scalar", [c_kmm, c_kro_prev])
                    c_ks = p.op("scalar", lambda e: e.activation(out=ks[:], in_=pb[:], func=AFT.Silu), inc="sc")
                else:
                    c_ks = None
                p.wait("tensor", c_qs)
                for c in (range(ECH) if "qkv" not in P2_SKIP else []):
                    for hcol in range(2):
                        fn = lambda e, n=n, c=c, hcol=hcol: e.matmul(
                            out=pa[:, hcol * 512:(hcol + 1) * 512],
                            lhsT=xn_fm[:, n * 1024 + c * 128:n * 1024 + (c + 1) * 128],
                            rhs=wv_r[:, c * 1024 + hcol * 512:c * 1024 + (hcol + 1) * 512],
                            start=(c == 0), stop=(c == ECH - 1))
                        c_vmm = p.op("tensor", fn, inc="te" if (c == ECH - 1 and hcol == 1) else None)
                if "qkv" not in P2_SKIP:
                    p.wait("scalar", [c_vmm, c_smm_prev])
                    c_vs = p.op("scalar", lambda e: e.activation(out=vs[:], in_=pa[:], func=AFT.Identity), inc="sc")
                else:
                    c_vs = None
                p.wait("tensor", c_ks)
                for hcol in (range(2) if "gate" not in P2_SKIP else []):
                    fn = lambda e, n=n, hcol=hcol: e.matmul(
                        out=pb[:, hcol * 512:(hcol + 1) * 512],
                        lhsT=gt_b[:, n * 128:(n + 1) * 128],
                        rhs=wg2_r[:, hcol * 512:(hcol + 1) * 512],
                        start=True, stop=True)
                    c_gmm = p.op("tensor", fn, inc="te" if hcol == 1 else None)
                if "gate" not in P2_SKIP:
                    p.wait("scalar", [c_gmm, po_rel])
                    c_gs = p.op("scalar", lambda e: e.activation(out=gs[:], in_=pb[:], func=AFT.Sigmoid), inc="sc")
                else:
                    c_gs = None

                def rope(src, dst, c_src, n=n):
                    p.wait("vector", c_src)
                    s3 = lambda a: a[:].rearrange("q (h j) -> q h j", j=64)
                    cc_t = ccb[:, n * 64:(n + 1) * 64].rearrange("q (o j) -> q o j", o=1).broadcast_to([128, 16, 64])
                    ssA = ssb[:, n * 64:n * 64 + 32].rearrange("q (o j) -> q o j", o=1).broadcast_to([128, 16, 32]).rearrange("q h (o j) -> q h o j", o=1)
                    ssB = ssb[:, n * 64 + 32:n * 64 + 64].rearrange("q (o j) -> q o j", o=1).broadcast_to([128, 16, 32]).rearrange("q h (o j) -> q h o j", o=1)
                    src4 = src[:].rearrange("q (h k j) -> q h k j", k=2, j=32)
                    tmp4 = em[:, 0:1024].rearrange("q (h k j) -> q h k j", k=2, j=32)
                    dst4 = dst[:].rearrange("q (h k j) -> q h k j", k=2, j=32)
                    p.op("vector", lambda e: e.tensor_tensor(out=s3(em[:, 0:1024]), in0=s3(src), in1=cc_t, op=ALU.mult))
                    p.op("vector", lambda e: e.tensor_tensor(out=dst4[:, :, 0:1, :], in0=src4[:, :, 1:2, :], in1=ssA, op=ALU.mult))
                    p.op("vector", lambda e: e.tensor_tensor(out=dst4[:, :, 0:1, :], in0=dst4[:, :, 0:1, :], in1=tmp4[:, :, 0:1, :], op=ALU.add))
                    p.op("vector", lambda e: e.tensor_tensor(out=dst4[:, :, 1:2, :], in0=src4[:, :, 0:1, :], in1=ssB, op=ALU.mult))
                    return p.op("vector", lambda e: e.tensor_tensor(out=dst4[:, :, 1:2, :], in0=dst4[:, :, 1:2, :], in1=tmp4[:, :, 1:2, :], op=ALU.add), inc="dv")
                if "rope" not in P2_SKIP:
                    c_qro = rope(qs, qro, c_qs)
                    c_qro_prev = c_qro
                    p.wait("vector", c_smm_prev)
                    c_kro = rope(ks, kro, c_ks)
                    c_kro_prev = c_kro
                else:
                    c_qro = c_kro = None

                p.wait("tensor", [c_qro, pt2_rel])
                for cch in (range(ECH) if "tp" not in P2_SKIP else []):
                    fn = lambda e, cch=cch: e.transpose(
                        out=pt2[:, cch * 128:(cch + 1) * 128],
                        in_=qro[:, cch * 128:(cch + 1) * 128], identity=ident[:])
                    c_tp = p.op("tensor", fn, inc="te" if cch == ECH - 1 else None)
                if "tp" not in P2_SKIP:
                    p.wait("vector", c_tp)
                    p.op("vector", lambda e: e.tensor_copy(out=qzA[0:64, :], in_=pt2[0:64, :]))
                    c_qf = p.op("vector", lambda e: e.tensor_copy(out=qzB[64:128, :], in_=pt2[64:128, :]), inc="dv")
                else:
                    c_qf = None
                p.wait("tensor", [c_kro, c_qf])
                for cch in (range(ECH) if "tp" not in P2_SKIP else []):
                    fn = lambda e, cch=cch: e.transpose(
                        out=pt2[:, cch * 128:(cch + 1) * 128],
                        in_=kro[:, cch * 128:(cch + 1) * 128], identity=ident[:])
                    c_tp = p.op("tensor", fn, inc="te" if cch == ECH - 1 else None)
                if "tp" not in P2_SKIP:
                    p.wait("vector", c_tp)
                    p.op("vector", lambda e: e.tensor_copy(out=kzA[0:64, :], in_=pt2[0:64, :]))
                    c_kf = p.op("vector", lambda e: e.tensor_copy(out=kzB[64:128, :], in_=pt2[64:128, :]), inc="dv")
                    pt2_rel = c_kf
                else:
                    c_kf = None

                if P2_LEVEL <= 3:
                    pt2_rel = c_kf
                    pa_rel = c_vs
                    pb_rel = c_gs
                    c_qro_prev = c_qro
                    c_kro_prev = c_kro
                    continue
                p.wait("tensor", c_kf)
                c_e = [None, None]
                for half, pXX in ((0, pa), (1, pb)):
                    p.wait("tensor", c_vs if half == 0 else c_gs)
                    for hl in range(8):
                        h = half * 8 + hl
                        kz = kzA if h % 2 == 0 else kzB
                        qz = qzA if h % 2 == 0 else qzB
                        cc2 = h // 2
                        fn = lambda e, kz=kz, qz=qz, cc2=cc2, hl=hl, pXX=pXX: e.matmul(
                            out=pXX[:, hl * 128:(hl + 1) * 128],
                            lhsT=kz[:, cc2 * 128:(cc2 + 1) * 128],
                            rhs=qz[:, cc2 * 128:(cc2 + 1) * 128],
                            start=True, stop=True)
                        c_e[half] = p.op("tensor", fn, inc="te" if hl == 7 else None)
                tri8 = triu[:, :].rearrange("q (o j) -> q o j", o=1).broadcast_to([128, 8, 128])
                if P2_LEVEL <= 4 or P2_LEVEL in (35, 36, 41, 42, 43, 44):
                    if P2_LEVEL == 41: hsel = [0]
                    elif P2_LEVEL == 42: hsel = [1]
                    elif P2_LEVEL == 43: hsel = [0, 1]
                    elif P2_LEVEL == 44: hsel = [0, 2, 4, 6]
                    else: hsel = list(range(8))
                    for half, pXX in ((0, pa), (1, pb)):
                        p.wait("tensor", c_vs if half == 0 else c_gs)
                        for hl in hsel:
                            h = half * 8 + hl
                            kz = kzA if h % 2 == 0 else kzB
                            qz = qzA if h % 2 == 0 else qzB
                            cc2 = h // 2
                            fn = lambda e, kz=kz, qz=qz, cc2=cc2, hl=hl, pXX=pXX: e.matmul(
                                out=pXX[:, hl * 128:(hl + 1) * 128],
                                lhsT=kz[:, cc2 * 128:(cc2 + 1) * 128],
                                rhs=qz[:, cc2 * 128:(cc2 + 1) * 128],
                                start=True, stop=True)
                            c_eh = p.op("tensor", fn, inc="te" if hl == hsel[-1] else None)
                        p.wait("vector", [c_eh, t_tri])
                        if P2_LEVEL in (35, 41, 42, 43, 44):
                            c_em = p.op("vector", lambda e, pXX=pXX: e.tensor_copy(out=em[:], in_=pXX[:]), inc="dv")
                        elif P2_LEVEL == 36:
                            c_em = p.op("vector", lambda e, pXX=pXX: e.tensor_tensor(
                                out=em[:].rearrange("q (hh j) -> q hh j", j=128), in0=pXX[:].rearrange("q (hh j) -> q hh j", j=128), in1=tri8, op=ALU.mult), inc="dv")
                        else:
                            c_em = p.op("vector", lambda e, pXX=pXX: e.tensor_tensor(
                                out=em[:].rearrange("q (hh j) -> q hh j", j=128), in0=pXX[:].rearrange("q (hh j) -> q hh j", j=128), in1=tri8, op=ALU.mult), inc="dv")
                        if half == 0:
                            pa_rel = c_em
                        else:
                            pb_rel = c_em
                    pt2_rel = c_kf
                    c_qro_prev = c_qro
                    c_kro_prev = c_kro
                    continue
                c_omm = None
                for half, pXX in ((0, pa), (1, pb)):
                    p.wait("vector", [c_e[half], t_tri, c_omm])
                    c_em = p.op("vector", lambda e, pXX=pXX: e.tensor_tensor(
                        out=em[:].rearrange("q (hh j) -> q hh j", j=128), in0=pXX[:].rearrange("q (hh j) -> q hh j", j=128), in1=tri8, op=ALU.mult), inc="dv")
                    if half == 0:
                        pa_rel = c_em
                    else:
                        pb_rel = c_em
                    p.wait("tensor", [c_em, po_rel, s_evac_prev])
                    for hl in range(8):
                        h = half * 8 + hl
                        bp = 64 * (h % 2)
                        cc2 = h // 2
                        fn1 = lambda e, h=h, hl=hl: e.matmul(
                            out=po[:, h * 64:(h + 1) * 64],
                            lhsT=em[:, hl * 128:(hl + 1) * 128],
                            rhs=vs[:, h * 64:(h + 1) * 64],
                            start=True, stop=False)
                        p.op("tensor", fn1)
                        qz = qzA if h % 2 == 0 else qzB
                        fn2 = lambda e, h=h, qz=qz, cc2=cc2: e.matmul(
                            out=po[:, h * 64:(h + 1) * 64],
                            lhsT=qz[:, cc2 * 128:(cc2 + 1) * 128],
                            rhs=S_sb[:, cc2 * 64:(cc2 + 1) * 64],
                            start=False, stop=True)
                        c_omm = p.op("tensor", fn2, inc="te" if hl == 7 else None)

                for h in range(H):
                    bp = 64 * (h % 2)
                    fn = lambda e, h=h, bp=bp, n=n: e.matmul(
                        out=pss[bp:bp + 64, (h // 2) * 64:(h // 2 + 1) * 64],
                        lhsT=kro[:, h * 64:(h + 1) * 64],
                        rhs=vs[:, h * 64:(h + 1) * 64],
                        start=(n == 0 and h < 2), stop=(n == NT - 1 and h >= H - 2),
                        tile_position=(0, bp), skip_group_check=True)
                    c_smm = p.op("tensor", fn, inc="te" if h == H - 1 else None)
                c_smm_prev = c_smm
                p.wait("scalar", c_smm)
                s_evac_prev = p.op("scalar", lambda e: e.activation(out=S_sb[:], in_=pss[:], func=AFT.Identity), inc="sc")

                p.wait("vector", [c_omm, c_gs, ab_rel[n % 2]])
                po_rel = p.op("vector", lambda e, n=n: e.tensor_tensor(
                    out=ab[n % 2][:], in0=po[:], in1=gs[:], op=ALU.mult), inc="dv")
                p.wait("scalar", po_rel)
                ab_rel[n % 2] = p.dma("gpsimd", attn_s[n * 128:(n + 1) * 128, :], ab[n % 2][:])
                if dbg_attn is not None:
                    ab_rel[n % 2] = p.dma("gpsimd", dbg_attn[:, n * 1024:(n + 1) * 1024], ab[n % 2][:])
                if ws is not None:
                    ws.pump(3)


def phase3(nc, p, ident, attn_s, gamb, betb, ln2_s, ws=None):
    p.barrier()
    with ExitStack() as s3:
        sb3 = lambda n, s, d: s3.enter_context(nc.sbuf_tensor(n, s, d))
        stat = sb3("stat3", [128, 8], F32)
        n1 = sb3("n1", [128, E], BF16)
        sq = sb3("sq3", [128, E], F32)
        ln2b = sb3("ln2b", [128, E], BF16)
        lnst = [sb3(f"lnst{i}", [128, E], BF16) for i in range(2)]
        abuf = [sb3(f"a3_{i}", [128, E], BF16) for i in range(2)]
        with nc.psum_tensor("pt3", [128, 1024], BF16) as pt3:
            ev_rel = None
            n1_rel = None
            a_rel = [None, None]
            ln_rel = [None, None]
            for t in range(NT):
                p.wait("sync", a_rel[t % 2])
                c_a = p.dma("sync", abuf[t % 2][:], attn_s[t * 128:(t + 1) * 128, :])
                a = abuf[t % 2][:]
                p.wait("vector", c_a)
                p.wait("scalar", c_a)
                c = p.op("vector", lambda e, a=a: e.reduce_sum(out=stat[:, 0:1], in_=a, axis=AX.X), inc="dv")
                p.wait("vector", c)
                c_mu = p.op("vector", lambda e: e.tensor_scalar_mul(out=stat[:, 0:1], in0=stat[:, 0:1], scalar1=1.0 / E), inc="dv")
                c_sq = p.op("scalar", lambda e, a=a: e.activation(out=sq[:], in_=a, func=AFT.Square, accum_out=stat[:, 1:2]), inc="sc")
                p.wait("vector", [c_sq, c_mu])
                p.op("vector", lambda e: e.tensor_scalar_mul(out=stat[:, 2:3], in0=stat[:, 1:2], scalar1=1.0 / E))
                c = p.op("vector", lambda e: e.tensor_tensor(out=stat[:, 3:4], in0=stat[:, 0:1], in1=stat[:, 0:1], op=ALU.mult), inc="dv")
                p.wait("vector", c)
                c = p.op("vector", lambda e: e.tensor_tensor(out=stat[:, 2:3], in0=stat[:, 2:3], in1=stat[:, 3:4], op=ALU.subtract), inc="dv")
                p.wait("vector", c)
                c = p.op("vector", lambda e: e.tensor_scalar_add(out=stat[:, 6:7], in0=stat[:, 2:3], scalar1=1e-5), inc="dv")
                p.wait("scalar", c)
                c_sqrt = p.op("scalar", lambda e: e.activation(out=stat[:, 4:5], in_=stat[:, 6:7], func=AFT.Sqrt), inc="sc")
                p.wait("vector", c_sqrt)
                c_rs = p.op("vector", lambda e: e.reciprocal(out=stat[:, 3:4], in_=stat[:, 4:5]), inc="dv")
                p.wait("vector", [c_rs, n1_rel])
                c_n1 = p.op("vector", lambda e, a=a: e.tensor_scalar(
                    out=n1[:], in0=a, scalar1=stat[:, 0:1], scalar2=stat[:, 3:4],
                    op0=ALU.subtract, op1=ALU.mult), inc="dv")
                a_rel[t % 2] = c_n1
                p.wait("vector", c_n1)
                p.op("vector", lambda e: e.tensor_tensor(out=ln2b[:], in0=n1[:], in1=gamb[:], op=ALU.mult))
                c_l2 = p.op("vector", lambda e: e.tensor_tensor(out=ln2b[:], in0=ln2b[:], in1=betb[:], op=ALU.add), inc="dv")
                n1_rel = c_l2
                p.wait("tensor", [c_l2, ev_rel])
                for cch in range(ECH):
                    fn = lambda e, cch=cch: e.transpose(
                        out=pt3[:, cch * 128:(cch + 1) * 128],
                        in_=ln2b[:, cch * 128:(cch + 1) * 128], identity=ident[:])
                    c_tp = p.op("tensor", fn, inc="te" if cch == ECH - 1 else None)
                p.wait("vector", [c_tp, ln_rel[t % 2]])
                ev_rel = p.op("vector", lambda e, t=t: e.tensor_copy(out=lnst[t % 2][:], in_=pt3[:]), inc="dv")
                p.wait("scalar", ev_rel)
                ln_rel[t % 2] = p.dma("gpsimd", ln2_s[:, t * 1024:(t + 1) * 1024], lnst[t % 2][:])
                if ws is not None:
                    ws.pump(1)


def phase4(nc, p, x_d, xq_fm, ln2_s, wg_q, wu_q, wd_q, wo_s, out_d, ident, w_fence):
    p.barrier()
    with ExitStack() as s4:
        sb4 = lambda n, s, d: s4.enter_context(nc.sbuf_tensor(n, s, d))
        ffn_acc = sb4("ffn_acc", [128, 8 * 1024], F32)
        wgs = [sb4(f"wgs{i}", [128, ECH * 1024], BF16) for i in range(2)]
        wus = [sb4(f"wus{i}", [128, ECH * 1024], BF16) for i in range(2)]
        wds = sb4("wds", [128, 8 * 1024], BF16)
        wos = wds
        sg = sb4("sg", [128, 1024], BF16)
        ub = sb4("ub", [128, 1024], BF16)
        hb = sb4("hb", [128, 1024], BF16)
        hqb = [sb4(f"hqb{i}", [128, 1024], BF16) for i in range(2)]
        hqf = [sb4(f"hqf{i}", [128, 1024], BF16) for i in range(2)]
        qb = QuantBufs(sb4, "p4", 1024, fp32=True)
        xrt = sb4("xrt", [128, E], F32)
        outb = sb4("outb", [128, E], F32)
        lnt = [sb4(f"lnt{i}", [128, E], BF16) for i in range(2)]
        with (nc.psum_tensor("pg", [128, 1024], F32) as pg,
              nc.psum_tensor("pu", [128, 1024], F32) as pu,
              nc.psum_tensor("pf", [128, 1024], F32) as pf,
              nc.psum_tensor("pt4", [128, 1024], BF16) as pt4):
            p.wait("sync", w_fence)
            w_rel = None
            c_wo_prev = None
            pt4_rel = None
            hq_rel = None
            c_ub_prev = None
            tp_tok = [None, None]
            dn_war = [None, None]
            pf_rel = None
            x_rel = None
            out_rel = None
            ln_war = [None, None]
            gu_war = [None, None]

            def issue_gu(k):
                grp_, ip_ = divmod(k, 3)
                W_ = WIPW[ip_]
                par = k % 2
                p.wait("sync", gu_war[par])
                toks = []
                for c in range(ECH):
                    toks.append(p.dma("sync", wgs[par][:, c * 1024:c * 1024 + W_], wg_q[ip_ * 1024:ip_ * 1024 + W_, c * 128:(c + 1) * 128], tp=True))
                    toks.append(p.dma("sync", wus[par][:, c * 1024:c * 1024 + W_], wu_q[ip_ * 1024:ip_ * 1024 + W_, c * 128:(c + 1) * 128], tp=True))
                return toks

            gu_toks = issue_gu(0)
            for grp in range(2):
                for ip in range(3):
                    W = WIPW[ip]
                    NIC = WICH[ip]
                    k = grp * 3 + ip
                    par = k % 2
                    p.wait("sync", [w_rel, c_wo_prev])
                    ldd = []
                    for icl in range(NIC):
                        ic = ip * 8 + icl
                        ldd.append(p.dma("sync", wds[:, icl * 1024:(icl + 1) * 1024], wd_q[:, ic * 128:(ic + 1) * 128], tp=True))
                    nxt = issue_gu(k + 1) if k + 1 < 6 else []
                    p.wait("tensor", gu_toks)
                    gu_toks = nxt
                    prev4 = None
                    for tl in range(9):
                        if tl < 8:
                            t = grp * 8 + tl
                            hpar = tl % 2
                            p.wait("tensor", c_ub_prev)
                            for c in range(ECH):
                                for hcol in range((W + 511) // 512):
                                    lo, hi = hcol * 512, min(W, (hcol + 1) * 512)
                                    fn = lambda e, t=t, c=c, lo=lo, hi=hi, par=par: e.matmul(
                                        out=pg[:, lo:hi],
                                        lhsT=xq_fm[:, t * 1024 + c * 128:t * 1024 + (c + 1) * 128],
                                        rhs=wgs[par][:, c * 1024 + lo:c * 1024 + hi],
                                        start=(c == 0), stop=(c == ECH - 1))
                                    p.op("tensor", fn)
                                    fn = lambda e, t=t, c=c, lo=lo, hi=hi, par=par: e.matmul(
                                        out=pu[:, lo:hi],
                                        lhsT=xq_fm[:, t * 1024 + c * 128:t * 1024 + (c + 1) * 128],
                                        rhs=wus[par][:, c * 1024 + lo:c * 1024 + hi],
                                        start=(c == 0), stop=(c == ECH - 1))
                                    c_gu = p.op("tensor", fn, inc="te" if (c == ECH - 1 and hi == W) else None)
                            p.wait("scalar", [c_gu, hq_rel])
                            p.op("scalar", lambda e, W=W: e.activation(out=sg[:, 0:W], in_=pg[:, 0:W], func=AFT.Silu))
                            c_ub = p.op("scalar", lambda e, W=W: e.activation(out=ub[:, 0:W], in_=pu[:, 0:W], func=AFT.Identity), inc="sc")
                            c_ub_prev = c_ub
                            p.wait("vector", c_ub)
                            c_h = p.op("vector", lambda e, W=W: e.tensor_tensor(out=hb[:, 0:W], in0=sg[:, 0:W], in1=ub[:, 0:W], op=ALU.mult), inc="dv")
                            p.wait("scalar", c_h)
                            c_hq = emit_quant(p, qb, hb[:, 0:W], hqb[hpar][:, 0:W], W, out_war=tp_tok[hpar])
                            hq_rel = c_hq
                        if prev4 is not None:
                            ptl, p_hq = prev4
                            ppar = ptl % 2
                            p.wait("tensor", [p_hq, pt4_rel])
                            for icl in range(NIC):
                                fn = lambda e, icl=icl, ppar=ppar: e.transpose(
                                    out=pt4[:, icl * 128:(icl + 1) * 128],
                                    in_=hqb[ppar][:, icl * 128:(icl + 1) * 128], identity=ident[:])
                                c_tp = p.op("tensor", fn, inc="te" if icl == NIC - 1 else None)
                            tp_tok[ppar] = c_tp
                            p.wait("vector", [c_tp, dn_war[ppar]])
                            c_hqf = p.op("vector", lambda e, NIC=NIC, ppar=ppar: e.tensor_copy(out=hqf[ppar][:, 0:NIC * 128], in_=pt4[:, 0:NIC * 128]), inc="dv")
                            pt4_rel = c_hqf
                            p.wait("tensor", [c_hqf, pf_rel] + (ldd if ptl == 0 else []))
                            for icl in range(NIC):
                                for hcol in range(2):
                                    fn = lambda e, icl=icl, hcol=hcol, ppar=ppar: e.matmul(
                                        out=pf[:, hcol * 512:(hcol + 1) * 512],
                                        lhsT=hqf[ppar][:, icl * 128:(icl + 1) * 128],
                                        rhs=wds[:, icl * 1024 + hcol * 512:icl * 1024 + (hcol + 1) * 512],
                                        start=(icl == 0), stop=(icl == NIC - 1))
                                    c_dn = p.op("tensor", fn, inc="te" if (icl == NIC - 1 and hcol == 1) else None)
                            dn_war[ppar] = c_dn
                            p.wait("vector", c_dn)
                            if ip == 0:
                                pf_rel = p.op("vector", lambda e, ptl=ptl: e.tensor_copy(out=ffn_acc[:, ptl * 1024:(ptl + 1) * 1024], in_=pf[:]), inc="dv")
                            else:
                                pf_rel = p.op("vector", lambda e, ptl=ptl: e.tensor_tensor(out=ffn_acc[:, ptl * 1024:(ptl + 1) * 1024], in0=ffn_acc[:, ptl * 1024:(ptl + 1) * 1024], in1=pf[:], op=ALU.add), inc="dv")
                        prev4 = (tl, c_hq) if tl < 8 else None
                    w_rel = c_dn
                    gu_war[par] = c_gu
                p.wait("sync", w_rel)
                ldo = []
                for c in range(ECH):
                    ldo.append(p.dma("sync", wos[:, c * 1024:(c + 1) * 1024], wo_s[c * 128:(c + 1) * 128, :]))
                p.wait("tensor", ldo)
                for tl in range(8):
                    t = grp * 8 + tl
                    p.wait("sync", [x_rel, ln_war[tl % 2]])
                    c_x = p.dma("sync", xrt[:], x_d[t * 128:(t + 1) * 128, :])
                    c_ln = p.dma("sync", lnt[tl % 2][:], ln2_s[:, t * 1024:(t + 1) * 1024])
                    p.wait("tensor", [pf_rel, c_ln])
                    for c in range(ECH):
                        for hcol in range(2):
                            fn = lambda e, t=t, c=c, hcol=hcol, tl=tl: e.matmul(
                                out=pf[:, hcol * 512:(hcol + 1) * 512],
                                lhsT=lnt[tl % 2][:, c * 128:(c + 1) * 128],
                                rhs=wos[:, c * 1024 + hcol * 512:c * 1024 + (hcol + 1) * 512],
                                start=(c == 0), stop=(c == ECH - 1))
                            c_wo = p.op("tensor", fn, inc="te" if (c == ECH - 1 and hcol == 1) else None)
                    c_wo_prev = c_wo
                    ln_war[tl % 2] = c_wo
                    p.wait("vector", [c_wo, c_x, out_rel])
                    c_o1 = p.op("vector", lambda e, tl=tl: e.tensor_tensor(out=outb[:], in0=pf[:], in1=ffn_acc[:, tl * 1024:(tl + 1) * 1024], op=ALU.add), inc="dv")
                    pf_rel = c_o1
                    p.wait("vector", c_o1)
                    c_o2 = p.op("vector", lambda e: e.tensor_tensor(out=outb[:], in0=outb[:], in1=xrt[:], op=ALU.add), inc="dv")
                    x_rel = c_o2
                    p.wait("scalar", c_o2)
                    out_rel = p.dma("gpsimd", out_d[t * 128:(t + 1) * 128, :], outb[:])


def make_consts():
    ident = np.eye(128, dtype=ml_dtypes.bfloat16)
    tri = np.triu(np.ones((128, 128), dtype=np.float32)).astype(ml_dtypes.bfloat16)
    return ident, tri


_NC_CACHE = {}

def _get_nc():
    if "nc" not in _NC_CACHE:
        _NC_CACHE["nc"] = build_nc(99)
    return _NC_CACHE["nc"]


def kernel(L0_attn_x, L0_attn_Wq, L0_attn_Wk, L0_attn_Wv, L0_attn_Wo,
           L0_attn_Wg1, L0_attn_Wg2, L0_attn_gamma, L0_attn_beta,
           L0_ffn_w_gate, L0_ffn_w_up, L0_ffn_w_down):
    x = np.ascontiguousarray(np.asarray(L0_attn_x, dtype=np.float32))
    ident, tri = make_consts()
    cc, ss = rope_tables()
    gam = np.ascontiguousarray(np.broadcast_to(np.asarray(L0_attn_gamma, np.float32), (128, E)))
    bet = np.ascontiguousarray(np.broadcast_to(np.asarray(L0_attn_beta, np.float32), (128, E)))
    common = {
        "wq": np.ascontiguousarray(np.asarray(L0_attn_Wq, np.float32)),
        "wk": np.ascontiguousarray(np.asarray(L0_attn_Wk, np.float32)),
        "wv": np.ascontiguousarray(np.asarray(L0_attn_Wv, np.float32)),
        "wo": np.ascontiguousarray(np.asarray(L0_attn_Wo, np.float32)),
        "wg1": np.ascontiguousarray(np.asarray(L0_attn_Wg1, np.float32)),
        "wg2": np.ascontiguousarray(np.asarray(L0_attn_Wg2, np.float32)),
        "wgate": np.ascontiguousarray(np.asarray(L0_ffn_w_gate, np.float32)),
        "wup": np.ascontiguousarray(np.asarray(L0_ffn_w_up, np.float32)),
        "wdown": np.ascontiguousarray(np.asarray(L0_ffn_w_down, np.float32)),
        "ident": ident, "triu": tri, "ctab": cc, "stab": ss,
        "gam": gam, "bet": bet,
    }
    nc = _get_nc()
    in_maps = [dict(common, x=np.ascontiguousarray(x[c])) for c in range(x.shape[0])]
    res = run_bass_kernel_spmd(nc, in_maps, core_ids=list(range(len(in_maps))))
    return np.stack([r["out"] for r in res.results], axis=0)


# version 16 (final)
# speedup vs baseline: 1.1180x; 1.0004x over previous
import numpy as np
import ml_dtypes
import concourse.bass as bass
import concourse.mybir as mybir
from concourse.bass_utils import run_bass_kernel_spmd
from contextlib import ExitStack

F32 = mybir.dt.float32
BF16 = mybir.dt.bfloat16
I16 = mybir.dt.int16
ALU = mybir.AluOpType
AX = mybir.AxisListType
AFT = mybir.ActivationFunctionType

S, E, IFF = 2048, 1024, 2816
NT = S // 128
ECH = E // 128
H, D = 16, 64
NBE = E // 16
ICH = IFF // 128
WICH = (8, 8, 6)
WIPW = (1024, 1024, 768)

DEBUG_PHASE = 99
P2_LEVEL = 5
P2_CHUNKS = NT
P2_SKIP = set()
N_SQ = 12
N_GQ = 8


class Prog:
    ENGINES = ("sync", "vector", "scalar", "tensor", "gpsimd")

    def __init__(self, nc, st):
        self.nc = nc
        names = ["dv", "sc", "te", "pl"]
        names += [f"sq{i}" for i in range(N_SQ)]
        names += [f"gq{i}" for i in range(N_GQ)]
        self.sems = {n: st.enter_context(nc.semaphore(n)) for n in names}
        self.cnt = {n: 0 for n in names}
        self.q = {e: [] for e in self.ENGINES}
        self.slot_rr = {"sync": 0, "gpsimd": 0}

    def emit(self, eng, fn):
        self.q[eng].append(fn)

    def wait(self, eng, tok):
        if tok is None:
            return
        if isinstance(tok, list):
            best = {}
            for t in tok:
                if t is None:
                    continue
                s, v = t
                best[s] = max(best.get(s, 0), v)
            for s, v in best.items():
                self._wait1(eng, s, v)
            return
        self._wait1(eng, tok[0], tok[1])

    def _wait1(self, eng, sem, val):
        if val <= 0:
            return
        h = self.sems[sem]
        self.emit(eng, lambda e, h=h, v=val: e.wait_ge(h, v))

    def op(self, eng, fn, inc=None, k=1):
        if inc is None:
            self.emit(eng, fn)
            return None
        h = self.sems[inc]
        self.cnt[inc] += k
        tgt = self.cnt[inc]
        self.emit(eng, lambda e, fn=fn, h=h, k=k: fn(e).then_inc(h, k))
        return (inc, tgt)

    def dma(self, eng, out, in_, tp=False):
        pool, npool = ("sq", N_SQ) if eng == "sync" else ("gq", N_GQ)
        i = self.slot_rr[eng] % npool
        self.slot_rr[eng] += 1
        iss = "scalar" if eng == "gpsimd" else eng
        sem = f"{pool}{i}"
        prev = self.cnt[sem]
        if prev > 0:
            self._wait1(iss, sem, prev)
        self.cnt[sem] += 16
        tgt = self.cnt[sem]
        h = self.sems[sem]
        if tp:
            self.emit(iss, lambda e, o=out, i_=in_, h=h: e.dma_start_transpose(out=o, in_=i_).then_inc(h, 16))
        else:
            self.emit(iss, lambda e, o=out, i_=in_, h=h: e.dma_start(out=o, in_=i_).then_inc(h, 16))
        return (sem, tgt)

    def barrier(self):
        snap = dict(self.cnt)
        for eng in self.ENGINES:
            for s, v in snap.items():
                if v > 0:
                    self._wait1(eng, s, v)

    def finalize(self, block):
        for eng in self.ENGINES:
            fns = self.q[eng]
            if not fns:
                continue
            getattr(block, eng)(lambda e, fns=fns: [f(e) for f in fns])


def rope_tables():
    try:
        import jax
        import jax.numpy as jnp
        with jax.default_device(jax.devices("cpu")[0]):
            inv_freq = 10000.0 ** (-(jnp.arange(D // 2, dtype=jnp.float32) * 2.0) / D)
            posj = jnp.arange(S, dtype=jnp.float32) + 0
            angj = posj[:, None] * inv_freq[None, :]
            cos = np.asarray(jnp.cos(angj))
            sin = np.asarray(jnp.sin(angj))
    except Exception:
        inv32 = (10000.0 ** (-(np.arange(D // 2, dtype=np.float64) * 2.0) / D)).astype(np.float32)
        pos = np.arange(S, dtype=np.float32)
        a64 = (pos[:, None] * inv32[None, :]).astype(np.float64)
        cos = np.cos(a64).astype(np.float32)
        sin = np.sin(a64).astype(np.float32)
    cc = np.concatenate([cos, cos], axis=1)
    ss = np.concatenate([-sin, sin], axis=1)
    cc = np.ascontiguousarray(cc.reshape(NT, 128, 64).transpose(1, 0, 2).reshape(128, NT * 64))
    ss = np.ascontiguousarray(ss.reshape(NT, 128, 64).transpose(1, 0, 2).reshape(128, NT * 64))
    return cc, ss


def sl3(ap, u=16):
    return ap.rearrange("q (nb u) -> q nb u", u=u)


def bc3(a, nb, u=16):
    return a.rearrange("q (nb o) -> q nb o", o=1).broadcast_to([128, nb, u])


class QuantBufs:
    def __init__(self, sb, tag, W, fp32=True):
        self.W = W
        nb = W // 16
        self.nb = nb
        f = F32 if fp32 else BF16
        self.uab = sb(f"{tag}_uab", [128, W], f)
        self.sgn = sb(f"{tag}_sgn", [128, W], f)
        self.ua = sb(f"{tag}_ua", [128, W], f)
        self.shs = sb(f"{tag}_shs", [128, W], f)
        self.rr = [sb(f"{tag}_rrA", [128, W], f),
                   sb(f"{tag}_rrB", [128, W], f)]
        self.fa = sb(f"{tag}_fa", [128, W], f)
        self.bmax = sb(f"{tag}_bmax", [128, nb], F32)
        self.bm2 = sb(f"{tag}_bm2", [128, nb], F32)
        self.hs = sb(f"{tag}_hs", [128, nb], F32)
        self.rcp = sb(f"{tag}_rcp", [128, nb], F32)
        self.par = 0
        self.g_sc = None
        self.g_rr = [None, None]


def emit_quant(p, qb, src_ap, out_b, W, prev=None, out_war=None):
    nb = W // 16
    par = qb.par
    qb.par ^= 1
    rr = qb.rr[par]
    p.wait("scalar", qb.g_sc)
    p.op("scalar", lambda e: e.activation(out=qb.uab[:, 0:W], in_=src_ap, func=AFT.Abs))
    csg = p.op("scalar", lambda e: e.activation(out=qb.sgn[:, 0:W], in_=src_ap, func=AFT.Sign), inc="sc")
    p.wait("vector", csg)
    c = p.op("vector", lambda e: e.reduce_max(out=qb.bmax[:, 0:nb], in_=sl3(qb.uab[:, 0:W]), axis=AX.X), inc="dv")
    p.wait("vector", c)
    c = p.op("vector", lambda e: e.tensor_scalar_max(out=qb.bm2[:, 0:nb], in0=qb.bmax[:, 0:nb], scalar1=6e-12), inc="dv")
    p.wait("vector", c)
    p.op("vector", lambda e: e.tensor_scalar_mul(out=qb.hs[:, 0:nb], in0=qb.bm2[:, 0:nb], scalar1=1.0 / 12))
    c = p.op("vector", lambda e: e.reciprocal(out=qb.rcp[:, 0:nb], in_=qb.bm2[:, 0:nb]), inc="dv")
    p.wait("vector", c)
    cn = p.op("vector", lambda e: e.tensor_scalar_mul(out=qb.rcp[:, 0:nb], in0=qb.rcp[:, 0:nb], scalar1=6.0), inc="dv")
    p.wait("vector", cn)
    c_ua = p.op("vector", lambda e: e.tensor_tensor(out=sl3(qb.ua[:, 0:W]), in0=sl3(qb.uab[:, 0:W]), in1=bc3(qb.rcp[:, 0:nb], nb), op=ALU.mult), inc="dv")
    p.op("vector", lambda e: e.tensor_single_scalar(out=qb.fa[:, 0:W], in_=qb.ua[:, 0:W], scalar=0.25, op=ALU.is_gt))
    for thr in (0.75, 1.25, 1.75):
        p.op("vector", lambda e, thr=thr: e.scalar_tensor_tensor(out=qb.fa[:, 0:W], in0=qb.ua[:, 0:W], scalar=thr, in1=qb.fa[:, 0:W], op0=ALU.is_gt, op1=ALU.add))
    p.wait("vector", qb.g_rr[par])
    p.op("vector", lambda e: e.tensor_single_scalar(out=rr[:, 0:W], in_=qb.ua[:, 0:W], scalar=2.5, op=ALU.is_gt))
    for thr in (3.5, 5.0, 5.0):
        p.op("vector", lambda e, thr=thr: e.scalar_tensor_tensor(out=rr[:, 0:W], in0=qb.ua[:, 0:W], scalar=thr, in1=rr[:, 0:W], op0=ALU.is_gt, op1=ALU.add))
    cm = p.op("vector", lambda e: e.scalar_tensor_tensor(out=rr[:, 0:W], in0=rr[:, 0:W], scalar=2.0, in1=qb.fa[:, 0:W], op0=ALU.mult, op1=ALU.add), inc="dv")
    p.wait("gpsimd", [csg, cn])
    cshs = p.op("gpsimd", lambda e: e.tensor_tensor(out=sl3(qb.shs[:, 0:W]), in0=sl3(qb.sgn[:, 0:W]), in1=bc3(qb.hs[:, 0:nb], nb), op=ALU.mult), inc="pl")
    p.wait("gpsimd", [cm, out_war])
    cq = p.op("gpsimd", lambda e: e.tensor_tensor(out=out_b, in0=rr[:, 0:W], in1=qb.shs[:, 0:W], op=ALU.mult), inc="pl")
    qb.g_sc = [c_ua, cshs]
    qb.g_rr[par] = cq
    return cq


def build_nc(debug_phase=DEBUG_PHASE, races=False):
    nc = bass.Bass("TRN2", target_bir_lowering=False, detect_race_conditions=races)
    dt = lambda n, s, d, k: nc.dram_tensor(n, s, d, kind=k).ap()
    x_d = dt("x", [S, E], F32, "ExternalInput")
    wq_d = dt("wq", [E, E], F32, "ExternalInput")
    wk_d = dt("wk", [E, E], F32, "ExternalInput")
    wv_d = dt("wv", [E, E], F32, "ExternalInput")
    wo_d = dt("wo", [E, E], F32, "ExternalInput")
    wg1_d = dt("wg1", [E, D], F32, "ExternalInput")
    wg2_d = dt("wg2", [D, E], F32, "ExternalInput")
    wgt_d = dt("wgate", [IFF, E], F32, "ExternalInput")
    wup_d = dt("wup", [IFF, E], F32, "ExternalInput")
    wdn_d = dt("wdown", [E, IFF], F32, "ExternalInput")
    id_d = dt("ident", [128, 128], BF16, "ExternalInput")
    tri_d = dt("triu", [128, 128], BF16, "ExternalInput")
    ct_d = dt("ctab", [128, NT * 64], F32, "ExternalInput")
    st_d = dt("stab", [128, NT * 64], F32, "ExternalInput")
    gam_d = dt("gam", [128, E], F32, "ExternalInput")
    bet_d = dt("bet", [128, E], F32, "ExternalInput")
    out_d = dt("out", [S, E], F32, "ExternalOutput")
    wg_q = dt("wg_q", [IFF, E], BF16, "Internal")
    wu_q = dt("wu_q", [IFF, E], BF16, "Internal")
    wd_q = dt("wd_q", [E, IFF], BF16, "Internal")
    ln2_s = dt("ln2_s", [128, NT * 1024], BF16, "Internal")
    wo_s = dt("wo_s", [E, E], BF16, "Internal")
    attn_s = dt("attn_s", [S, E], BF16, "Internal")
    dbg = {}
    if debug_phase == 2:
        dbg["attn"] = dt("dbg_attn", [128, NT * E], BF16, "ExternalOutput")
    if debug_phase == 3:
        dbg["ln2"] = dt("dbg_ln2", [128, ECH * S], BF16, "ExternalOutput")

    with ExitStack() as st:
        sb = lambda n, s, d: st.enter_context(nc.sbuf_tensor(n, s, d))
        p = Prog(nc, st)

        ident = sb("identt", [128, 128], BF16)
        triu = sb("triut", [128, 128], BF16)
        ccb = sb("ccb", [128, NT * 64], BF16)
        ssb = sb("ssb", [128, NT * 64], BF16)
        gamb = sb("gamb", [128, E], BF16)
        betb = sb("betb", [128, E], BF16)
        xq_fm = sb("xq_fm", [128, ECH * S], BF16)

        with nc.sbuf_tensor("cload", [128, NT * 64], F32) as cload:
            t_id = p.dma("sync", ident[:], id_d[:, :])
            t_tri = p.dma("sync", triu[:], tri_d[:, :])
            cv = None
            for buf, d, w in ((gamb, gam_d, E), (betb, bet_d, E),
                              (ccb, ct_d, NT * 64), (ssb, st_d, NT * 64)):
                p.wait("sync", cv)
                tok = p.dma("sync", cload[:, 0:w], d[:, :])
                p.wait("vector", tok)
                cv = p.op("vector", lambda e, buf=buf, w=w: e.tensor_copy(out=buf[:], in_=cload[:, 0:w]), inc="dv")
            c_consts = cv
            p.wait("sync", c_consts)

        ws_stack = ExitStack()
        ws = WSteps(nc, p, ws_stack, wgt_d, wup_d, wdn_d, wo_d, wg_q, wu_q, wd_q, wo_s)
        with nc.sbuf_tensor("xn_fm", [128, ECH * S], BF16) as xn_fm:
            phase1(nc, p, x_d, ident, xn_fm, xq_fm, t_id, ws=ws)
            phase2(nc, p, wq_d, wk_d, wv_d, wg1_d, wg2_d,
                   ident, triu, ccb, ssb, xn_fm, attn_s, c_consts, t_tri,
                   dbg.get("attn"), ws=ws)
        phase3(nc, p, ident, attn_s, gamb, betb, ln2_s, ws=ws)
        ws.drain()
        ws_stack.close()
        if debug_phase == 3:
            p.barrier()
            p.dma("gpsimd", dbg["ln2"][:, :], ln2_s[:, :])
        if debug_phase >= 4:
            phase4(nc, p, x_d, xq_fm, ln2_s, wg_q, wu_q, wd_q, wo_s, out_d, ident, ws.stores)

        if debug_phase < 4:
            p.barrier()
            with nc.sbuf_tensor("zz", [128, E], F32) as zz:
                cz = p.op("vector", lambda e: e.memset(zz[:], 0.0), inc="dv")
                p.wait("scalar", cz)
                for t in range(NT):
                    p.dma("gpsimd", out_d[t * 128:(t + 1) * 128, :], zz[:])
        p.barrier()

        with nc.Block() as block:
            p.finalize(block)
    return nc


def phase1(nc, p, x_d, ident, xn_fm, xq_fm, t_id, ws=None):
    p.barrier()
    with ExitStack() as s1:
        sb1 = lambda n, s, d: s1.enter_context(nc.sbuf_tensor(n, s, d))
        xr = [sb1(f"xr{i}", [128, E], F32) for i in range(2)]
        stat = sb1("stat", [128, 8], F32)
        xnf = sb1("xnf", [128, E], F32)
        xnb2 = [sb1(f"xnb{i}", [128, E], BF16) for i in range(2)]
        xqb2 = [sb1(f"xqb{i}", [128, E], BF16) for i in range(2)]
        qb = QuantBufs(sb1, "p1", E)
        with nc.psum_tensor("pt1", [128, 1024], BF16) as pt1:
            xr_rel = [None, None]
            tp_n = [None, None]
            tp_q = [None, None]
            evq_rel = None
            pend = {"v": None}

            def flush_p1():
                nonlocal evq_rel
                if pend["v"] is None:
                    return
                t, par, c_nb, c_q = pend["v"]
                pend["v"] = None
                p.wait("tensor", [c_nb, evq_rel, t_id])
                for cch in range(ECH):
                    fn = lambda e, cch=cch, par=par: e.transpose(
                        out=pt1[:, cch * 128:(cch + 1) * 128],
                        in_=xnb2[par][:, cch * 128:(cch + 1) * 128], identity=ident[:])
                    c_tp = p.op("tensor", fn, inc="te" if cch == ECH - 1 else None)
                tp_n[par] = c_tp
                p.wait("vector", c_tp)
                c_evn = p.op("vector", lambda e, t=t: e.tensor_copy(
                    out=xn_fm[:, t * 1024:(t + 1) * 1024], in_=pt1[:]), inc="dv")
                p.wait("tensor", [c_evn, c_q])
                for cch in range(ECH):
                    fn = lambda e, cch=cch, par=par: e.transpose(
                        out=pt1[:, cch * 128:(cch + 1) * 128],
                        in_=xqb2[par][:, cch * 128:(cch + 1) * 128], identity=ident[:])
                    c_tp = p.op("tensor", fn, inc="te" if cch == ECH - 1 else None)
                tp_q[par] = c_tp
                p.wait("vector", c_tp)
                evq_rel = p.op("vector", lambda e, t=t: e.tensor_copy(
                    out=xq_fm[:, t * 1024:(t + 1) * 1024], in_=pt1[:]), inc="dv")

            for t in range(NT):
                par = t % 2
                b = xr[t % 2]
                p.wait("sync", xr_rel[t % 2])
                c_x = p.dma("sync", b[:], x_d[t * 128:(t + 1) * 128, :])
                p.wait("vector", c_x)
                c = p.op("vector", lambda e, b=b: e.reduce_sum(out=stat[:, 0:1], in_=b[:], axis=AX.X), inc="dv")
                p.wait("vector", c)
                c_mu = p.op("vector", lambda e: e.tensor_scalar_mul(out=stat[:, 0:1], in0=stat[:, 0:1], scalar1=1.0 / E), inc="dv")
                p.wait("scalar", c_x)
                p.wait("scalar", qb.g_sc)
                c_sq = p.op("scalar", lambda e, b=b: e.activation(out=qb.uab[:], in_=b[:], func=AFT.Square, accum_out=stat[:, 1:2]), inc="sc")
                p.wait("vector", [c_sq, c_mu])
                p.op("vector", lambda e: e.tensor_scalar_mul(out=stat[:, 2:3], in0=stat[:, 1:2], scalar1=1.0 / E))
                c = p.op("vector", lambda e: e.tensor_tensor(out=stat[:, 3:4], in0=stat[:, 0:1], in1=stat[:, 0:1], op=ALU.mult), inc="dv")
                p.wait("vector", c)
                c = p.op("vector", lambda e: e.tensor_tensor(out=stat[:, 2:3], in0=stat[:, 2:3], in1=stat[:, 3:4], op=ALU.subtract), inc="dv")
                p.wait("vector", c)
                c = p.op("vector", lambda e: e.tensor_scalar_add(out=stat[:, 6:7], in0=stat[:, 2:3], scalar1=1e-5), inc="dv")
                p.wait("scalar", c)
                c_sqrt = p.op("scalar", lambda e: e.activation(out=stat[:, 4:5], in_=stat[:, 6:7], func=AFT.Sqrt), inc="sc")
                p.wait("vector", c_sqrt)
                c_rs = p.op("vector", lambda e: e.reciprocal(out=stat[:, 3:4], in_=stat[:, 4:5]), inc="dv")
                p.wait("vector", c_rs)
                c_xn = p.op("vector", lambda e, b=b: e.tensor_scalar(
                    out=xnf[:], in0=b[:], scalar1=stat[:, 0:1], scalar2=stat[:, 3:4],
                    op0=ALU.subtract, op1=ALU.mult), inc="dv")
                xr_rel[t % 2] = c_xn
                p.wait("scalar", [c_xn, tp_n[par]])
                c_nb = p.op("scalar", lambda e, par=par: e.activation(out=xnb2[par][:], in_=xnf[:], func=AFT.Identity), inc="sc")
                c_q = emit_quant(p, qb, xnf[:], xqb2[par][:, 0:E], E, out_war=tp_q[par])
                prev = pend["v"]
                pend["v"] = None
                if prev is not None:
                    pend["v"], prev2 = prev, None
                    flush_p1()
                pend["v"] = (t, par, c_nb, c_q)
                if ws is not None:
                    ws.pump(1)
            flush_p1()


class WSteps:

    def __init__(self, nc, p, stack, wgt_d, wup_d, wdn_d, wo_d, wg_q, wu_q, wd_q, wo_s):
        sbw = lambda n, s, d: stack.enter_context(nc.sbuf_tensor(n, s, d))
        self.p = p
        self.raw = [sbw(f"wraw{i}", [128, 1024], F32) for i in range(2)]
        self.wqb = [sbw(f"wqbo{i}", [128, 1024], BF16) for i in range(2)]
        self.qb = QuantBufs(sbw, "pw", 1024, fp32=True)
        self.raw_rel = [None, None]
        self.store_war = [None, None]
        self.stores = []
        self.steps = []
        self.k = 0
        self.i = 0

        def quant_step(w_d, dst, r0, c0, w):
            def do():
                p = self.p
                li = self.k
                self.k += 1
                b = self.raw[li % 2]
                ob = self.wqb[li % 2]
                p.wait("sync", self.raw_rel[li % 2])
                c_w = p.dma("sync", b[:, 0:w], w_d[r0:r0 + 128, c0:c0 + w])
                p.wait("scalar", c_w)
                cq = emit_quant(p, self.qb, b[:, 0:w], ob[:, 0:w], w, out_war=self.store_war[li % 2])
                self.raw_rel[li % 2] = cq
                p.wait("scalar", cq)
                tok = p.dma("gpsimd", dst[r0:r0 + 128, c0:c0 + w], ob[:, 0:w])
                self.store_war[li % 2] = tok
                self.stores.append(tok)
            return do

        def wo_step(r):
            def do():
                p = self.p
                li = self.k
                self.k += 1
                b = self.raw[li % 2]
                ob = self.wqb[li % 2]
                p.wait("sync", self.raw_rel[li % 2])
                c_w = p.dma("sync", b[:, 0:1024], wo_d[r * 128:(r + 1) * 128, :])
                p.wait("scalar", [c_w, self.store_war[li % 2]])
                cc = p.op("scalar", lambda e, b=b, ob=ob: e.activation(out=ob[:, 0:1024], in_=b[:, 0:1024], func=AFT.Identity), inc="sc")
                self.raw_rel[li % 2] = cc
                p.wait("scalar", cc)
                tok = p.dma("gpsimd", wo_s[r * 128:(r + 1) * 128, :], ob[:, 0:1024])
                self.store_war[li % 2] = tok
                self.stores.append(tok)
            return do

        for w_d, dst in ((wgt_d, wg_q), (wup_d, wu_q)):
            for r in range(ICH):
                self.steps.append(quant_step(w_d, dst, r * 128, 0, E))
        for r in range(ECH):
            for c0, w in ((0, 1024), (1024, 1024), (2048, 768)):
                self.steps.append(quant_step(wdn_d, wd_q, r * 128, c0, w))
        for r in range(ECH):
            self.steps.append(wo_step(r))

    def pump(self, k):
        while k > 0 and self.i < len(self.steps):
            self.steps[self.i]()
            self.i += 1
            k -= 1

    def drain(self):
        self.pump(len(self.steps))


def phase2(nc, p, wq_d, wk_d, wv_d, wg1_d, wg2_d, ident, triu, ccb, ssb, xn_fm, attn_s, c_consts, t_tri, dbg_attn=None, ws=None):
    p.barrier()
    with ExitStack() as s2:
        sb2 = lambda n, s, d: s2.enter_context(nc.sbuf_tensor(n, s, d))
        wq_r = sb2("wq_r", [128, ECH * 1024], BF16)
        wk_r = sb2("wk_r", [128, ECH * 1024], BF16)
        wv_r = sb2("wv_r", [128, ECH * 1024], BF16)
        wg1_r = sb2("wg1_r", [128, ECH * 64], BF16)
        wg2_r = sb2("wg2_r", [64, 1024], BF16)
        gt_b = sb2("gt_b", [64, S], BF16)
        qs = sb2("qs", [128, 1024], BF16)
        ks = sb2("ks", [128, 1024], BF16)
        vs = sb2("vs", [128, 1024], BF16)
        gs = sb2("gs", [128, 1024], BF16)
        qro = sb2("qro", [128, 1024], BF16)
        kro = sb2("kro", [128, 1024], BF16)
        qzA = sb2("qzA", [128, 1024], BF16)
        qzB = sb2("qzB", [128, 1024], BF16)
        kzA = sb2("kzA", [128, 1024], BF16)
        kzB = sb2("kzB", [128, 1024], BF16)
        em = sb2("em", [128, 1024], BF16)
        S_sb = sb2("S_sb", [128, 512], BF16)
        ab = [sb2(f"ab{i}", [128, 1024], BF16) for i in range(2)]

        raw_scope = ExitStack()
        raw = [raw_scope.enter_context(nc.sbuf_tensor(f"w2raw{i}", [128, 1024], F32)) for i in range(2)]
        raw_rel = [None, None]
        i = 0
        for w_d, w_r in ((wq_d, wq_r), (wk_d, wk_r), (wv_d, wv_r)):
            for c in range(ECH):
                b = raw[i % 2]
                p.wait("sync", raw_rel[i % 2])
                c_w = p.dma("sync", b[:], w_d[c * 128:(c + 1) * 128, :])
                p.wait("vector", c_w)
                c_c = p.op("vector", lambda e, b=b, w_r=w_r, c=c: e.tensor_copy(out=w_r[:, c * 1024:(c + 1) * 1024], in_=b[:]), inc="dv")
                raw_rel[i % 2] = c_c
                i += 1
        for c in range(ECH):
            b = raw[i % 2]
            p.wait("sync", raw_rel[i % 2])
            c_w = p.dma("sync", b[:, 0:D], wg1_d[c * 128:(c + 1) * 128, :])
            p.wait("vector", c_w)
            c_c = p.op("vector", lambda e, b=b, c=c: e.tensor_copy(out=wg1_r[:, c * 64:(c + 1) * 64], in_=b[:, 0:D]), inc="dv")
            raw_rel[i % 2] = c_c
            i += 1
        b = raw[i % 2]
        p.wait("sync", raw_rel[i % 2])
        c_w = p.dma("sync", b[0:64, 0:E], wg2_d[:, :])
        p.wait("vector", c_w)
        c_wcast = p.op("vector", lambda e, b=b: e.tensor_copy(out=wg2_r[:, :], in_=b[0:64, 0:E]), inc="dv")
        c_szero = p.op("vector", lambda e: e.memset(S_sb[:], 0.0), inc="dv")
        p.op("vector", lambda e: e.memset(qzA[64:128, :], 0.0))
        p.op("vector", lambda e: e.memset(qzB[0:64, :], 0.0))
        p.op("vector", lambda e: e.memset(kzA[64:128, :], 0.0))
        c_zpad = p.op("vector", lambda e: e.memset(kzB[0:64, :], 0.0), inc="dv")
        p.barrier()
        raw_scope.close()
        if ws is not None:
            ws.pump(4)
        if P2_LEVEL <= 1:
            return

        with (nc.psum_tensor("pa", [128, 1024], F32) as pa,
              nc.psum_tensor("pb", [128, 1024], F32) as pb,
              nc.psum_tensor("po", [128, 1024], F32) as po,
              nc.psum_tensor("pss", [128, 512], F32) as pss,
              nc.psum_tensor("pt2", [128, 1024], BF16) as pt2):

            p.wait("tensor", [c_wcast, c_consts])
            c_gt = None
            for th in (range(4) if "gt" not in P2_SKIP else []):
                p.wait("tensor", c_gt)
                for tt in range(4):
                    t = th * 4 + tt
                    for c in range(ECH):
                        fn = lambda e, t=t, c=c, tt=tt: e.matmul(
                            out=pa[0:64, tt * 128:(tt + 1) * 128],
                            lhsT=wg1_r[:, c * 64:(c + 1) * 64],
                            rhs=xn_fm[:, t * 1024 + c * 128:t * 1024 + (c + 1) * 128],
                            start=(c == 0), stop=(c == ECH - 1))
                        c_mm = p.op("tensor", fn, inc="te" if (c == ECH - 1 and tt == 3) else None)
                p.wait("scalar", c_mm)
                c_gt = p.op("scalar", lambda e, th=th: e.activation(out=gt_b[:, th * 512:(th + 1) * 512], in_=pa[0:64, 0:512], func=AFT.Identity), inc="sc")
            p.wait("tensor", c_gt)
            if P2_LEVEL <= 2:
                return

            po_rel = None
            pt2_rel = None
            ab_rel = [None, None]
            pa_rel = None
            pb_rel = None
            c_qro_prev = None
            c_kro_prev = None
            c_smm_prev = None
            s_evac_prev = c_szero
            for n in range(P2_CHUNKS):
                p.wait("tensor", pa_rel)
                for c in (range(ECH) if "qkv" not in P2_SKIP else []):
                    for hcol in range(2):
                        fn = lambda e, n=n, c=c, hcol=hcol: e.matmul(
                            out=pa[:, hcol * 512:(hcol + 1) * 512],
                            lhsT=xn_fm[:, n * 1024 + c * 128:n * 1024 + (c + 1) * 128],
                            rhs=wq_r[:, c * 1024 + hcol * 512:c * 1024 + (hcol + 1) * 512],
                            start=(c == 0), stop=(c == ECH - 1))
                        c_qmm = p.op("tensor", fn, inc="te" if (c == ECH - 1 and hcol == 1) else None)
                if "qkv" not in P2_SKIP:
                    p.wait("scalar", [c_qmm, c_qro_prev])
                    c_qs = p.op("scalar", lambda e: e.activation(out=qs[:], in_=pa[:], func=AFT.Silu), inc="sc")
                else:
                    c_qs = None
                p.wait("tensor", pb_rel)
                for c in (range(ECH) if "qkv" not in P2_SKIP else []):
                    for hcol in range(2):
                        fn = lambda e, n=n, c=c, hcol=hcol: e.matmul(
                            out=pb[:, hcol * 512:(hcol + 1) * 512],
                            lhsT=xn_fm[:, n * 1024 + c * 128:n * 1024 + (c + 1) * 128],
                            rhs=wk_r[:, c * 1024 + hcol * 512:c * 1024 + (hcol + 1) * 512],
                            start=(c == 0), stop=(c == ECH - 1))
                        c_kmm = p.op("tensor", fn, inc="te" if (c == ECH - 1 and hcol == 1) else None)
                if "qkv" not in P2_SKIP:
                    p.wait("scalar", [c_kmm, c_kro_prev])
                    c_ks = p.op("scalar", lambda e: e.activation(out=ks[:], in_=pb[:], func=AFT.Silu), inc="sc")
                else:
                    c_ks = None
                p.wait("tensor", c_qs)
                for c in (range(ECH) if "qkv" not in P2_SKIP else []):
                    for hcol in range(2):
                        fn = lambda e, n=n, c=c, hcol=hcol: e.matmul(
                            out=pa[:, hcol * 512:(hcol + 1) * 512],
                            lhsT=xn_fm[:, n * 1024 + c * 128:n * 1024 + (c + 1) * 128],
                            rhs=wv_r[:, c * 1024 + hcol * 512:c * 1024 + (hcol + 1) * 512],
                            start=(c == 0), stop=(c == ECH - 1))
                        c_vmm = p.op("tensor", fn, inc="te" if (c == ECH - 1 and hcol == 1) else None)
                if "qkv" not in P2_SKIP:
                    p.wait("scalar", [c_vmm, c_smm_prev])
                    c_vs = p.op("scalar", lambda e: e.activation(out=vs[:], in_=pa[:], func=AFT.Identity), inc="sc")
                else:
                    c_vs = None
                p.wait("tensor", c_ks)
                for hcol in (range(2) if "gate" not in P2_SKIP else []):
                    fn = lambda e, n=n, hcol=hcol: e.matmul(
                        out=pb[:, hcol * 512:(hcol + 1) * 512],
                        lhsT=gt_b[:, n * 128:(n + 1) * 128],
                        rhs=wg2_r[:, hcol * 512:(hcol + 1) * 512],
                        start=True, stop=True)
                    c_gmm = p.op("tensor", fn, inc="te" if hcol == 1 else None)
                if "gate" not in P2_SKIP:
                    p.wait("scalar", [c_gmm, po_rel])
                    c_gs = p.op("scalar", lambda e: e.activation(out=gs[:], in_=pb[:], func=AFT.Sigmoid), inc="sc")
                else:
                    c_gs = None

                def rope(src, dst, c_src, n=n):
                    p.wait("vector", c_src)
                    s3 = lambda a: a[:].rearrange("q (h j) -> q h j", j=64)
                    cc_t = ccb[:, n * 64:(n + 1) * 64].rearrange("q (o j) -> q o j", o=1).broadcast_to([128, 16, 64])
                    ssA = ssb[:, n * 64:n * 64 + 32].rearrange("q (o j) -> q o j", o=1).broadcast_to([128, 16, 32]).rearrange("q h (o j) -> q h o j", o=1)
                    ssB = ssb[:, n * 64 + 32:n * 64 + 64].rearrange("q (o j) -> q o j", o=1).broadcast_to([128, 16, 32]).rearrange("q h (o j) -> q h o j", o=1)
                    src4 = src[:].rearrange("q (h k j) -> q h k j", k=2, j=32)
                    tmp4 = em[:, 0:1024].rearrange("q (h k j) -> q h k j", k=2, j=32)
                    dst4 = dst[:].rearrange("q (h k j) -> q h k j", k=2, j=32)
                    p.op("vector", lambda e: e.tensor_tensor(out=s3(em[:, 0:1024]), in0=s3(src), in1=cc_t, op=ALU.mult))
                    p.op("vector", lambda e: e.tensor_tensor(out=dst4[:, :, 0:1, :], in0=src4[:, :, 1:2, :], in1=ssA, op=ALU.mult))
                    p.op("vector", lambda e: e.tensor_tensor(out=dst4[:, :, 0:1, :], in0=dst4[:, :, 0:1, :], in1=tmp4[:, :, 0:1, :], op=ALU.add))
                    p.op("vector", lambda e: e.tensor_tensor(out=dst4[:, :, 1:2, :], in0=src4[:, :, 0:1, :], in1=ssB, op=ALU.mult))
                    return p.op("vector", lambda e: e.tensor_tensor(out=dst4[:, :, 1:2, :], in0=dst4[:, :, 1:2, :], in1=tmp4[:, :, 1:2, :], op=ALU.add), inc="dv")
                if "rope" not in P2_SKIP:
                    c_qro = rope(qs, qro, c_qs)
                    c_qro_prev = c_qro
                    p.wait("vector", c_smm_prev)
                    c_kro = rope(ks, kro, c_ks)
                    c_kro_prev = c_kro
                else:
                    c_qro = c_kro = None

                p.wait("tensor", [c_qro, pt2_rel])
                for cch in (range(ECH) if "tp" not in P2_SKIP else []):
                    fn = lambda e, cch=cch: e.transpose(
                        out=pt2[:, cch * 128:(cch + 1) * 128],
                        in_=qro[:, cch * 128:(cch + 1) * 128], identity=ident[:])
                    c_tp = p.op("tensor", fn, inc="te" if cch == ECH - 1 else None)
                if "tp" not in P2_SKIP:
                    p.wait("vector", c_tp)
                    p.op("vector", lambda e: e.tensor_copy(out=qzA[0:64, :], in_=pt2[0:64, :]))
                    c_qf = p.op("vector", lambda e: e.tensor_copy(out=qzB[64:128, :], in_=pt2[64:128, :]), inc="dv")
                else:
                    c_qf = None
                p.wait("tensor", [c_kro, c_qf])
                for cch in (range(ECH) if "tp" not in P2_SKIP else []):
                    fn = lambda e, cch=cch: e.transpose(
                        out=pt2[:, cch * 128:(cch + 1) * 128],
                        in_=kro[:, cch * 128:(cch + 1) * 128], identity=ident[:])
                    c_tp = p.op("tensor", fn, inc="te" if cch == ECH - 1 else None)
                if "tp" not in P2_SKIP:
                    p.wait("vector", c_tp)
                    p.op("vector", lambda e: e.tensor_copy(out=kzA[0:64, :], in_=pt2[0:64, :]))
                    c_kf = p.op("vector", lambda e: e.tensor_copy(out=kzB[64:128, :], in_=pt2[64:128, :]), inc="dv")
                    pt2_rel = c_kf
                else:
                    c_kf = None

                if P2_LEVEL <= 3:
                    pt2_rel = c_kf
                    pa_rel = c_vs
                    pb_rel = c_gs
                    c_qro_prev = c_qro
                    c_kro_prev = c_kro
                    continue
                p.wait("tensor", c_kf)
                c_e = [None, None]
                for half, pXX in ((0, pa), (1, pb)):
                    p.wait("tensor", c_vs if half == 0 else c_gs)
                    for hl in range(8):
                        h = half * 8 + hl
                        kz = kzA if h % 2 == 0 else kzB
                        qz = qzA if h % 2 == 0 else qzB
                        cc2 = h // 2
                        fn = lambda e, kz=kz, qz=qz, cc2=cc2, hl=hl, pXX=pXX: e.matmul(
                            out=pXX[:, hl * 128:(hl + 1) * 128],
                            lhsT=kz[:, cc2 * 128:(cc2 + 1) * 128],
                            rhs=qz[:, cc2 * 128:(cc2 + 1) * 128],
                            start=True, stop=True)
                        c_e[half] = p.op("tensor", fn, inc="te" if hl == 7 else None)
                tri8 = triu[:, :].rearrange("q (o j) -> q o j", o=1).broadcast_to([128, 8, 128])
                if P2_LEVEL <= 4 or P2_LEVEL in (35, 36, 41, 42, 43, 44):
                    if P2_LEVEL == 41: hsel = [0]
                    elif P2_LEVEL == 42: hsel = [1]
                    elif P2_LEVEL == 43: hsel = [0, 1]
                    elif P2_LEVEL == 44: hsel = [0, 2, 4, 6]
                    else: hsel = list(range(8))
                    for half, pXX in ((0, pa), (1, pb)):
                        p.wait("tensor", c_vs if half == 0 else c_gs)
                        for hl in hsel:
                            h = half * 8 + hl
                            kz = kzA if h % 2 == 0 else kzB
                            qz = qzA if h % 2 == 0 else qzB
                            cc2 = h // 2
                            fn = lambda e, kz=kz, qz=qz, cc2=cc2, hl=hl, pXX=pXX: e.matmul(
                                out=pXX[:, hl * 128:(hl + 1) * 128],
                                lhsT=kz[:, cc2 * 128:(cc2 + 1) * 128],
                                rhs=qz[:, cc2 * 128:(cc2 + 1) * 128],
                                start=True, stop=True)
                            c_eh = p.op("tensor", fn, inc="te" if hl == hsel[-1] else None)
                        p.wait("vector", [c_eh, t_tri])
                        if P2_LEVEL in (35, 41, 42, 43, 44):
                            c_em = p.op("vector", lambda e, pXX=pXX: e.tensor_copy(out=em[:], in_=pXX[:]), inc="dv")
                        elif P2_LEVEL == 36:
                            c_em = p.op("vector", lambda e, pXX=pXX: e.tensor_tensor(
                                out=em[:].rearrange("q (hh j) -> q hh j", j=128), in0=pXX[:].rearrange("q (hh j) -> q hh j", j=128), in1=tri8, op=ALU.mult), inc="dv")
                        else:
                            c_em = p.op("vector", lambda e, pXX=pXX: e.tensor_tensor(
                                out=em[:].rearrange("q (hh j) -> q hh j", j=128), in0=pXX[:].rearrange("q (hh j) -> q hh j", j=128), in1=tri8, op=ALU.mult), inc="dv")
                        if half == 0:
                            pa_rel = c_em
                        else:
                            pb_rel = c_em
                    pt2_rel = c_kf
                    c_qro_prev = c_qro
                    c_kro_prev = c_kro
                    continue
                c_omm = None
                for half, pXX in ((0, pa), (1, pb)):
                    p.wait("vector", [c_e[half], t_tri, c_omm])
                    c_em = p.op("vector", lambda e, pXX=pXX: e.tensor_tensor(
                        out=em[:].rearrange("q (hh j) -> q hh j", j=128), in0=pXX[:].rearrange("q (hh j) -> q hh j", j=128), in1=tri8, op=ALU.mult), inc="dv")
                    if half == 0:
                        pa_rel = c_em
                    else:
                        pb_rel = c_em
                    p.wait("tensor", [c_em, po_rel, s_evac_prev])
                    for hl in range(8):
                        h = half * 8 + hl
                        bp = 64 * (h % 2)
                        cc2 = h // 2
                        fn1 = lambda e, h=h, hl=hl: e.matmul(
                            out=po[:, h * 64:(h + 1) * 64],
                            lhsT=em[:, hl * 128:(hl + 1) * 128],
                            rhs=vs[:, h * 64:(h + 1) * 64],
                            start=True, stop=False)
                        p.op("tensor", fn1)
                        qz = qzA if h % 2 == 0 else qzB
                        fn2 = lambda e, h=h, qz=qz, cc2=cc2: e.matmul(
                            out=po[:, h * 64:(h + 1) * 64],
                            lhsT=qz[:, cc2 * 128:(cc2 + 1) * 128],
                            rhs=S_sb[:, cc2 * 64:(cc2 + 1) * 64],
                            start=False, stop=True)
                        c_omm = p.op("tensor", fn2, inc="te" if hl == 7 else None)

                for h in range(H):
                    bp = 64 * (h % 2)
                    fn = lambda e, h=h, bp=bp, n=n: e.matmul(
                        out=pss[bp:bp + 64, (h // 2) * 64:(h // 2 + 1) * 64],
                        lhsT=kro[:, h * 64:(h + 1) * 64],
                        rhs=vs[:, h * 64:(h + 1) * 64],
                        start=(n == 0 and h < 2), stop=(n == NT - 1 and h >= H - 2),
                        tile_position=(0, bp), skip_group_check=True)
                    c_smm = p.op("tensor", fn, inc="te" if h == H - 1 else None)
                c_smm_prev = c_smm
                p.wait("scalar", c_smm)
                s_evac_prev = p.op("scalar", lambda e: e.activation(out=S_sb[:], in_=pss[:], func=AFT.Identity), inc="sc")

                p.wait("vector", [c_omm, c_gs, ab_rel[n % 2]])
                po_rel = p.op("vector", lambda e, n=n: e.tensor_tensor(
                    out=ab[n % 2][:], in0=po[:], in1=gs[:], op=ALU.mult), inc="dv")
                p.wait("scalar", po_rel)
                ab_rel[n % 2] = p.dma("gpsimd", attn_s[n * 128:(n + 1) * 128, :], ab[n % 2][:])
                if dbg_attn is not None:
                    ab_rel[n % 2] = p.dma("gpsimd", dbg_attn[:, n * 1024:(n + 1) * 1024], ab[n % 2][:])
                if ws is not None:
                    ws.pump(3)


def phase3(nc, p, ident, attn_s, gamb, betb, ln2_s, ws=None):
    p.barrier()
    with ExitStack() as s3:
        sb3 = lambda n, s, d: s3.enter_context(nc.sbuf_tensor(n, s, d))
        stat = sb3("stat3", [128, 8], F32)
        n1 = sb3("n1", [128, E], BF16)
        sq = sb3("sq3", [128, E], F32)
        ln2b = sb3("ln2b", [128, E], BF16)
        lnst = [sb3(f"lnst{i}", [128, E], BF16) for i in range(2)]
        abuf = [sb3(f"a3_{i}", [128, E], BF16) for i in range(2)]
        with nc.psum_tensor("pt3", [128, 1024], BF16) as pt3:
            ev_rel = None
            n1_rel = None
            a_rel = [None, None]
            ln_rel = [None, None]
            for t in range(NT):
                p.wait("sync", a_rel[t % 2])
                c_a = p.dma("sync", abuf[t % 2][:], attn_s[t * 128:(t + 1) * 128, :])
                a = abuf[t % 2][:]
                p.wait("vector", c_a)
                p.wait("scalar", c_a)
                c = p.op("vector", lambda e, a=a: e.reduce_sum(out=stat[:, 0:1], in_=a, axis=AX.X), inc="dv")
                p.wait("vector", c)
                c_mu = p.op("vector", lambda e: e.tensor_scalar_mul(out=stat[:, 0:1], in0=stat[:, 0:1], scalar1=1.0 / E), inc="dv")
                c_sq = p.op("scalar", lambda e, a=a: e.activation(out=sq[:], in_=a, func=AFT.Square, accum_out=stat[:, 1:2]), inc="sc")
                p.wait("vector", [c_sq, c_mu])
                p.op("vector", lambda e: e.tensor_scalar_mul(out=stat[:, 2:3], in0=stat[:, 1:2], scalar1=1.0 / E))
                c = p.op("vector", lambda e: e.tensor_tensor(out=stat[:, 3:4], in0=stat[:, 0:1], in1=stat[:, 0:1], op=ALU.mult), inc="dv")
                p.wait("vector", c)
                c = p.op("vector", lambda e: e.tensor_tensor(out=stat[:, 2:3], in0=stat[:, 2:3], in1=stat[:, 3:4], op=ALU.subtract), inc="dv")
                p.wait("vector", c)
                c = p.op("vector", lambda e: e.tensor_scalar_add(out=stat[:, 6:7], in0=stat[:, 2:3], scalar1=1e-5), inc="dv")
                p.wait("scalar", c)
                c_sqrt = p.op("scalar", lambda e: e.activation(out=stat[:, 4:5], in_=stat[:, 6:7], func=AFT.Sqrt), inc="sc")
                p.wait("vector", c_sqrt)
                c_rs = p.op("vector", lambda e: e.reciprocal(out=stat[:, 3:4], in_=stat[:, 4:5]), inc="dv")
                p.wait("vector", [c_rs, n1_rel])
                c_n1 = p.op("vector", lambda e, a=a: e.tensor_scalar(
                    out=n1[:], in0=a, scalar1=stat[:, 0:1], scalar2=stat[:, 3:4],
                    op0=ALU.subtract, op1=ALU.mult), inc="dv")
                a_rel[t % 2] = c_n1
                p.wait("vector", c_n1)
                p.op("vector", lambda e: e.tensor_tensor(out=ln2b[:], in0=n1[:], in1=gamb[:], op=ALU.mult))
                c_l2 = p.op("vector", lambda e: e.tensor_tensor(out=ln2b[:], in0=ln2b[:], in1=betb[:], op=ALU.add), inc="dv")
                n1_rel = c_l2
                p.wait("tensor", [c_l2, ev_rel])
                for cch in range(ECH):
                    fn = lambda e, cch=cch: e.transpose(
                        out=pt3[:, cch * 128:(cch + 1) * 128],
                        in_=ln2b[:, cch * 128:(cch + 1) * 128], identity=ident[:])
                    c_tp = p.op("tensor", fn, inc="te" if cch == ECH - 1 else None)
                p.wait("vector", [c_tp, ln_rel[t % 2]])
                ev_rel = p.op("vector", lambda e, t=t: e.tensor_copy(out=lnst[t % 2][:], in_=pt3[:]), inc="dv")
                p.wait("scalar", ev_rel)
                ln_rel[t % 2] = p.dma("gpsimd", ln2_s[:, t * 1024:(t + 1) * 1024], lnst[t % 2][:])
                if ws is not None:
                    ws.pump(1)


def phase4(nc, p, x_d, xq_fm, ln2_s, wg_q, wu_q, wd_q, wo_s, out_d, ident, w_fence):
    p.barrier()
    with ExitStack() as s4:
        sb4 = lambda n, s, d: s4.enter_context(nc.sbuf_tensor(n, s, d))
        ffn_acc = sb4("ffn_acc", [128, 8 * 1024], F32)
        wgs = [sb4(f"wgs{i}", [128, ECH * 1024], BF16) for i in range(2)]
        wus = [sb4(f"wus{i}", [128, ECH * 1024], BF16) for i in range(2)]
        wds = sb4("wds", [128, 8 * 1024], BF16)
        wos = wds
        sg = sb4("sg", [128, 1024], BF16)
        ub = sb4("ub", [128, 1024], BF16)
        hb = sb4("hb", [128, 1024], BF16)
        hqb = [sb4(f"hqb{i}", [128, 1024], BF16) for i in range(2)]
        hqf = [sb4(f"hqf{i}", [128, 1024], BF16) for i in range(2)]
        qb = QuantBufs(sb4, "p4", 1024, fp32=True)
        xrt = sb4("xrt", [128, E], F32)
        outb = sb4("outb", [128, E], F32)
        lnt = [sb4(f"lnt{i}", [128, E], BF16) for i in range(2)]
        with (nc.psum_tensor("pg", [128, 1024], F32) as pg,
              nc.psum_tensor("pu", [128, 1024], F32) as pu,
              nc.psum_tensor("pf", [128, 1024], F32) as pf,
              nc.psum_tensor("pt4", [128, 1024], BF16) as pt4):
            p.wait("sync", w_fence)
            w_rel = None
            c_wo_prev = None
            pt4_rel = None
            hq_rel = None
            c_ub_prev = None
            tp_tok = [None, None]
            dn_war = [None, None]
            pf_rel = None
            x_rel = None
            out_rel = None
            ln_war = [None, None]
            gu_war = [None, None]

            def issue_gu(k):
                grp_, ip_ = divmod(k, 3)
                W_ = WIPW[ip_]
                par = k % 2
                p.wait("sync", gu_war[par])
                toks = []
                for c in range(ECH):
                    toks.append(p.dma("sync", wgs[par][:, c * 1024:c * 1024 + W_], wg_q[ip_ * 1024:ip_ * 1024 + W_, c * 128:(c + 1) * 128], tp=True))
                    toks.append(p.dma("sync", wus[par][:, c * 1024:c * 1024 + W_], wu_q[ip_ * 1024:ip_ * 1024 + W_, c * 128:(c + 1) * 128], tp=True))
                return toks

            gu_toks = issue_gu(0)
            for grp in range(2):
                for ip in range(3):
                    W = WIPW[ip]
                    NIC = WICH[ip]
                    k = grp * 3 + ip
                    par = k % 2
                    p.wait("sync", [w_rel, c_wo_prev])
                    ldd = []
                    for icl in range(NIC):
                        ic = ip * 8 + icl
                        ldd.append(p.dma("sync", wds[:, icl * 1024:(icl + 1) * 1024], wd_q[:, ic * 128:(ic + 1) * 128], tp=True))
                    nxt = issue_gu(k + 1) if k + 1 < 6 else []
                    p.wait("tensor", gu_toks)
                    gu_toks = nxt
                    prev4 = None
                    for tl in range(9):
                        if tl < 8:
                            t = grp * 8 + tl
                            hpar = tl % 2
                            p.wait("tensor", c_ub_prev)
                            for c in range(ECH):
                                for hcol in range((W + 511) // 512):
                                    lo, hi = hcol * 512, min(W, (hcol + 1) * 512)
                                    fn = lambda e, t=t, c=c, lo=lo, hi=hi, par=par: e.matmul(
                                        out=pg[:, lo:hi],
                                        lhsT=xq_fm[:, t * 1024 + c * 128:t * 1024 + (c + 1) * 128],
                                        rhs=wgs[par][:, c * 1024 + lo:c * 1024 + hi],
                                        start=(c == 0), stop=(c == ECH - 1))
                                    p.op("tensor", fn)
                                    fn = lambda e, t=t, c=c, lo=lo, hi=hi, par=par: e.matmul(
                                        out=pu[:, lo:hi],
                                        lhsT=xq_fm[:, t * 1024 + c * 128:t * 1024 + (c + 1) * 128],
                                        rhs=wus[par][:, c * 1024 + lo:c * 1024 + hi],
                                        start=(c == 0), stop=(c == ECH - 1))
                                    c_gu = p.op("tensor", fn, inc="te" if (c == ECH - 1 and hi == W) else None)
                            p.wait("scalar", [c_gu, hq_rel])
                            p.op("scalar", lambda e, W=W: e.activation(out=sg[:, 0:W], in_=pg[:, 0:W], func=AFT.Silu))
                            c_ub = p.op("scalar", lambda e, W=W: e.activation(out=ub[:, 0:W], in_=pu[:, 0:W], func=AFT.Identity), inc="sc")
                            c_ub_prev = c_ub
                            p.wait("vector", c_ub)
                            c_h = p.op("vector", lambda e, W=W: e.tensor_tensor(out=hb[:, 0:W], in0=sg[:, 0:W], in1=ub[:, 0:W], op=ALU.mult), inc="dv")
                            p.wait("scalar", c_h)
                            c_hq = emit_quant(p, qb, hb[:, 0:W], hqb[hpar][:, 0:W], W, out_war=tp_tok[hpar])
                            hq_rel = c_hq
                        if prev4 is not None:
                            ptl, p_hq = prev4
                            ppar = ptl % 2
                            p.wait("tensor", [p_hq, pt4_rel])
                            for icl in range(NIC):
                                fn = lambda e, icl=icl, ppar=ppar: e.transpose(
                                    out=pt4[:, icl * 128:(icl + 1) * 128],
                                    in_=hqb[ppar][:, icl * 128:(icl + 1) * 128], identity=ident[:])
                                c_tp = p.op("tensor", fn, inc="te" if icl == NIC - 1 else None)
                            tp_tok[ppar] = c_tp
                            p.wait("vector", [c_tp, dn_war[ppar]])
                            c_hqf = p.op("vector", lambda e, NIC=NIC, ppar=ppar: e.tensor_copy(out=hqf[ppar][:, 0:NIC * 128], in_=pt4[:, 0:NIC * 128]), inc="dv")
                            pt4_rel = c_hqf
                            p.wait("tensor", [c_hqf, pf_rel] + (ldd if ptl == 0 else []))
                            for icl in range(NIC):
                                for hcol in range(2):
                                    fn = lambda e, icl=icl, hcol=hcol, ppar=ppar, NIC=NIC: e.matmul(
                                        out=pf[:, hcol * 512:(hcol + 1) * 512],
                                        lhsT=hqf[ppar][:, icl * 128:(icl + 1) * 128],
                                        rhs=wds[:, icl * 1024 + hcol * 512:icl * 1024 + (hcol + 1) * 512],
                                        start=(icl == 0), stop=(icl == NIC - 1))
                                    c_dn = p.op("tensor", fn, inc="te" if (icl == NIC - 1 and hcol == 1) else None)
                            dn_war[ppar] = c_dn
                            p.wait("vector", c_dn)
                            if ip == 0:
                                pf_rel = p.op("vector", lambda e, ptl=ptl: e.tensor_copy(out=ffn_acc[:, ptl * 1024:(ptl + 1) * 1024], in_=pf[:]), inc="dv")
                            else:
                                pf_rel = p.op("vector", lambda e, ptl=ptl: e.tensor_tensor(out=ffn_acc[:, ptl * 1024:(ptl + 1) * 1024], in0=ffn_acc[:, ptl * 1024:(ptl + 1) * 1024], in1=pf[:], op=ALU.add), inc="dv")
                        prev4 = (tl, c_hq) if tl < 8 else None
                    w_rel = c_dn
                    gu_war[par] = c_gu
                p.wait("sync", w_rel)
                ldo = []
                for c in range(ECH):
                    ldo.append(p.dma("sync", wos[:, c * 1024:(c + 1) * 1024], wo_s[c * 128:(c + 1) * 128, :]))
                p.wait("tensor", ldo)
                for tl in range(8):
                    t = grp * 8 + tl
                    p.wait("sync", [x_rel, ln_war[tl % 2]])
                    c_x = p.dma("sync", xrt[:], x_d[t * 128:(t + 1) * 128, :])
                    c_ln = p.dma("sync", lnt[tl % 2][:], ln2_s[:, t * 1024:(t + 1) * 1024])
                    p.wait("tensor", [pf_rel, c_ln])
                    for c in range(ECH):
                        for hcol in range(2):
                            fn = lambda e, t=t, c=c, hcol=hcol, tl=tl: e.matmul(
                                out=pf[:, hcol * 512:(hcol + 1) * 512],
                                lhsT=lnt[tl % 2][:, c * 128:(c + 1) * 128],
                                rhs=wos[:, c * 1024 + hcol * 512:c * 1024 + (hcol + 1) * 512],
                                start=(c == 0), stop=(c == ECH - 1))
                            c_wo = p.op("tensor", fn, inc="te" if (c == ECH - 1 and hcol == 1) else None)
                    c_wo_prev = c_wo
                    ln_war[tl % 2] = c_wo
                    p.wait("vector", [c_wo, c_x, out_rel])
                    c_o1 = p.op("vector", lambda e, tl=tl: e.tensor_tensor(out=outb[:], in0=pf[:], in1=ffn_acc[:, tl * 1024:(tl + 1) * 1024], op=ALU.add), inc="dv")
                    pf_rel = c_o1
                    p.wait("vector", c_o1)
                    c_o2 = p.op("vector", lambda e: e.tensor_tensor(out=outb[:], in0=outb[:], in1=xrt[:], op=ALU.add), inc="dv")
                    x_rel = c_o2
                    p.wait("scalar", c_o2)
                    out_rel = p.dma("gpsimd", out_d[t * 128:(t + 1) * 128, :], outb[:])


def make_consts():
    ident = np.eye(128, dtype=ml_dtypes.bfloat16)
    tri = np.triu(np.ones((128, 128), dtype=np.float32)).astype(ml_dtypes.bfloat16)
    return ident, tri


_NC_CACHE = {}

def _get_nc():
    if "nc" not in _NC_CACHE:
        _NC_CACHE["nc"] = build_nc(99)
    return _NC_CACHE["nc"]


def kernel(L0_attn_x, L0_attn_Wq, L0_attn_Wk, L0_attn_Wv, L0_attn_Wo,
           L0_attn_Wg1, L0_attn_Wg2, L0_attn_gamma, L0_attn_beta,
           L0_ffn_w_gate, L0_ffn_w_up, L0_ffn_w_down):
    x = np.ascontiguousarray(np.asarray(L0_attn_x, dtype=np.float32))
    ident, tri = make_consts()
    cc, ss = rope_tables()
    gam = np.ascontiguousarray(np.broadcast_to(np.asarray(L0_attn_gamma, np.float32), (128, E)))
    bet = np.ascontiguousarray(np.broadcast_to(np.asarray(L0_attn_beta, np.float32), (128, E)))
    common = {
        "wq": np.ascontiguousarray(np.asarray(L0_attn_Wq, np.float32)),
        "wk": np.ascontiguousarray(np.asarray(L0_attn_Wk, np.float32)),
        "wv": np.ascontiguousarray(np.asarray(L0_attn_Wv, np.float32)),
        "wo": np.ascontiguousarray(np.asarray(L0_attn_Wo, np.float32)),
        "wg1": np.ascontiguousarray(np.asarray(L0_attn_Wg1, np.float32)),
        "wg2": np.ascontiguousarray(np.asarray(L0_attn_Wg2, np.float32)),
        "wgate": np.ascontiguousarray(np.asarray(L0_ffn_w_gate, np.float32)),
        "wup": np.ascontiguousarray(np.asarray(L0_ffn_w_up, np.float32)),
        "wdown": np.ascontiguousarray(np.asarray(L0_ffn_w_down, np.float32)),
        "ident": ident, "triu": tri, "ctab": cc, "stab": ss,
        "gam": gam, "bet": bet,
    }
    nc = _get_nc()
    in_maps = [dict(common, x=np.ascontiguousarray(x[c])) for c in range(x.shape[0])]
    res = run_bass_kernel_spmd(nc, in_maps, core_ids=list(range(len(in_maps))))
    return np.stack([r["out"] for r in res.results], axis=0)


# version 17
# speedup vs baseline: 1.1213x; 1.0029x over previous
import numpy as np
import ml_dtypes
import concourse.bass as bass
import concourse.mybir as mybir
from concourse.bass_utils import run_bass_kernel_spmd
from contextlib import ExitStack

F32 = mybir.dt.float32
BF16 = mybir.dt.bfloat16
I16 = mybir.dt.int16
ALU = mybir.AluOpType
AX = mybir.AxisListType
AFT = mybir.ActivationFunctionType

S, E, IFF = 2048, 1024, 2816
NT = S // 128
ECH = E // 128
H, D = 16, 64
NBE = E // 16
ICH = IFF // 128
WICH = (8, 8, 6)
WIPW = (1024, 1024, 768)

DEBUG_PHASE = 99
P2_LEVEL = 5
P2_CHUNKS = NT
P2_SKIP = set()
N_SQ = 16
N_GQ = 12


class Prog:
    ENGINES = ("sync", "vector", "scalar", "tensor", "gpsimd")

    def __init__(self, nc, st):
        self.nc = nc
        names = ["dv", "sc", "te", "pl"]
        names += [f"sq{i}" for i in range(N_SQ)]
        names += [f"gq{i}" for i in range(N_GQ)]
        self.sems = {n: st.enter_context(nc.semaphore(n)) for n in names}
        self.cnt = {n: 0 for n in names}
        self.q = {e: [] for e in self.ENGINES}
        self.slot_rr = {"sync": 0, "gpsimd": 0}

    def emit(self, eng, fn):
        self.q[eng].append(fn)

    def wait(self, eng, tok):
        if tok is None:
            return
        if isinstance(tok, list):
            best = {}
            for t in tok:
                if t is None:
                    continue
                s, v = t
                best[s] = max(best.get(s, 0), v)
            for s, v in best.items():
                self._wait1(eng, s, v)
            return
        self._wait1(eng, tok[0], tok[1])

    def _wait1(self, eng, sem, val):
        if val <= 0:
            return
        h = self.sems[sem]
        self.emit(eng, lambda e, h=h, v=val: e.wait_ge(h, v))

    def op(self, eng, fn, inc=None, k=1):
        if inc is None:
            self.emit(eng, fn)
            return None
        h = self.sems[inc]
        self.cnt[inc] += k
        tgt = self.cnt[inc]
        self.emit(eng, lambda e, fn=fn, h=h, k=k: fn(e).then_inc(h, k))
        return (inc, tgt)

    def dma(self, eng, out, in_, tp=False):
        pool, npool = ("sq", N_SQ) if eng == "sync" else ("gq", N_GQ)
        i = self.slot_rr[eng] % npool
        self.slot_rr[eng] += 1
        iss = "scalar" if eng == "gpsimd" else eng
        sem = f"{pool}{i}"
        prev = self.cnt[sem]
        if prev > 0:
            self._wait1(iss, sem, prev)
        self.cnt[sem] += 16
        tgt = self.cnt[sem]
        h = self.sems[sem]
        if tp:
            self.emit(iss, lambda e, o=out, i_=in_, h=h: e.dma_start_transpose(out=o, in_=i_).then_inc(h, 16))
        else:
            self.emit(iss, lambda e, o=out, i_=in_, h=h: e.dma_start(out=o, in_=i_).then_inc(h, 16))
        return (sem, tgt)

    def barrier(self):
        snap = dict(self.cnt)
        for eng in self.ENGINES:
            for s, v in snap.items():
                if v > 0:
                    self._wait1(eng, s, v)

    def finalize(self, block):
        for eng in self.ENGINES:
            fns = self.q[eng]
            if not fns:
                continue
            getattr(block, eng)(lambda e, fns=fns: [f(e) for f in fns])


def rope_tables():
    try:
        import jax
        import jax.numpy as jnp
        with jax.default_device(jax.devices("cpu")[0]):
            inv_freq = 10000.0 ** (-(jnp.arange(D // 2, dtype=jnp.float32) * 2.0) / D)
            posj = jnp.arange(S, dtype=jnp.float32) + 0
            angj = posj[:, None] * inv_freq[None, :]
            cos = np.asarray(jnp.cos(angj))
            sin = np.asarray(jnp.sin(angj))
    except Exception:
        inv32 = (10000.0 ** (-(np.arange(D // 2, dtype=np.float64) * 2.0) / D)).astype(np.float32)
        pos = np.arange(S, dtype=np.float32)
        a64 = (pos[:, None] * inv32[None, :]).astype(np.float64)
        cos = np.cos(a64).astype(np.float32)
        sin = np.sin(a64).astype(np.float32)
    cc = np.concatenate([cos, cos], axis=1)
    ss = np.concatenate([-sin, sin], axis=1)
    cc = np.ascontiguousarray(cc.reshape(NT, 128, 64).transpose(1, 0, 2).reshape(128, NT * 64))
    ss = np.ascontiguousarray(ss.reshape(NT, 128, 64).transpose(1, 0, 2).reshape(128, NT * 64))
    return cc, ss


def sl3(ap, u=16):
    return ap.rearrange("q (nb u) -> q nb u", u=u)


def bc3(a, nb, u=16):
    return a.rearrange("q (nb o) -> q nb o", o=1).broadcast_to([128, nb, u])


class QuantBufs:
    def __init__(self, sb, tag, W, fp32=True):
        self.W = W
        nb = W // 16
        self.nb = nb
        f = F32 if fp32 else BF16
        self.uab = sb(f"{tag}_uab", [128, W], f)
        self.sgn = sb(f"{tag}_sgn", [128, W], f)
        self.ua = sb(f"{tag}_ua", [128, W], f)
        self.shs = sb(f"{tag}_shs", [128, W], f)
        self.rr = [sb(f"{tag}_rrA", [128, W], f),
                   sb(f"{tag}_rrB", [128, W], f)]
        self.fa = sb(f"{tag}_fa", [128, W], f)
        self.bmax = sb(f"{tag}_bmax", [128, nb], F32)
        self.bm2 = sb(f"{tag}_bm2", [128, nb], F32)
        self.hs = sb(f"{tag}_hs", [128, nb], F32)
        self.rcp = sb(f"{tag}_rcp", [128, nb], F32)
        self.par = 0
        self.g_sc = None
        self.g_rr = [None, None]


def emit_quant(p, qb, src_ap, out_b, W, prev=None, out_war=None):
    nb = W // 16
    par = qb.par
    qb.par ^= 1
    rr = qb.rr[par]
    p.wait("scalar", qb.g_sc)
    p.op("scalar", lambda e: e.activation(out=qb.uab[:, 0:W], in_=src_ap, func=AFT.Abs))
    csg = p.op("scalar", lambda e: e.activation(out=qb.sgn[:, 0:W], in_=src_ap, func=AFT.Sign), inc="sc")
    p.wait("vector", csg)
    c = p.op("vector", lambda e: e.reduce_max(out=qb.bmax[:, 0:nb], in_=sl3(qb.uab[:, 0:W]), axis=AX.X), inc="dv")
    p.wait("vector", c)
    c = p.op("vector", lambda e: e.tensor_scalar_max(out=qb.bm2[:, 0:nb], in0=qb.bmax[:, 0:nb], scalar1=6e-12), inc="dv")
    p.wait("vector", c)
    p.op("vector", lambda e: e.tensor_scalar_mul(out=qb.hs[:, 0:nb], in0=qb.bm2[:, 0:nb], scalar1=1.0 / 12))
    c = p.op("vector", lambda e: e.reciprocal(out=qb.rcp[:, 0:nb], in_=qb.bm2[:, 0:nb]), inc="dv")
    p.wait("vector", c)
    cn = p.op("vector", lambda e: e.tensor_scalar_mul(out=qb.rcp[:, 0:nb], in0=qb.rcp[:, 0:nb], scalar1=6.0), inc="dv")
    p.wait("vector", cn)
    c_ua = p.op("vector", lambda e: e.tensor_tensor(out=sl3(qb.ua[:, 0:W]), in0=sl3(qb.uab[:, 0:W]), in1=bc3(qb.rcp[:, 0:nb], nb), op=ALU.mult), inc="dv")
    p.op("vector", lambda e: e.tensor_single_scalar(out=qb.fa[:, 0:W], in_=qb.ua[:, 0:W], scalar=0.25, op=ALU.is_gt))
    for thr in (0.75, 1.25, 1.75):
        p.op("vector", lambda e, thr=thr: e.scalar_tensor_tensor(out=qb.fa[:, 0:W], in0=qb.ua[:, 0:W], scalar=thr, in1=qb.fa[:, 0:W], op0=ALU.is_gt, op1=ALU.add))
    p.wait("vector", qb.g_rr[par])
    p.op("vector", lambda e: e.tensor_single_scalar(out=rr[:, 0:W], in_=qb.ua[:, 0:W], scalar=2.5, op=ALU.is_gt))
    for thr in (3.5, 5.0, 5.0):
        p.op("vector", lambda e, thr=thr: e.scalar_tensor_tensor(out=rr[:, 0:W], in0=qb.ua[:, 0:W], scalar=thr, in1=rr[:, 0:W], op0=ALU.is_gt, op1=ALU.add))
    cm = p.op("vector", lambda e: e.scalar_tensor_tensor(out=rr[:, 0:W], in0=rr[:, 0:W], scalar=2.0, in1=qb.fa[:, 0:W], op0=ALU.mult, op1=ALU.add), inc="dv")
    p.wait("gpsimd", [csg, cn])
    cshs = p.op("gpsimd", lambda e: e.tensor_tensor(out=sl3(qb.shs[:, 0:W]), in0=sl3(qb.sgn[:, 0:W]), in1=bc3(qb.hs[:, 0:nb], nb), op=ALU.mult), inc="pl")
    p.wait("gpsimd", [cm, out_war])
    cq = p.op("gpsimd", lambda e: e.tensor_tensor(out=out_b, in0=rr[:, 0:W], in1=qb.shs[:, 0:W], op=ALU.mult), inc="pl")
    qb.g_sc = [c_ua, cshs]
    qb.g_rr[par] = cq
    return cq


def build_nc(debug_phase=DEBUG_PHASE, races=False):
    nc = bass.Bass("TRN2", target_bir_lowering=False, detect_race_conditions=races)
    dt = lambda n, s, d, k: nc.dram_tensor(n, s, d, kind=k).ap()
    x_d = dt("x", [S, E], F32, "ExternalInput")
    wq_d = dt("wq", [E, E], F32, "ExternalInput")
    wk_d = dt("wk", [E, E], F32, "ExternalInput")
    wv_d = dt("wv", [E, E], F32, "ExternalInput")
    wo_d = dt("wo", [E, E], F32, "ExternalInput")
    wg1_d = dt("wg1", [E, D], F32, "ExternalInput")
    wg2_d = dt("wg2", [D, E], F32, "ExternalInput")
    wgt_d = dt("wgate", [IFF, E], F32, "ExternalInput")
    wup_d = dt("wup", [IFF, E], F32, "ExternalInput")
    wdn_d = dt("wdown", [E, IFF], F32, "ExternalInput")
    id_d = dt("ident", [128, 128], BF16, "ExternalInput")
    tri_d = dt("triu", [128, 128], BF16, "ExternalInput")
    ct_d = dt("ctab", [128, NT * 64], F32, "ExternalInput")
    st_d = dt("stab", [128, NT * 64], F32, "ExternalInput")
    gam_d = dt("gam", [128, E], F32, "ExternalInput")
    bet_d = dt("bet", [128, E], F32, "ExternalInput")
    out_d = dt("out", [S, E], F32, "ExternalOutput")
    wg_q = dt("wg_q", [IFF, E], BF16, "Internal")
    wu_q = dt("wu_q", [IFF, E], BF16, "Internal")
    wd_q = dt("wd_q", [E, IFF], BF16, "Internal")
    ln2_s = dt("ln2_s", [128, NT * 1024], BF16, "Internal")
    wo_s = dt("wo_s", [E, E], BF16, "Internal")
    attn_s = dt("attn_s", [S, E], BF16, "Internal")
    dbg = {}
    if debug_phase == 2:
        dbg["attn"] = dt("dbg_attn", [128, NT * E], BF16, "ExternalOutput")
    if debug_phase == 3:
        dbg["ln2"] = dt("dbg_ln2", [128, ECH * S], BF16, "ExternalOutput")

    with ExitStack() as st:
        sb = lambda n, s, d: st.enter_context(nc.sbuf_tensor(n, s, d))
        p = Prog(nc, st)

        ident = sb("identt", [128, 128], BF16)
        triu = sb("triut", [128, 128], BF16)
        ccb = sb("ccb", [128, NT * 64], BF16)
        ssb = sb("ssb", [128, NT * 64], BF16)
        gamb = sb("gamb", [128, E], BF16)
        betb = sb("betb", [128, E], BF16)
        xq_fm = sb("xq_fm", [128, ECH * S], BF16)

        with nc.sbuf_tensor("cload", [128, NT * 64], F32) as cload:
            t_id = p.dma("sync", ident[:], id_d[:, :])
            t_tri = p.dma("sync", triu[:], tri_d[:, :])
            cv = None
            for buf, d, w in ((gamb, gam_d, E), (betb, bet_d, E),
                              (ccb, ct_d, NT * 64), (ssb, st_d, NT * 64)):
                p.wait("sync", cv)
                tok = p.dma("sync", cload[:, 0:w], d[:, :])
                p.wait("vector", tok)
                cv = p.op("vector", lambda e, buf=buf, w=w: e.tensor_copy(out=buf[:], in_=cload[:, 0:w]), inc="dv")
            c_consts = cv
            p.wait("sync", c_consts)

        ws_stack = ExitStack()
        ws = WSteps(nc, p, ws_stack, wgt_d, wup_d, wdn_d, wo_d, wg_q, wu_q, wd_q, wo_s)
        with nc.sbuf_tensor("xn_fm", [128, ECH * S], BF16) as xn_fm:
            phase1(nc, p, x_d, ident, xn_fm, xq_fm, t_id, ws=ws)
            phase2(nc, p, wq_d, wk_d, wv_d, wg1_d, wg2_d,
                   ident, triu, ccb, ssb, xn_fm, attn_s, c_consts, t_tri,
                   dbg.get("attn"), ws=ws)
        phase3(nc, p, ident, attn_s, gamb, betb, ln2_s, ws=ws)
        ws.drain()
        ws_stack.close()
        if debug_phase == 3:
            p.barrier()
            p.dma("gpsimd", dbg["ln2"][:, :], ln2_s[:, :])
        if debug_phase >= 4:
            phase4(nc, p, x_d, xq_fm, ln2_s, wg_q, wu_q, wd_q, wo_s, out_d, ident, ws.stores)

        if debug_phase < 4:
            p.barrier()
            with nc.sbuf_tensor("zz", [128, E], F32) as zz:
                cz = p.op("vector", lambda e: e.memset(zz[:], 0.0), inc="dv")
                p.wait("scalar", cz)
                for t in range(NT):
                    p.dma("gpsimd", out_d[t * 128:(t + 1) * 128, :], zz[:])
        p.barrier()

        with nc.Block() as block:
            p.finalize(block)
    return nc


def phase1(nc, p, x_d, ident, xn_fm, xq_fm, t_id, ws=None):
    p.barrier()
    with ExitStack() as s1:
        sb1 = lambda n, s, d: s1.enter_context(nc.sbuf_tensor(n, s, d))
        xr = [sb1(f"xr{i}", [128, E], F32) for i in range(2)]
        stat = sb1("stat", [128, 8], F32)
        xnf = sb1("xnf", [128, E], F32)
        xnb2 = [sb1(f"xnb{i}", [128, E], BF16) for i in range(2)]
        xqb2 = [sb1(f"xqb{i}", [128, E], BF16) for i in range(2)]
        qb = QuantBufs(sb1, "p1", E)
        with nc.psum_tensor("pt1", [128, 1024], BF16) as pt1:
            xr_rel = [None, None]
            tp_n = [None, None]
            tp_q = [None, None]
            evq_rel = None
            pend = {"v": None}

            def flush_p1():
                nonlocal evq_rel
                if pend["v"] is None:
                    return
                t, par, c_nb, c_q = pend["v"]
                pend["v"] = None
                p.wait("tensor", [c_nb, evq_rel, t_id])
                for cch in range(ECH):
                    fn = lambda e, cch=cch, par=par: e.transpose(
                        out=pt1[:, cch * 128:(cch + 1) * 128],
                        in_=xnb2[par][:, cch * 128:(cch + 1) * 128], identity=ident[:])
                    c_tp = p.op("tensor", fn, inc="te" if cch == ECH - 1 else None)
                tp_n[par] = c_tp
                p.wait("vector", c_tp)
                c_evn = p.op("vector", lambda e, t=t: e.tensor_copy(
                    out=xn_fm[:, t * 1024:(t + 1) * 1024], in_=pt1[:]), inc="dv")
                p.wait("tensor", [c_evn, c_q])
                for cch in range(ECH):
                    fn = lambda e, cch=cch, par=par: e.transpose(
                        out=pt1[:, cch * 128:(cch + 1) * 128],
                        in_=xqb2[par][:, cch * 128:(cch + 1) * 128], identity=ident[:])
                    c_tp = p.op("tensor", fn, inc="te" if cch == ECH - 1 else None)
                tp_q[par] = c_tp
                p.wait("vector", c_tp)
                evq_rel = p.op("vector", lambda e, t=t: e.tensor_copy(
                    out=xq_fm[:, t * 1024:(t + 1) * 1024], in_=pt1[:]), inc="dv")

            for t in range(NT):
                par = t % 2
                b = xr[t % 2]
                p.wait("sync", xr_rel[t % 2])
                c_x = p.dma("sync", b[:], x_d[t * 128:(t + 1) * 128, :])
                p.wait("vector", c_x)
                c = p.op("vector", lambda e, b=b: e.reduce_sum(out=stat[:, 0:1], in_=b[:], axis=AX.X), inc="dv")
                p.wait("vector", c)
                c_mu = p.op("vector", lambda e: e.tensor_scalar_mul(out=stat[:, 0:1], in0=stat[:, 0:1], scalar1=1.0 / E), inc="dv")
                p.wait("scalar", c_x)
                p.wait("scalar", qb.g_sc)
                c_sq = p.op("scalar", lambda e, b=b: e.activation(out=qb.uab[:], in_=b[:], func=AFT.Square, accum_out=stat[:, 1:2]), inc="sc")
                p.wait("vector", [c_sq, c_mu])
                p.op("vector", lambda e: e.tensor_scalar_mul(out=stat[:, 2:3], in0=stat[:, 1:2], scalar1=1.0 / E))
                c = p.op("vector", lambda e: e.tensor_tensor(out=stat[:, 3:4], in0=stat[:, 0:1], in1=stat[:, 0:1], op=ALU.mult), inc="dv")
                p.wait("vector", c)
                c = p.op("vector", lambda e: e.tensor_tensor(out=stat[:, 2:3], in0=stat[:, 2:3], in1=stat[:, 3:4], op=ALU.subtract), inc="dv")
                p.wait("vector", c)
                c = p.op("vector", lambda e: e.tensor_scalar_add(out=stat[:, 6:7], in0=stat[:, 2:3], scalar1=1e-5), inc="dv")
                p.wait("scalar", c)
                c_sqrt = p.op("scalar", lambda e: e.activation(out=stat[:, 4:5], in_=stat[:, 6:7], func=AFT.Sqrt), inc="sc")
                p.wait("vector", c_sqrt)
                c_rs = p.op("vector", lambda e: e.reciprocal(out=stat[:, 3:4], in_=stat[:, 4:5]), inc="dv")
                p.wait("vector", c_rs)
                c_xn = p.op("vector", lambda e, b=b: e.tensor_scalar(
                    out=xnf[:], in0=b[:], scalar1=stat[:, 0:1], scalar2=stat[:, 3:4],
                    op0=ALU.subtract, op1=ALU.mult), inc="dv")
                xr_rel[t % 2] = c_xn
                p.wait("scalar", [c_xn, tp_n[par]])
                c_nb = p.op("scalar", lambda e, par=par: e.activation(out=xnb2[par][:], in_=xnf[:], func=AFT.Identity), inc="sc")
                c_q = emit_quant(p, qb, xnf[:], xqb2[par][:, 0:E], E, out_war=tp_q[par])
                prev = pend["v"]
                pend["v"] = None
                if prev is not None:
                    pend["v"], prev2 = prev, None
                    flush_p1()
                pend["v"] = (t, par, c_nb, c_q)
                if ws is not None:
                    ws.pump(1)
            flush_p1()


class WSteps:

    def __init__(self, nc, p, stack, wgt_d, wup_d, wdn_d, wo_d, wg_q, wu_q, wd_q, wo_s):
        sbw = lambda n, s, d: stack.enter_context(nc.sbuf_tensor(n, s, d))
        self.p = p
        self.raw = [sbw(f"wraw{i}", [128, 1024], F32) for i in range(2)]
        self.wqb = [sbw(f"wqbo{i}", [128, 1024], BF16) for i in range(2)]
        self.qb = QuantBufs(sbw, "pw", 1024, fp32=True)
        self.raw_rel = [None, None]
        self.store_war = [None, None]
        self.stores = []
        self.steps = []
        self.k = 0
        self.i = 0

        def quant_step(w_d, dst, r0, c0, w):
            def do():
                p = self.p
                li = self.k
                self.k += 1
                b = self.raw[li % 2]
                ob = self.wqb[li % 2]
                p.wait("sync", self.raw_rel[li % 2])
                c_w = p.dma("sync", b[:, 0:w], w_d[r0:r0 + 128, c0:c0 + w])
                p.wait("scalar", c_w)
                cq = emit_quant(p, self.qb, b[:, 0:w], ob[:, 0:w], w, out_war=self.store_war[li % 2])
                self.raw_rel[li % 2] = cq
                p.wait("scalar", cq)
                tok = p.dma("gpsimd", dst[r0:r0 + 128, c0:c0 + w], ob[:, 0:w])
                self.store_war[li % 2] = tok
                self.stores.append(tok)
            return do

        def wo_step(r):
            def do():
                p = self.p
                li = self.k
                self.k += 1
                b = self.raw[li % 2]
                ob = self.wqb[li % 2]
                p.wait("sync", self.raw_rel[li % 2])
                c_w = p.dma("sync", b[:, 0:1024], wo_d[r * 128:(r + 1) * 128, :])
                p.wait("scalar", [c_w, self.store_war[li % 2]])
                cc = p.op("scalar", lambda e, b=b, ob=ob: e.activation(out=ob[:, 0:1024], in_=b[:, 0:1024], func=AFT.Identity), inc="sc")
                self.raw_rel[li % 2] = cc
                p.wait("scalar", cc)
                tok = p.dma("gpsimd", wo_s[r * 128:(r + 1) * 128, :], ob[:, 0:1024])
                self.store_war[li % 2] = tok
                self.stores.append(tok)
            return do

        for w_d, dst in ((wgt_d, wg_q), (wup_d, wu_q)):
            for r in range(ICH):
                self.steps.append(quant_step(w_d, dst, r * 128, 0, E))
        for r in range(ECH):
            for c0, w in ((0, 1024), (1024, 1024), (2048, 768)):
                self.steps.append(quant_step(wdn_d, wd_q, r * 128, c0, w))
        for r in range(ECH):
            self.steps.append(wo_step(r))

    def pump(self, k):
        while k > 0 and self.i < len(self.steps):
            self.steps[self.i]()
            self.i += 1
            k -= 1

    def drain(self):
        self.pump(len(self.steps))


def phase2(nc, p, wq_d, wk_d, wv_d, wg1_d, wg2_d, ident, triu, ccb, ssb, xn_fm, attn_s, c_consts, t_tri, dbg_attn=None, ws=None):
    p.barrier()
    with ExitStack() as s2:
        sb2 = lambda n, s, d: s2.enter_context(nc.sbuf_tensor(n, s, d))
        wq_r = sb2("wq_r", [128, ECH * 1024], BF16)
        wk_r = sb2("wk_r", [128, ECH * 1024], BF16)
        wv_r = sb2("wv_r", [128, ECH * 1024], BF16)
        wg1_r = sb2("wg1_r", [128, ECH * 64], BF16)
        wg2_r = sb2("wg2_r", [64, 1024], BF16)
        gt_b = sb2("gt_b", [64, S], BF16)
        qs = sb2("qs", [128, 1024], BF16)
        ks = sb2("ks", [128, 1024], BF16)
        vs = sb2("vs", [128, 1024], BF16)
        gs = sb2("gs", [128, 1024], BF16)
        qro = sb2("qro", [128, 1024], BF16)
        kro = sb2("kro", [128, 1024], BF16)
        qzA = sb2("qzA", [128, 1024], BF16)
        qzB = sb2("qzB", [128, 1024], BF16)
        kzA = sb2("kzA", [128, 1024], BF16)
        kzB = sb2("kzB", [128, 1024], BF16)
        em = sb2("em", [128, 1024], BF16)
        S_sb = sb2("S_sb", [128, 512], BF16)
        ab = [sb2(f"ab{i}", [128, 1024], BF16) for i in range(2)]

        raw_scope = ExitStack()
        raw = [raw_scope.enter_context(nc.sbuf_tensor(f"w2raw{i}", [128, 1024], F32)) for i in range(2)]
        raw_rel = [None, None]
        i = 0
        for w_d, w_r in ((wq_d, wq_r), (wk_d, wk_r), (wv_d, wv_r)):
            for c in range(ECH):
                b = raw[i % 2]
                p.wait("sync", raw_rel[i % 2])
                c_w = p.dma("sync", b[:], w_d[c * 128:(c + 1) * 128, :])
                p.wait("vector", c_w)
                c_c = p.op("vector", lambda e, b=b, w_r=w_r, c=c: e.tensor_copy(out=w_r[:, c * 1024:(c + 1) * 1024], in_=b[:]), inc="dv")
                raw_rel[i % 2] = c_c
                i += 1
        for c in range(ECH):
            b = raw[i % 2]
            p.wait("sync", raw_rel[i % 2])
            c_w = p.dma("sync", b[:, 0:D], wg1_d[c * 128:(c + 1) * 128, :])
            p.wait("vector", c_w)
            c_c = p.op("vector", lambda e, b=b, c=c: e.tensor_copy(out=wg1_r[:, c * 64:(c + 1) * 64], in_=b[:, 0:D]), inc="dv")
            raw_rel[i % 2] = c_c
            i += 1
        b = raw[i % 2]
        p.wait("sync", raw_rel[i % 2])
        c_w = p.dma("sync", b[0:64, 0:E], wg2_d[:, :])
        p.wait("vector", c_w)
        c_wcast = p.op("vector", lambda e, b=b: e.tensor_copy(out=wg2_r[:, :], in_=b[0:64, 0:E]), inc="dv")
        c_szero = p.op("vector", lambda e: e.memset(S_sb[:], 0.0), inc="dv")
        p.op("vector", lambda e: e.memset(qzA[64:128, :], 0.0))
        p.op("vector", lambda e: e.memset(qzB[0:64, :], 0.0))
        p.op("vector", lambda e: e.memset(kzA[64:128, :], 0.0))
        c_zpad = p.op("vector", lambda e: e.memset(kzB[0:64, :], 0.0), inc="dv")
        p.barrier()
        raw_scope.close()
        if ws is not None:
            ws.pump(4)
        if P2_LEVEL <= 1:
            return

        with (nc.psum_tensor("pa", [128, 1024], F32) as pa,
              nc.psum_tensor("pb", [128, 1024], F32) as pb,
              nc.psum_tensor("po", [128, 1024], F32) as po,
              nc.psum_tensor("pss", [128, 512], F32) as pss,
              nc.psum_tensor("pt2", [128, 1024], BF16) as pt2):

            p.wait("tensor", [c_wcast, c_consts])
            c_gt = None
            for th in (range(4) if "gt" not in P2_SKIP else []):
                p.wait("tensor", c_gt)
                for tt in range(4):
                    t = th * 4 + tt
                    for c in range(ECH):
                        fn = lambda e, t=t, c=c, tt=tt: e.matmul(
                            out=pa[0:64, tt * 128:(tt + 1) * 128],
                            lhsT=wg1_r[:, c * 64:(c + 1) * 64],
                            rhs=xn_fm[:, t * 1024 + c * 128:t * 1024 + (c + 1) * 128],
                            start=(c == 0), stop=(c == ECH - 1))
                        c_mm = p.op("tensor", fn, inc="te" if (c == ECH - 1 and tt == 3) else None)
                p.wait("scalar", c_mm)
                c_gt = p.op("scalar", lambda e, th=th: e.activation(out=gt_b[:, th * 512:(th + 1) * 512], in_=pa[0:64, 0:512], func=AFT.Identity), inc="sc")
            p.wait("tensor", c_gt)
            if P2_LEVEL <= 2:
                return

            po_rel = None
            pt2_rel = None
            ab_rel = [None, None]
            pa_rel = None
            pb_rel = None
            c_qro_prev = None
            c_kro_prev = None
            c_smm_prev = None
            s_evac_prev = c_szero
            for n in range(P2_CHUNKS):
                p.wait("tensor", pa_rel)
                for c in (range(ECH) if "qkv" not in P2_SKIP else []):
                    for hcol in range(2):
                        fn = lambda e, n=n, c=c, hcol=hcol: e.matmul(
                            out=pa[:, hcol * 512:(hcol + 1) * 512],
                            lhsT=xn_fm[:, n * 1024 + c * 128:n * 1024 + (c + 1) * 128],
                            rhs=wq_r[:, c * 1024 + hcol * 512:c * 1024 + (hcol + 1) * 512],
                            start=(c == 0), stop=(c == ECH - 1))
                        c_qmm = p.op("tensor", fn, inc="te" if (c == ECH - 1 and hcol == 1) else None)
                if "qkv" not in P2_SKIP:
                    p.wait("scalar", [c_qmm, c_qro_prev])
                    c_qs = p.op("scalar", lambda e: e.activation(out=qs[:], in_=pa[:], func=AFT.Silu), inc="sc")
                else:
                    c_qs = None
                p.wait("tensor", pb_rel)
                for c in (range(ECH) if "qkv" not in P2_SKIP else []):
                    for hcol in range(2):
                        fn = lambda e, n=n, c=c, hcol=hcol: e.matmul(
                            out=pb[:, hcol * 512:(hcol + 1) * 512],
                            lhsT=xn_fm[:, n * 1024 + c * 128:n * 1024 + (c + 1) * 128],
                            rhs=wk_r[:, c * 1024 + hcol * 512:c * 1024 + (hcol + 1) * 512],
                            start=(c == 0), stop=(c == ECH - 1))
                        c_kmm = p.op("tensor", fn, inc="te" if (c == ECH - 1 and hcol == 1) else None)
                if "qkv" not in P2_SKIP:
                    p.wait("scalar", [c_kmm, c_kro_prev])
                    c_ks = p.op("scalar", lambda e: e.activation(out=ks[:], in_=pb[:], func=AFT.Silu), inc="sc")
                else:
                    c_ks = None
                p.wait("tensor", c_qs)
                for c in (range(ECH) if "qkv" not in P2_SKIP else []):
                    for hcol in range(2):
                        fn = lambda e, n=n, c=c, hcol=hcol: e.matmul(
                            out=pa[:, hcol * 512:(hcol + 1) * 512],
                            lhsT=xn_fm[:, n * 1024 + c * 128:n * 1024 + (c + 1) * 128],
                            rhs=wv_r[:, c * 1024 + hcol * 512:c * 1024 + (hcol + 1) * 512],
                            start=(c == 0), stop=(c == ECH - 1))
                        c_vmm = p.op("tensor", fn, inc="te" if (c == ECH - 1 and hcol == 1) else None)
                if "qkv" not in P2_SKIP:
                    p.wait("scalar", [c_vmm, c_smm_prev])
                    c_vs = p.op("scalar", lambda e: e.activation(out=vs[:], in_=pa[:], func=AFT.Identity), inc="sc")
                else:
                    c_vs = None
                p.wait("tensor", c_ks)
                for hcol in (range(2) if "gate" not in P2_SKIP else []):
                    fn = lambda e, n=n, hcol=hcol: e.matmul(
                        out=pb[:, hcol * 512:(hcol + 1) * 512],
                        lhsT=gt_b[:, n * 128:(n + 1) * 128],
                        rhs=wg2_r[:, hcol * 512:(hcol + 1) * 512],
                        start=True, stop=True)
                    c_gmm = p.op("tensor", fn, inc="te" if hcol == 1 else None)
                if "gate" not in P2_SKIP:
                    p.wait("scalar", [c_gmm, po_rel])
                    c_gs = p.op("scalar", lambda e: e.activation(out=gs[:], in_=pb[:], func=AFT.Sigmoid), inc="sc")
                else:
                    c_gs = None

                def rope(src, dst, c_src, n=n):
                    p.wait("vector", c_src)
                    s3 = lambda a: a[:].rearrange("q (h j) -> q h j", j=64)
                    cc_t = ccb[:, n * 64:(n + 1) * 64].rearrange("q (o j) -> q o j", o=1).broadcast_to([128, 16, 64])
                    ssA = ssb[:, n * 64:n * 64 + 32].rearrange("q (o j) -> q o j", o=1).broadcast_to([128, 16, 32]).rearrange("q h (o j) -> q h o j", o=1)
                    ssB = ssb[:, n * 64 + 32:n * 64 + 64].rearrange("q (o j) -> q o j", o=1).broadcast_to([128, 16, 32]).rearrange("q h (o j) -> q h o j", o=1)
                    src4 = src[:].rearrange("q (h k j) -> q h k j", k=2, j=32)
                    tmp4 = em[:, 0:1024].rearrange("q (h k j) -> q h k j", k=2, j=32)
                    dst4 = dst[:].rearrange("q (h k j) -> q h k j", k=2, j=32)
                    p.op("vector", lambda e: e.tensor_tensor(out=s3(em[:, 0:1024]), in0=s3(src), in1=cc_t, op=ALU.mult))
                    p.op("vector", lambda e: e.tensor_tensor(out=dst4[:, :, 0:1, :], in0=src4[:, :, 1:2, :], in1=ssA, op=ALU.mult))
                    p.op("vector", lambda e: e.tensor_tensor(out=dst4[:, :, 0:1, :], in0=dst4[:, :, 0:1, :], in1=tmp4[:, :, 0:1, :], op=ALU.add))
                    p.op("vector", lambda e: e.tensor_tensor(out=dst4[:, :, 1:2, :], in0=src4[:, :, 0:1, :], in1=ssB, op=ALU.mult))
                    return p.op("vector", lambda e: e.tensor_tensor(out=dst4[:, :, 1:2, :], in0=dst4[:, :, 1:2, :], in1=tmp4[:, :, 1:2, :], op=ALU.add), inc="dv")
                if "rope" not in P2_SKIP:
                    c_qro = rope(qs, qro, c_qs)
                    c_qro_prev = c_qro
                    p.wait("vector", c_smm_prev)
                    c_kro = rope(ks, kro, c_ks)
                    c_kro_prev = c_kro
                else:
                    c_qro = c_kro = None

                p.wait("tensor", [c_qro, pt2_rel])
                for cch in (range(ECH) if "tp" not in P2_SKIP else []):
                    fn = lambda e, cch=cch: e.transpose(
                        out=pt2[:, cch * 128:(cch + 1) * 128],
                        in_=qro[:, cch * 128:(cch + 1) * 128], identity=ident[:])
                    c_tp = p.op("tensor", fn, inc="te" if cch == ECH - 1 else None)
                if "tp" not in P2_SKIP:
                    p.wait("vector", c_tp)
                    p.op("vector", lambda e: e.tensor_copy(out=qzA[0:64, :], in_=pt2[0:64, :]))
                    c_qf = p.op("vector", lambda e: e.tensor_copy(out=qzB[64:128, :], in_=pt2[64:128, :]), inc="dv")
                else:
                    c_qf = None
                p.wait("tensor", [c_kro, c_qf])
                for cch in (range(ECH) if "tp" not in P2_SKIP else []):
                    fn = lambda e, cch=cch: e.transpose(
                        out=pt2[:, cch * 128:(cch + 1) * 128],
                        in_=kro[:, cch * 128:(cch + 1) * 128], identity=ident[:])
                    c_tp = p.op("tensor", fn, inc="te" if cch == ECH - 1 else None)
                if "tp" not in P2_SKIP:
                    p.wait("vector", c_tp)
                    p.op("vector", lambda e: e.tensor_copy(out=kzA[0:64, :], in_=pt2[0:64, :]))
                    c_kf = p.op("vector", lambda e: e.tensor_copy(out=kzB[64:128, :], in_=pt2[64:128, :]), inc="dv")
                    pt2_rel = c_kf
                else:
                    c_kf = None

                if P2_LEVEL <= 3:
                    pt2_rel = c_kf
                    pa_rel = c_vs
                    pb_rel = c_gs
                    c_qro_prev = c_qro
                    c_kro_prev = c_kro
                    continue
                p.wait("tensor", c_kf)
                c_e = [None, None]
                for half, pXX in ((0, pa), (1, pb)):
                    p.wait("tensor", c_vs if half == 0 else c_gs)
                    for hl in range(8):
                        h = half * 8 + hl
                        kz = kzA if h % 2 == 0 else kzB
                        qz = qzA if h % 2 == 0 else qzB
                        cc2 = h // 2
                        fn = lambda e, kz=kz, qz=qz, cc2=cc2, hl=hl, pXX=pXX: e.matmul(
                            out=pXX[:, hl * 128:(hl + 1) * 128],
                            lhsT=kz[:, cc2 * 128:(cc2 + 1) * 128],
                            rhs=qz[:, cc2 * 128:(cc2 + 1) * 128],
                            start=True, stop=True)
                        c_e[half] = p.op("tensor", fn, inc="te" if hl == 7 else None)
                tri8 = triu[:, :].rearrange("q (o j) -> q o j", o=1).broadcast_to([128, 8, 128])
                if P2_LEVEL <= 4 or P2_LEVEL in (35, 36, 41, 42, 43, 44):
                    if P2_LEVEL == 41: hsel = [0]
                    elif P2_LEVEL == 42: hsel = [1]
                    elif P2_LEVEL == 43: hsel = [0, 1]
                    elif P2_LEVEL == 44: hsel = [0, 2, 4, 6]
                    else: hsel = list(range(8))
                    for half, pXX in ((0, pa), (1, pb)):
                        p.wait("tensor", c_vs if half == 0 else c_gs)
                        for hl in hsel:
                            h = half * 8 + hl
                            kz = kzA if h % 2 == 0 else kzB
                            qz = qzA if h % 2 == 0 else qzB
                            cc2 = h // 2
                            fn = lambda e, kz=kz, qz=qz, cc2=cc2, hl=hl, pXX=pXX: e.matmul(
                                out=pXX[:, hl * 128:(hl + 1) * 128],
                                lhsT=kz[:, cc2 * 128:(cc2 + 1) * 128],
                                rhs=qz[:, cc2 * 128:(cc2 + 1) * 128],
                                start=True, stop=True)
                            c_eh = p.op("tensor", fn, inc="te" if hl == hsel[-1] else None)
                        p.wait("vector", [c_eh, t_tri])
                        if P2_LEVEL in (35, 41, 42, 43, 44):
                            c_em = p.op("vector", lambda e, pXX=pXX: e.tensor_copy(out=em[:], in_=pXX[:]), inc="dv")
                        elif P2_LEVEL == 36:
                            c_em = p.op("vector", lambda e, pXX=pXX: e.tensor_tensor(
                                out=em[:].rearrange("q (hh j) -> q hh j", j=128), in0=pXX[:].rearrange("q (hh j) -> q hh j", j=128), in1=tri8, op=ALU.mult), inc="dv")
                        else:
                            c_em = p.op("vector", lambda e, pXX=pXX: e.tensor_tensor(
                                out=em[:].rearrange("q (hh j) -> q hh j", j=128), in0=pXX[:].rearrange("q (hh j) -> q hh j", j=128), in1=tri8, op=ALU.mult), inc="dv")
                        if half == 0:
                            pa_rel = c_em
                        else:
                            pb_rel = c_em
                    pt2_rel = c_kf
                    c_qro_prev = c_qro
                    c_kro_prev = c_kro
                    continue
                c_omm = None
                for half, pXX in ((0, pa), (1, pb)):
                    p.wait("vector", [c_e[half], t_tri, c_omm])
                    c_em = p.op("vector", lambda e, pXX=pXX: e.tensor_tensor(
                        out=em[:].rearrange("q (hh j) -> q hh j", j=128), in0=pXX[:].rearrange("q (hh j) -> q hh j", j=128), in1=tri8, op=ALU.mult), inc="dv")
                    if half == 0:
                        pa_rel = c_em
                    else:
                        pb_rel = c_em
                    p.wait("tensor", [c_em, po_rel, s_evac_prev])
                    for hl in range(8):
                        h = half * 8 + hl
                        bp = 64 * (h % 2)
                        cc2 = h // 2
                        fn1 = lambda e, h=h, hl=hl: e.matmul(
                            out=po[:, h * 64:(h + 1) * 64],
                            lhsT=em[:, hl * 128:(hl + 1) * 128],
                            rhs=vs[:, h * 64:(h + 1) * 64],
                            start=True, stop=False)
                        p.op("tensor", fn1)
                        qz = qzA if h % 2 == 0 else qzB
                        fn2 = lambda e, h=h, qz=qz, cc2=cc2: e.matmul(
                            out=po[:, h * 64:(h + 1) * 64],
                            lhsT=qz[:, cc2 * 128:(cc2 + 1) * 128],
                            rhs=S_sb[:, cc2 * 64:(cc2 + 1) * 64],
                            start=False, stop=True)
                        c_omm = p.op("tensor", fn2, inc="te" if hl == 7 else None)

                for h in range(H):
                    bp = 64 * (h % 2)
                    fn = lambda e, h=h, bp=bp, n=n: e.matmul(
                        out=pss[bp:bp + 64, (h // 2) * 64:(h // 2 + 1) * 64],
                        lhsT=kro[:, h * 64:(h + 1) * 64],
                        rhs=vs[:, h * 64:(h + 1) * 64],
                        start=(n == 0 and h < 2), stop=(n == NT - 1 and h >= H - 2),
                        tile_position=(0, bp), skip_group_check=True)
                    c_smm = p.op("tensor", fn, inc="te" if h == H - 1 else None)
                c_smm_prev = c_smm
                p.wait("scalar", c_smm)
                s_evac_prev = p.op("scalar", lambda e: e.activation(out=S_sb[:], in_=pss[:], func=AFT.Identity), inc="sc")

                p.wait("vector", [c_omm, c_gs, ab_rel[n % 2]])
                po_rel = p.op("vector", lambda e, n=n: e.tensor_tensor(
                    out=ab[n % 2][:], in0=po[:], in1=gs[:], op=ALU.mult), inc="dv")
                p.wait("scalar", po_rel)
                ab_rel[n % 2] = p.dma("gpsimd", attn_s[n * 128:(n + 1) * 128, :], ab[n % 2][:])
                if dbg_attn is not None:
                    ab_rel[n % 2] = p.dma("gpsimd", dbg_attn[:, n * 1024:(n + 1) * 1024], ab[n % 2][:])
                if ws is not None:
                    ws.pump(3)


def phase3(nc, p, ident, attn_s, gamb, betb, ln2_s, ws=None):
    p.barrier()
    with ExitStack() as s3:
        sb3 = lambda n, s, d: s3.enter_context(nc.sbuf_tensor(n, s, d))
        stat = sb3("stat3", [128, 8], F32)
        n1 = sb3("n1", [128, E], BF16)
        sq = sb3("sq3", [128, E], F32)
        ln2b = sb3("ln2b", [128, E], BF16)
        lnst = [sb3(f"lnst{i}", [128, E], BF16) for i in range(2)]
        abuf = [sb3(f"a3_{i}", [128, E], BF16) for i in range(2)]
        with nc.psum_tensor("pt3", [128, 1024], BF16) as pt3:
            ev_rel = None
            n1_rel = None
            a_rel = [None, None]
            ln_rel = [None, None]
            for t in range(NT):
                p.wait("sync", a_rel[t % 2])
                c_a = p.dma("sync", abuf[t % 2][:], attn_s[t * 128:(t + 1) * 128, :])
                a = abuf[t % 2][:]
                p.wait("vector", c_a)
                p.wait("scalar", c_a)
                c = p.op("vector", lambda e, a=a: e.reduce_sum(out=stat[:, 0:1], in_=a, axis=AX.X), inc="dv")
                p.wait("vector", c)
                c_mu = p.op("vector", lambda e: e.tensor_scalar_mul(out=stat[:, 0:1], in0=stat[:, 0:1], scalar1=1.0 / E), inc="dv")
                c_sq = p.op("scalar", lambda e, a=a: e.activation(out=sq[:], in_=a, func=AFT.Square, accum_out=stat[:, 1:2]), inc="sc")
                p.wait("vector", [c_sq, c_mu])
                p.op("vector", lambda e: e.tensor_scalar_mul(out=stat[:, 2:3], in0=stat[:, 1:2], scalar1=1.0 / E))
                c = p.op("vector", lambda e: e.tensor_tensor(out=stat[:, 3:4], in0=stat[:, 0:1], in1=stat[:, 0:1], op=ALU.mult), inc="dv")
                p.wait("vector", c)
                c = p.op("vector", lambda e: e.tensor_tensor(out=stat[:, 2:3], in0=stat[:, 2:3], in1=stat[:, 3:4], op=ALU.subtract), inc="dv")
                p.wait("vector", c)
                c = p.op("vector", lambda e: e.tensor_scalar_add(out=stat[:, 6:7], in0=stat[:, 2:3], scalar1=1e-5), inc="dv")
                p.wait("scalar", c)
                c_sqrt = p.op("scalar", lambda e: e.activation(out=stat[:, 4:5], in_=stat[:, 6:7], func=AFT.Sqrt), inc="sc")
                p.wait("vector", c_sqrt)
                c_rs = p.op("vector", lambda e: e.reciprocal(out=stat[:, 3:4], in_=stat[:, 4:5]), inc="dv")
                p.wait("vector", [c_rs, n1_rel])
                c_n1 = p.op("vector", lambda e, a=a: e.tensor_scalar(
                    out=n1[:], in0=a, scalar1=stat[:, 0:1], scalar2=stat[:, 3:4],
                    op0=ALU.subtract, op1=ALU.mult), inc="dv")
                a_rel[t % 2] = c_n1
                p.wait("vector", c_n1)
                p.op("vector", lambda e: e.tensor_tensor(out=ln2b[:], in0=n1[:], in1=gamb[:], op=ALU.mult))
                c_l2 = p.op("vector", lambda e: e.tensor_tensor(out=ln2b[:], in0=ln2b[:], in1=betb[:], op=ALU.add), inc="dv")
                n1_rel = c_l2
                p.wait("tensor", [c_l2, ev_rel])
                for cch in range(ECH):
                    fn = lambda e, cch=cch: e.transpose(
                        out=pt3[:, cch * 128:(cch + 1) * 128],
                        in_=ln2b[:, cch * 128:(cch + 1) * 128], identity=ident[:])
                    c_tp = p.op("tensor", fn, inc="te" if cch == ECH - 1 else None)
                p.wait("vector", [c_tp, ln_rel[t % 2]])
                ev_rel = p.op("vector", lambda e, t=t: e.tensor_copy(out=lnst[t % 2][:], in_=pt3[:]), inc="dv")
                p.wait("scalar", ev_rel)
                ln_rel[t % 2] = p.dma("gpsimd", ln2_s[:, t * 1024:(t + 1) * 1024], lnst[t % 2][:])
                if ws is not None:
                    ws.pump(1)


def phase4(nc, p, x_d, xq_fm, ln2_s, wg_q, wu_q, wd_q, wo_s, out_d, ident, w_fence):
    p.barrier()
    with ExitStack() as s4:
        sb4 = lambda n, s, d: s4.enter_context(nc.sbuf_tensor(n, s, d))
        ffn_acc = sb4("ffn_acc", [128, 8 * 1024], F32)
        wgs = [sb4(f"wgs{i}", [128, ECH * 1024], BF16) for i in range(2)]
        wus = [sb4(f"wus{i}", [128, ECH * 1024], BF16) for i in range(2)]
        wds = sb4("wds", [128, 8 * 1024], BF16)
        wos = wds
        sg = sb4("sg", [128, 1024], BF16)
        ub = sb4("ub", [128, 1024], BF16)
        hb = sb4("hb", [128, 1024], BF16)
        hqb = [sb4(f"hqb{i}", [128, 1024], BF16) for i in range(2)]
        hqf = [sb4(f"hqf{i}", [128, 1024], BF16) for i in range(2)]
        qb = QuantBufs(sb4, "p4", 1024, fp32=True)
        xrt = sb4("xrt", [128, E], F32)
        outb = sb4("outb", [128, E], F32)
        lnt = [sb4(f"lnt{i}", [128, E], BF16) for i in range(2)]
        with (nc.psum_tensor("pg", [128, 1024], F32) as pg,
              nc.psum_tensor("pu", [128, 1024], F32) as pu,
              nc.psum_tensor("pf", [128, 1024], F32) as pf,
              nc.psum_tensor("pt4", [128, 1024], BF16) as pt4):
            p.wait("sync", w_fence)
            w_rel = None
            c_wo_prev = None
            pt4_rel = None
            hq_rel = None
            c_ub_prev = None
            tp_tok = [None, None]
            dn_war = [None, None]
            pf_rel = None
            x_rel = None
            out_rel = None
            ln_war = [None, None]
            gu_war = [None, None]

            def issue_gu(k):
                grp_, ip_ = divmod(k, 3)
                W_ = WIPW[ip_]
                par = k % 2
                p.wait("sync", gu_war[par])
                toks = []
                for c in range(ECH):
                    toks.append(p.dma("sync", wgs[par][:, c * 1024:c * 1024 + W_], wg_q[ip_ * 1024:ip_ * 1024 + W_, c * 128:(c + 1) * 128], tp=True))
                    toks.append(p.dma("sync", wus[par][:, c * 1024:c * 1024 + W_], wu_q[ip_ * 1024:ip_ * 1024 + W_, c * 128:(c + 1) * 128], tp=True))
                return toks

            gu_toks = issue_gu(0)
            for grp in range(2):
                for ip in range(3):
                    W = WIPW[ip]
                    NIC = WICH[ip]
                    k = grp * 3 + ip
                    par = k % 2
                    p.wait("sync", [w_rel, c_wo_prev])
                    ldd = []
                    for icl in range(NIC):
                        ic = ip * 8 + icl
                        ldd.append(p.dma("sync", wds[:, icl * 1024:(icl + 1) * 1024], wd_q[:, ic * 128:(ic + 1) * 128], tp=True))
                    nxt = issue_gu(k + 1) if k + 1 < 6 else []
                    p.wait("tensor", gu_toks)
                    gu_toks = nxt
                    prev4 = None
                    for tl in range(9):
                        if tl < 8:
                            t = grp * 8 + tl
                            hpar = tl % 2
                            p.wait("tensor", c_ub_prev)
                            for c in range(ECH):
                                for hcol in range((W + 511) // 512):
                                    lo, hi = hcol * 512, min(W, (hcol + 1) * 512)
                                    fn = lambda e, t=t, c=c, lo=lo, hi=hi, par=par: e.matmul(
                                        out=pg[:, lo:hi],
                                        lhsT=xq_fm[:, t * 1024 + c * 128:t * 1024 + (c + 1) * 128],
                                        rhs=wgs[par][:, c * 1024 + lo:c * 1024 + hi],
                                        start=(c == 0), stop=(c == ECH - 1))
                                    p.op("tensor", fn)
                                    fn = lambda e, t=t, c=c, lo=lo, hi=hi, par=par: e.matmul(
                                        out=pu[:, lo:hi],
                                        lhsT=xq_fm[:, t * 1024 + c * 128:t * 1024 + (c + 1) * 128],
                                        rhs=wus[par][:, c * 1024 + lo:c * 1024 + hi],
                                        start=(c == 0), stop=(c == ECH - 1))
                                    c_gu = p.op("tensor", fn, inc="te" if (c == ECH - 1 and hi == W) else None)
                            p.wait("scalar", [c_gu, hq_rel])
                            p.op("scalar", lambda e, W=W: e.activation(out=sg[:, 0:W], in_=pg[:, 0:W], func=AFT.Silu))
                            c_ub = p.op("scalar", lambda e, W=W: e.activation(out=ub[:, 0:W], in_=pu[:, 0:W], func=AFT.Identity), inc="sc")
                            c_ub_prev = c_ub
                            p.wait("vector", c_ub)
                            c_h = p.op("vector", lambda e, W=W: e.tensor_tensor(out=hb[:, 0:W], in0=sg[:, 0:W], in1=ub[:, 0:W], op=ALU.mult), inc="dv")
                            p.wait("scalar", c_h)
                            c_hq = emit_quant(p, qb, hb[:, 0:W], hqb[hpar][:, 0:W], W, out_war=tp_tok[hpar])
                            hq_rel = c_hq
                        if prev4 is not None:
                            ptl, p_hq = prev4
                            ppar = ptl % 2
                            p.wait("tensor", [p_hq, pt4_rel])
                            for icl in range(NIC):
                                fn = lambda e, icl=icl, ppar=ppar: e.transpose(
                                    out=pt4[:, icl * 128:(icl + 1) * 128],
                                    in_=hqb[ppar][:, icl * 128:(icl + 1) * 128], identity=ident[:])
                                c_tp = p.op("tensor", fn, inc="te" if icl == NIC - 1 else None)
                            tp_tok[ppar] = c_tp
                            p.wait("vector", [c_tp, dn_war[ppar]])
                            c_hqf = p.op("vector", lambda e, NIC=NIC, ppar=ppar: e.tensor_copy(out=hqf[ppar][:, 0:NIC * 128], in_=pt4[:, 0:NIC * 128]), inc="dv")
                            pt4_rel = c_hqf
                            p.wait("tensor", [c_hqf, pf_rel] + (ldd if ptl == 0 else []))
                            for icl in range(NIC):
                                for hcol in range(2):
                                    fn = lambda e, icl=icl, hcol=hcol, ppar=ppar, NIC=NIC: e.matmul(
                                        out=pf[:, hcol * 512:(hcol + 1) * 512],
                                        lhsT=hqf[ppar][:, icl * 128:(icl + 1) * 128],
                                        rhs=wds[:, icl * 1024 + hcol * 512:icl * 1024 + (hcol + 1) * 512],
                                        start=(icl == 0), stop=(icl == NIC - 1))
                                    c_dn = p.op("tensor", fn, inc="te" if (icl == NIC - 1 and hcol == 1) else None)
                            dn_war[ppar] = c_dn
                            p.wait("vector", c_dn)
                            if ip == 0:
                                pf_rel = p.op("vector", lambda e, ptl=ptl: e.tensor_copy(out=ffn_acc[:, ptl * 1024:(ptl + 1) * 1024], in_=pf[:]), inc="dv")
                            else:
                                pf_rel = p.op("vector", lambda e, ptl=ptl: e.tensor_tensor(out=ffn_acc[:, ptl * 1024:(ptl + 1) * 1024], in0=ffn_acc[:, ptl * 1024:(ptl + 1) * 1024], in1=pf[:], op=ALU.add), inc="dv")
                        prev4 = (tl, c_hq) if tl < 8 else None
                    w_rel = c_dn
                    gu_war[par] = c_gu
                p.wait("sync", w_rel)
                ldo = []
                for c in range(ECH):
                    ldo.append(p.dma("sync", wos[:, c * 1024:(c + 1) * 1024], wo_s[c * 128:(c + 1) * 128, :]))
                p.wait("tensor", ldo)
                for tl in range(8):
                    t = grp * 8 + tl
                    p.wait("sync", [x_rel, ln_war[tl % 2]])
                    c_x = p.dma("sync", xrt[:], x_d[t * 128:(t + 1) * 128, :])
                    c_ln = p.dma("sync", lnt[tl % 2][:], ln2_s[:, t * 1024:(t + 1) * 1024])
                    p.wait("tensor", [pf_rel, c_ln])
                    for c in range(ECH):
                        for hcol in range(2):
                            fn = lambda e, t=t, c=c, hcol=hcol, tl=tl: e.matmul(
                                out=pf[:, hcol * 512:(hcol + 1) * 512],
                                lhsT=lnt[tl % 2][:, c * 128:(c + 1) * 128],
                                rhs=wos[:, c * 1024 + hcol * 512:c * 1024 + (hcol + 1) * 512],
                                start=(c == 0), stop=(c == ECH - 1))
                            c_wo = p.op("tensor", fn, inc="te" if (c == ECH - 1 and hcol == 1) else None)
                    c_wo_prev = c_wo
                    ln_war[tl % 2] = c_wo
                    p.wait("vector", [c_wo, c_x, out_rel])
                    c_o1 = p.op("vector", lambda e, tl=tl: e.tensor_tensor(out=outb[:], in0=pf[:], in1=ffn_acc[:, tl * 1024:(tl + 1) * 1024], op=ALU.add), inc="dv")
                    pf_rel = c_o1
                    p.wait("vector", c_o1)
                    c_o2 = p.op("vector", lambda e: e.tensor_tensor(out=outb[:], in0=outb[:], in1=xrt[:], op=ALU.add), inc="dv")
                    x_rel = c_o2
                    p.wait("scalar", c_o2)
                    out_rel = p.dma("gpsimd", out_d[t * 128:(t + 1) * 128, :], outb[:])


def make_consts():
    ident = np.eye(128, dtype=ml_dtypes.bfloat16)
    tri = np.triu(np.ones((128, 128), dtype=np.float32)).astype(ml_dtypes.bfloat16)
    return ident, tri


_NC_CACHE = {}

def _get_nc():
    if "nc" not in _NC_CACHE:
        _NC_CACHE["nc"] = build_nc(99)
    return _NC_CACHE["nc"]


def kernel(L0_attn_x, L0_attn_Wq, L0_attn_Wk, L0_attn_Wv, L0_attn_Wo,
           L0_attn_Wg1, L0_attn_Wg2, L0_attn_gamma, L0_attn_beta,
           L0_ffn_w_gate, L0_ffn_w_up, L0_ffn_w_down):
    x = np.ascontiguousarray(np.asarray(L0_attn_x, dtype=np.float32))
    ident, tri = make_consts()
    cc, ss = rope_tables()
    gam = np.ascontiguousarray(np.broadcast_to(np.asarray(L0_attn_gamma, np.float32), (128, E)))
    bet = np.ascontiguousarray(np.broadcast_to(np.asarray(L0_attn_beta, np.float32), (128, E)))
    common = {
        "wq": np.ascontiguousarray(np.asarray(L0_attn_Wq, np.float32)),
        "wk": np.ascontiguousarray(np.asarray(L0_attn_Wk, np.float32)),
        "wv": np.ascontiguousarray(np.asarray(L0_attn_Wv, np.float32)),
        "wo": np.ascontiguousarray(np.asarray(L0_attn_Wo, np.float32)),
        "wg1": np.ascontiguousarray(np.asarray(L0_attn_Wg1, np.float32)),
        "wg2": np.ascontiguousarray(np.asarray(L0_attn_Wg2, np.float32)),
        "wgate": np.ascontiguousarray(np.asarray(L0_ffn_w_gate, np.float32)),
        "wup": np.ascontiguousarray(np.asarray(L0_ffn_w_up, np.float32)),
        "wdown": np.ascontiguousarray(np.asarray(L0_ffn_w_down, np.float32)),
        "ident": ident, "triu": tri, "ctab": cc, "stab": ss,
        "gam": gam, "bet": bet,
    }
    nc = _get_nc()
    in_maps = [dict(common, x=np.ascontiguousarray(x[c])) for c in range(x.shape[0])]
    res = run_bass_kernel_spmd(nc, in_maps, core_ids=list(range(len(in_maps))))
    return np.stack([r["out"] for r in res.results], axis=0)
